# version 05
# baseline (speedup 1.0000x reference)
import sys

sys.path.insert(0, "/opt/trn_rl_repo")
from contextlib import ExitStack

import numpy as np
import ml_dtypes
import concourse.bass as bass
import concourse.mybir as mybir
from concourse.bass_utils import run_bass_kernel_spmd
from concourse.alu_op_type import AluOpType as OP

F32 = mybir.dt.float32
BF16 = mybir.dt.bfloat16
AF = mybir.ActivationFunctionType
AX = mybir.AxisListType

D = 1024
S = 4096
NCH = 8
T = 2048
NQT = 16
NKT = 32
VW = 72


class Sem:
    __slots__ = ("h", "v")

    def __init__(self, h):
        self.h = h
        self.v = 0


class Wr:
    def __init__(self, e):
        self.e = e
        self.seen = {}

    def wait(self, sem, val=None):
        if val is None:
            val = sem.v
        if val <= 0 or self.seen.get(id(sem), 0) >= val:
            return
        self.e.wait_ge(sem.h, val)
        self.seen[id(sem)] = val


def inc(instr, sem, n=1):
    instr.then_inc(sem.h, n)
    sem.v += n


DBGQG = 0


def build(debug=0):
    nc = bass.Bass("TRN2", target_bir_lowering=False)

    def din(name, shape, dt=F32):
        return nc.dram_tensor(name, shape, dt, kind="ExternalInput").ap()

    xq = din("xq", [T, D])
    xb = din("xb", [S, D])
    mask_a = din("mask_a", [128, 128])
    mask_b = din("mask_b", [128, 128])
    l0_wqT = din("l0_wqT", [D, D])
    l0_wkT = din("l0_wkT", [D, D])
    l0_wvT = din("l0_wvT", [D, D])
    l0_woT = din("l0_woT", [D, D])
    l0_bo = din("l0_bo", [128, 8])
    l0_gq = din("l0_gq", [128, 1])
    l0_bq = din("l0_bq", [128, 1])
    l0_gk = din("l0_gk", [128, 1])
    l0_bk = din("l0_bk", [128, 1])
    l0_u1 = din("l0_u1", [D, 256])
    l0_v1 = din("l0_v1", [256, 4096])
    l0_b1 = din("l0_b1", [128, 32])
    l0_u2 = din("l0_u2", [4096, 256])
    l0_v2 = din("l0_v2", [256, D])
    l0_b2 = din("l0_b2", [128, 8])
    l1_wqT = din("l1_wqT", [D, D])
    l1_wkT2 = din("l1_wkT2", [D, 128])
    l1_wvT = din("l1_wvT", [D, 64])
    l1_bias = din("l1_bias", [128, 8])
    wrT_i = din("wrT", [D, D])
    wiT_i = din("wiT", [D, D])
    cCb = din("cCb", [D, D], BF16)
    cSb = din("cSb", [D, D], BF16)
    cSnb = din("cSnb", [D, D], BF16)
    e2bf_i = din("e2bf", [128, 2], BF16)
    e2tf_i = din("e2tf", [2, 128])
    ones1_i = din("ones1", [1, 128])
    neg1_i = din("neg1", [1, 128])
    onesc_i = din("onesc", [128, 1])
    identf_i = din("identf", [128, 128])
    o64_i = din("o64", [64, 1], BF16)
    oc128_i = din("oc128", [128, 1], BF16)

    xout = nc.dram_tensor("xout", [T, D], F32, kind="ExternalOutput").ap()
    dbgo = {}
    if debug:
        for nm in ("d_x1", "d_x2", "d_x3"):
            dbgo[nm] = nc.dram_tensor(nm, [128, NCH, T], F32, kind="ExternalOutput").ap()
        dbgo["d_k2"] = nc.dram_tensor("d_k2", [128, S], F32, kind="ExternalOutput").ap()
        dbgo["d_vt1"] = nc.dram_tensor("d_vt1", [128, NKT * VW], F32,
                                       kind="ExternalOutput").ap()
        dbgo["d_negb"] = nc.dram_tensor("d_negb", [128, 1], F32, kind="ExternalOutput").ap()
        dbgo["d_qg"] = nc.dram_tensor("d_qg", [128, NCH, 256], F32,
                                      kind="ExternalOutput").ap()
        dbgo["d_pt0"] = nc.dram_tensor("d_pt0", [128, 1024], F32, kind="ExternalOutput").ap()
        dbgo["d_ob1"] = nc.dram_tensor("d_ob1", [128, NCH, 256], F32,
                                       kind="ExternalOutput").ap()
        dbgo["d_scal"] = nc.dram_tensor("d_scal", [1, 16], F32, kind="ExternalOutput").ap()
        dbgo["d_xr"] = nc.dram_tensor("d_xr", [128, NCH, 512], F32,
                                      kind="ExternalOutput").ap()
        dbgo["d_xr0"] = nc.dram_tensor("d_xr0", [128, NCH, 256], F32,
                                       kind="ExternalOutput").ap()
        dbgo["d_stm"] = nc.dram_tensor("d_stm", [2, 2048], F32, kind="ExternalOutput").ap()
        dbgo["d_sts"] = nc.dram_tensor("d_sts", [2, 2048], F32, kind="ExternalOutput").ap()
        dbgo["d_str"] = nc.dram_tensor("d_str", [2, 2048], F32, kind="ExternalOutput").ap()

    cc_kin = nc.dram_tensor("cc_kin", [64, T], BF16)
    cc_kout = nc.dram_tensor("cc_kout", [128, T], BF16)
    cc_vin = nc.dram_tensor("cc_vin", [128, NQT * VW], BF16)
    cc_vout = nc.dram_tensor("cc_vout", [256, NQT * VW], BF16)
    pairs = [[0, 1], [2, 3], [4, 5], [6, 7]]

    est = ExitStack()
    E = est.enter_context

    sDL = Sem(E(nc.semaphore("sDL")))
    sDT = Sem(E(nc.semaphore("sDT")))
    sST = Sem(E(nc.semaphore("sST")))
    sPE = Sem(E(nc.semaphore("sPE")))
    sVE = Sem(E(nc.semaphore("sVE")))
    sAC = Sem(E(nc.semaphore("sAC")))
    sGP = Sem(E(nc.semaphore("sGP")))
    sCC = Sem(E(nc.semaphore("sCC")))
    sX0 = Sem(E(nc.semaphore("sX0")))
    sX1 = Sem(E(nc.semaphore("sX1")))
    sX = [sX0, sX1]
    ALLSEMS = [sDL, sDT, sST, sPE, sVE, sAC, sGP, sCC]

    pe = Wr(nc.tensor)
    ve = Wr(nc.vector)
    ac = Wr(nc.scalar)
    gp = Wr(nc.gpsimd)
    sy = Wr(nc.sync)

    def fence(*engs):
        for e_ in engs:
            for sm_ in ALLSEMS:
                e_.wait(sm_, sm_.v)

    def gbar():
        gp.wait(sVE, sVE.v)
        gp.wait(sAC, sAC.v)
        inc(gp.e.nop(), sGP, 1)
        ve.wait(sGP, sGP.v)
        ac.wait(sGP, sGP.v)
        pe.wait(sGP, sGP.v)

    def wload(engw, sem, dst3, src2d, nj, c0, w):
        engw.wait(sem, sem.v)
        for j_ in range(nj):
            inc(engw.e.dma_start(out=dst3[:, j_, :],
                                 in_=src2d[j_ * 128:(j_ + 1) * 128, c0:c0 + w]), sem, 16)

    c_e2bf = E(nc.sbuf_tensor("c_e2bf", [128, 2], BF16))
    c_e2tf = E(nc.sbuf_tensor("c_e2tf", [2, 128], F32))
    c_ones1 = E(nc.sbuf_tensor("c_ones1", [1, 128], F32))
    c_neg1 = E(nc.sbuf_tensor("c_neg1", [1, 128], F32))
    c_onesc = E(nc.sbuf_tensor("c_onesc", [128, 1], F32))
    c_ident = E(nc.sbuf_tensor("c_ident", [128, 128], F32))
    c_maska = E(nc.sbuf_tensor("c_maska", [128, 128], F32))
    c_maskb = E(nc.sbuf_tensor("c_maskb", [128, 128], F32))
    c_o64 = E(nc.sbuf_tensor("c_o64", [64, 1], BF16))
    c_oc128 = E(nc.sbuf_tensor("c_oc128", [128, 1], BF16))
    rowb = E(nc.sbuf_tensor("rowb", [1, 128], F32))
    qmtmp = E(nc.sbuf_tensor("qmtmp", [1, 1], F32))
    qmaxs = E(nc.sbuf_tensor("qmaxs", [1, 1], F32))
    dscal = E(nc.sbuf_tensor("dscal", [1, 16], F32))
    c_gq = E(nc.sbuf_tensor("c_gq", [128, 1], F32))
    c_bq = E(nc.sbuf_tensor("c_bq", [128, 1], F32))
    c_gk = E(nc.sbuf_tensor("c_gk", [128, 1], F32))
    c_bk = E(nc.sbuf_tensor("c_bk", [128, 1], F32))
    c_bo = E(nc.sbuf_tensor("c_bo", [128, 8], F32))
    c_b1 = E(nc.sbuf_tensor("c_b1", [128, 32], F32))
    c_b2 = E(nc.sbuf_tensor("c_b2", [128, 8], F32))
    c_l1b = E(nc.sbuf_tensor("c_l1b", [128, 8], F32))
    c_zero = E(nc.sbuf_tensor("c_zero", [128, 1], F32))
    c_i64 = E(nc.sbuf_tensor("c_i64", [2, 1], F32))
    c_iD = E(nc.sbuf_tensor("c_iD", [1, 1], F32))
    c_iD128 = E(nc.sbuf_tensor("c_iD128", [128, 1], F32))
    c_eps2 = E(nc.sbuf_tensor("c_eps2", [2, 1], F32))

    amax = E(nc.sbuf_tensor("amax", [128, 16], F32))
    negm2 = E(nc.sbuf_tensor("negm2", [128, 2], F32))
    ellp = E(nc.sbuf_tensor("ellp", [128, 8], F32))
    recl2 = E(nc.sbuf_tensor("recl2", [128, 2], F32))
    reclT = E(nc.sbuf_tensor("reclT", [2, 128], F32))
    mtile = E(nc.sbuf_tensor("mtile", [128, 256], F32))
    st_m = E(nc.sbuf_tensor("st_m", [2, 2048], F32))
    st_s = E(nc.sbuf_tensor("st_s", [2, 2048], F32))
    st_r = E(nc.sbuf_tensor("st_r", [2, 2048], F32))
    st_t = E(nc.sbuf_tensor("st_t", [2, 2048], F32))
    kmax2 = E(nc.sbuf_tensor("kmax2", [1, 1], F32))
    kmtmp = E(nc.sbuf_tensor("kmtmp", [1, 1], F32))
    qmax2 = E(nc.sbuf_tensor("qmax2", [2, 8], F32))
    qmaxr = E(nc.sbuf_tensor("qmaxr", [2, 1], F32))
    qmaxf = E(nc.sbuf_tensor("qmaxf", [1, 2], F32))
    bsc = E(nc.sbuf_tensor("bsc", [1, 1], F32))
    negB = E(nc.sbuf_tensor("negB", [128, 1], F32))
    tmpa = E(nc.sbuf_tensor("tmpa", [128, 512], F32))
    tmpb = E(nc.sbuf_tensor("tmpb", [128, 512], F32))
    stg = E(nc.sbuf_tensor("stg", [128, 512], BF16))
    stg2 = E(nc.sbuf_tensor("stg2", [128, 512], BF16))

    X1BF = E(nc.sbuf_tensor("X1BF", [128, NCH, T], BF16))
    XTM = [E(nc.sbuf_tensor(f"XTM{i}", [128, 1024], BF16)) for i in range(2)]

    PS_A = E(nc.psum_tensor("PS_A", [128, 2048], F32))
    PS_B = E(nc.psum_tensor("PS_B", [128, 2048], F32))

    def bA(i, w=512, p0=0, p1=128):
        return PS_A[p0:p1, i * 512:i * 512 + w]

    def bB(i, w=512, p0=0, p1=128):
        return PS_B[p0:p1, i * 512:i * 512 + w]

    bg = {}

    def guard(key):
        if key in bg:
            sem, val = bg[key]
            pe.wait(sem, val)

    def release(key, sem):
        bg[key] = (sem, sem.v)

    for dst, src in [
        (c_e2bf, e2bf_i), (c_e2tf, e2tf_i), (c_ones1, ones1_i), (c_neg1, neg1_i),
        (c_onesc, onesc_i), (c_ident, identf_i), (c_maska, mask_a), (c_maskb, mask_b),
        (c_o64, o64_i), (c_oc128, oc128_i),
        (c_gq, l0_gq), (c_bq, l0_bq), (c_gk, l0_gk), (c_bk, l0_bk),
        (c_bo, l0_bo), (c_b1, l0_b1), (c_b2, l0_b2), (c_l1b, l1_bias),
    ]:
        inc(sy.e.dma_start(out=dst[:, :], in_=src), sDL, 16)
    nc.vector.memset(c_zero[:, :], 0.0)
    nc.vector.memset(c_i64[:, :], 1.0 / 64.0)
    nc.vector.memset(c_iD[:, :], 1.0 / 1024.0)
    nc.vector.memset(c_iD128[:, :], 1.0 / 1024.0)
    inc(nc.vector.memset(c_eps2[:, :], 1e-5), sVE, 1)
    consts_d = sDL.v
    nc.sync.wait_ge(sDL.h, consts_d)

    xtm_free = [0, 0]

    def load_fm(dst3, src2d, nt, coff=0):
        sy.wait(sDT, sDT.v)
        for t in range(nt):
            b = t % 2
            sb = sX[b]
            gp.wait(sDT, sDT.v)
            inc(gp.e.dma_start(out=XTM[b][:, :], in_=src2d[t * 128:(t + 1) * 128, :]), sb, 16)
            sy.wait(sb, sb.v)
            inc(sy.e.dma_start_transpose(
                out=dst3[:, :, coff + t * 128:coff + (t + 1) * 128], in_=XTM[b][:, :]), sDT, 16)
            xtm_free[b] = sDT.v

    def rstd_from_stats(n, rows=2):
        ve.wait(sVE, sVE.v)
        ve.wait(sAC, sAC.v)
        inc(ve.e.tensor_tensor(st_t[0:rows, :n], st_m[0:rows, :n], st_m[0:rows, :n],
                               op=OP.mult), sVE, 1)
        ve.wait(sVE, sVE.v)
        inc(ve.e.tensor_tensor(st_t[0:rows, :n], st_s[0:rows, :n], st_t[0:rows, :n],
                               op=OP.subtract), sVE, 1)
        ac.wait(sVE, sVE.v)
        ac.wait(sPE, sPE.v)
        inc(ac.e.activation(st_r[0:rows, :n], st_t[0:rows, :n], AF.Ln,
                            bias=c_eps2[0:rows, :]), sAC, 1)
        ac.wait(sAC, sAC.v)
        inc(ac.e.activation(st_r[0:rows, :n], st_r[0:rows, :n], AF.Exp,
                            scale=np.float32(-0.5)), sAC, 1)
        return sAC.v

    def postln(XRES, c0, ct):
        nqb = max(1, ct // 512)
        bw = min(512, ct)
        for qb in range(nqb):
            cs = c0 + qb * bw
            lo = qb * bw
            guard("A0")
            guard("A1")
            for j in range(NCH):
                src = XRES[:, j, cs:cs + bw]
                ve.wait(sPE, sPE.v)
                inc(ve.e.tensor_tensor(tmpa[:, :bw], src, src, op=OP.mult), sVE, 1)
                pe.wait(sVE, sVE.v)
                nc.tensor.matmul(bA(0, bw, 0, 1), c_onesc[:, :], src,
                                 start=(j == 0), stop=(j == NCH - 1))
                mm = nc.tensor.matmul(bA(1, bw, 0, 1), c_onesc[:, :], tmpa[:, :bw],
                                      start=(j == 0), stop=(j == NCH - 1))
                inc(mm, sPE, 1)
            ve.wait(sPE, sPE.v)
            ve.wait(sAC, sAC.v)
            inc(ve.e.tensor_scalar(st_m[0:1, lo:lo + bw], bA(0, bw, 0, 1),
                                   c_iD[0:1, :], None, op0=OP.mult), sVE, 1)
            inc(ve.e.tensor_scalar(st_s[0:1, lo:lo + bw], bA(1, bw, 0, 1),
                                   c_iD[0:1, :], None, op0=OP.mult), sVE, 1)
            release("A0", sVE)
            release("A1", sVE)
        rstd_at = rstd_from_stats(ct, rows=1)
        for qb in range(nqb):
            cs = c0 + qb * bw
            lo = qb * bw
            guard("A2")
            guard("A3")
            pe.wait(sAC, rstd_at)
            inc(nc.tensor.matmul(bA(2, bw), c_ones1[:, :], st_m[0:1, lo:lo + bw],
                                 start=True, stop=True), sPE, 1)
            inc(nc.tensor.matmul(bA(3, bw), c_ones1[:, :], st_r[0:1, lo:lo + bw],
                                 start=True, stop=True), sPE, 1)
            ve.wait(sPE, sPE.v)
            for j in range(NCH):
                src = XRES[:, j, cs:cs + bw]
                inc(ve.e.scalar_tensor_tensor(tmpa[:, :bw], src, c_zero[:, :], bA(2, bw),
                                              op0=OP.add, op1=OP.subtract), sVE, 1)
                ve.wait(sVE, sVE.v)
                inc(ve.e.tensor_tensor(src, tmpa[:, :bw], bA(3, bw), op=OP.mult), sVE, 1)
                ve.wait(sVE, sVE.v)
                inc(ve.e.tensor_copy(X1BF[:, j, cs:cs + bw], src), sVE, 1)
            release("A2", sVE)
            release("A3", sVE)

    def kqstats(buf_slice, m_dst, s_dst):
        ve.wait(sPE, sPE.v)
        inc(ve.e.tensor_tensor(stg[:, :], buf_slice, buf_slice, op=OP.mult), sVE, 1)
        guard("B2")
        guard("B3")
        pe.wait(sVE, sVE.v)
        nc.tensor.matmul(bB(2, 512, 0, 2), c_e2bf[:, :], buf_slice, start=True, stop=True)
        inc(nc.tensor.matmul(bB(3, 512, 0, 2), c_e2bf[:, :], stg[:, :],
                             start=True, stop=True), sPE, 1)
        ve.wait(sPE, sPE.v)
        inc(ve.e.tensor_scalar(m_dst, bB(2, 512, 0, 2), c_i64[:, :], None, op0=OP.mult),
            sVE, 1)
        inc(ve.e.tensor_scalar(s_dst, bB(3, 512, 0, 2), c_i64[:, :], None, op0=OP.mult),
            sVE, 1)
        release("B2", sVE)
        release("B3", sVE)

    def lnapply(buf3, cl, c0, n, rstd_at, gv, bv):
        for cb_ in range(n // 512):
            col = c0 + cb_ * 512
            guard("A2")
            guard("A3")
            pe.wait(sAC, rstd_at)
            pe.wait(sVE, sVE.v)
            inc(nc.tensor.matmul(bA(2), c_e2tf[:, :], st_m[0:2, cb_ * 512:(cb_ + 1) * 512],
                                 start=True, stop=True), sPE, 1)
            inc(nc.tensor.matmul(bA(3), c_e2tf[:, :], st_r[0:2, cb_ * 512:(cb_ + 1) * 512],
                                 start=True, stop=True), sPE, 1)
            dst = buf3[:, cl, col:col + 512] if cl is not None else buf3[:, col:col + 512]
            ve.wait(sPE, sPE.v)
            inc(ve.e.scalar_tensor_tensor(tmpa[:, :], dst, c_zero[:, :], bA(2),
                                          op0=OP.add, op1=OP.subtract), sVE, 1)
            ve.wait(sVE, sVE.v)
            inc(ve.e.tensor_tensor(tmpb[:, :], tmpa[:, :], bA(3), op=OP.mult), sVE, 1)
            ve.wait(sVE, sVE.v)
            inc(ve.e.tensor_scalar(dst, tmpb[:, :], gv[:, :], bv[:, :],
                                   op0=OP.mult, op1=OP.add), sVE, 1)
            release("A2", sVE)
            release("A3", sVE)

    sy.wait(sDL, consts_d)
    with ExitStack() as l0s:
        L = l0s.enter_context
        XQBF = L(nc.sbuf_tensor("XQBF", [128, NCH, T], BF16))
        XBF = L(nc.sbuf_tensor("XBF", [128, NCH, T], BF16))
        KH = L(nc.sbuf_tensor("KH", [128, S], BF16))
        VT = L(nc.sbuf_tensor("VT", [128, NKT * 128], BF16))
        QH = L(nc.sbuf_tensor("QH", [128, T], BF16))
        PB = [L(nc.sbuf_tensor(f"PB{i}", [128, 4096], BF16)) for i in range(2)]
        PT = [L(nc.sbuf_tensor(f"PT{i}", [128, NKT, 128], BF16)) for i in range(2)]
        wq_g = L(nc.sbuf_tensor("wq_g", [128, NCH, 128], BF16))
        wk_g = L(nc.sbuf_tensor("wk_g", [128, NCH, 128], BF16))
        wv_g = L(nc.sbuf_tensor("wv_g", [128, NCH, 128], BF16))

        load_fm(XQBF, xq, 16)
        xq_fm_at = sDT.v
        pb_free = 0
        pt_free = 0

        for g in range(8):
            gp.wait(sPE, sPE.v)
            for dst, srcw in [(wq_g, l0_wqT), (wk_g, l0_wkT), (wv_g, l0_wvT)]:
                wload(gp, sGP, dst, srcw, 8, g * 128, 128)
            wg_at = sGP.v

            for half in range(2):
                sy.wait(sPE, sPE.v)
                load_fm(XBF, xb[half * T:(half + 1) * T, :], 16)
                xbf_at = sDT.v
                for tb in range(4):
                    key = f"A{tb % 2}"
                    guard(key)
                    pe.wait(sGP, wg_at)
                    pe.wait(sDT, xbf_at)
                    for j in range(NCH):
                        mm = nc.tensor.matmul(bA(tb % 2), wk_g[:, j, :],
                                              XBF[:, j, tb * 512:(tb + 1) * 512],
                                              start=(j == 0), stop=(j == NCH - 1))
                    inc(mm, sPE, 1)
                    khs = KH[:, half * T + tb * 512:half * T + (tb + 1) * 512]
                    ve.wait(sPE, sPE.v)
                    inc(ve.e.tensor_copy(khs, bA(tb % 2)), sVE, 1)
                    release(key, sVE)
                    ve.wait(sVE, sVE.v)
                    kqstats(khs, st_m[:, tb * 512:(tb + 1) * 512],
                            st_s[:, tb * 512:(tb + 1) * 512])
                r_at = rstd_from_stats(T)
                lnapply(KH, None, half * T, T, r_at, c_gk, c_bk)
                for kt in range(16):
                    key = f"B{kt % 2}"
                    guard(key)
                    for j in range(NCH):
                        mm = nc.tensor.matmul(bB(kt % 2, 128), XBF[:, j, kt * 128:(kt + 1) * 128],
                                              wv_g[:, j, :], start=(j == 0), stop=(j == NCH - 1))
                    inc(mm, sPE, 1)
                    gkt = half * 16 + kt
                    ve.wait(sPE, sPE.v)
                    inc(ve.e.tensor_copy(VT[:, gkt * 128:(gkt + 1) * 128], bB(kt % 2, 128)),
                        sVE, 1)
                    release(key, sVE)
            for tb in range(4):
                key = f"A{tb % 2}"
                guard(key)
                pe.wait(sDT, xq_fm_at)
                for j in range(NCH):
                    mm = nc.tensor.matmul(bA(tb % 2), wq_g[:, j, :],
                                          XQBF[:, j, tb * 512:(tb + 1) * 512],
                                          start=(j == 0), stop=(j == NCH - 1))
                inc(mm, sPE, 1)
                qhs = QH[:, tb * 512:(tb + 1) * 512]
                ve.wait(sPE, sPE.v)
                inc(ve.e.tensor_copy(qhs, bA(tb % 2)), sVE, 1)
                release(key, sVE)
                ve.wait(sVE, sVE.v)
                kqstats(qhs, st_m[:, tb * 512:(tb + 1) * 512],
                        st_s[:, tb * 512:(tb + 1) * 512])
            r_at = rstd_from_stats(T)
            lnapply(QH, None, 0, T, r_at, c_gq, c_bq)
            gbar()
            lnk_at = sVE.v
            for qt in range(NQT):
                nk = 2 * qt + 2
                qlh = QH[0:64, qt * 128:(qt + 1) * 128]
                qhh = QH[64:128, qt * 128:(qt + 1) * 128]
                ca = (nk + 3) // 4
                for c in range(ca):
                    w = min(4, nk - 4 * c) * 128
                    k0 = c * 512
                    b1_, b2_ = c % 2, 2 + c % 2
                    guard(f"A{b1_}")
                    guard(f"A{b2_}")
                    pe.wait(sVE, lnk_at)
                    nc.tensor.matmul(bA(b1_, w), qlh, KH[0:64, k0:k0 + w],
                                     start=True, stop=True)
                    inc(nc.tensor.matmul(bA(b2_, w), qhh, KH[64:128, k0:k0 + w],
                                         start=True, stop=True, tile_position=(64, 0)),
                        sPE, 1)
                    ve.wait(sPE, sPE.v)
                    if c == ca - 1:
                        for bi in (b1_, b2_):
                            inc(ve.e.tensor_tensor(bA(bi, 512)[:, w - 256:w - 128],
                                                   bA(bi, 512)[:, w - 256:w - 128],
                                                   c_maska[:, :], op=OP.add), sVE, 1)
                            inc(ve.e.tensor_tensor(bA(bi, 512)[:, w - 128:w],
                                                   bA(bi, 512)[:, w - 128:w],
                                                   c_maskb[:, :], op=OP.add), sVE, 1)
                        ve.wait(sVE, sVE.v)
                    inc(ve.e.tensor_reduce(amax[:, c:c + 1], bA(b1_, w), axis=AX.X,
                                           op=OP.max), sVE, 1)
                    inc(ve.e.tensor_reduce(amax[:, 8 + c:9 + c], bA(b2_, w), axis=AX.X,
                                           op=OP.max), sVE, 1)
                    release(f"A{b1_}", sVE)
                    release(f"A{b2_}", sVE)
                ve.wait(sVE, sVE.v)
                ve.wait(sAC, sAC.v)
                inc(ve.e.tensor_reduce(negm2[:, 0:1], amax[:, 0:ca], axis=AX.X,
                                       op=OP.max, negate=True), sVE, 1)
                inc(ve.e.tensor_reduce(negm2[:, 1:2], amax[:, 8:8 + ca], axis=AX.X,
                                       op=OP.max, negate=True), sVE, 1)
                negm_at = sVE.v
                cb = (nk + 7) // 8
                for c in range(cb):
                    w = min(8, nk - 8 * c) * 128
                    k0 = c * 1024
                    PSX = PS_A if c % 2 == 0 else PS_B
                    kyb = "A" if c % 2 == 0 else "B"
                    for bi in range(4):
                        guard(f"{kyb}{bi}")
                    for sub in range((w + 511) // 512):
                        ws = min(512, w - sub * 512)
                        o_ = sub * 512
                        nc.tensor.matmul(PSX[:, o_:o_ + ws], qlh,
                                         KH[0:64, k0 + o_:k0 + o_ + ws],
                                         start=True, stop=True)
                        mm = nc.tensor.matmul(PSX[:, 1024 + o_:1024 + o_ + ws], qhh,
                                              KH[64:128, k0 + o_:k0 + o_ + ws],
                                              start=True, stop=True, tile_position=(64, 0))
                    inc(mm, sPE, 1)
                    if c == cb - 1:
                        ve.wait(sPE, sPE.v)
                        for off in (0, 1024):
                            inc(ve.e.tensor_tensor(PSX[:, off + w - 256:off + w - 128],
                                                   PSX[:, off + w - 256:off + w - 128],
                                                   c_maska[:, :], op=OP.add), sVE, 1)
                            inc(ve.e.tensor_tensor(PSX[:, off + w - 128:off + w],
                                                   PSX[:, off + w - 128:off + w],
                                                   c_maskb[:, :], op=OP.add), sVE, 1)
                    ac.wait(sPE, sPE.v)
                    ac.wait(sVE, sVE.v)
                    ac.wait(sDT, pb_free)
                    inc(ac.e.activation(PB[0][:, k0:k0 + w], PSX[:, 0:w], AF.Exp,
                                        bias=negm2[:, 0:1], accum_out=ellp[:, c:c + 1]),
                        sAC, 1)
                    inc(ac.e.activation(PB[1][:, k0:k0 + w], PSX[:, 1024:1024 + w], AF.Exp,
                                        bias=negm2[:, 1:2], accum_out=ellp[:, 4 + c:5 + c]),
                        sAC, 1)
                    for bi in range(4):
                        release(f"{kyb}{bi}", sAC)
                exp_at = sAC.v
                ve.wait(sAC, exp_at)
                ve.wait(sPE, sPE.v)
                inc(ve.e.tensor_reduce(recl2[:, 0:1], ellp[:, 0:cb], axis=AX.X, op=OP.add),
                    sVE, 1)
                inc(ve.e.tensor_reduce(recl2[:, 1:2], ellp[:, 4:4 + cb], axis=AX.X,
                                       op=OP.add), sVE, 1)
                ve.wait(sVE, sVE.v)
                inc(ve.e.reciprocal(recl2[:, 0:2], recl2[:, 0:2]), sVE, 1)
                recl_at = sVE.v
                sy.wait(sAC, exp_at)
                sy.wait(sPE, pt_free)
                sy.wait(sDT, sDT.v)
                inc(sy.e.dma_start_transpose(out=PT[0][:, 0:nk, :],
                                             in_=PB[0][:, 0:nk * 128]), sDT, 16)
                inc(sy.e.dma_start_transpose(out=PT[1][:, 0:nk, :],
                                             in_=PB[1][:, 0:nk * 128]), sDT, 16)
                pb_free = sDT.v
                guard("B0")
                pe.wait(sDT, sDT.v)
                for kt in range(nk):
                    base = kt * 128
                    nc.tensor.matmul(bB(0, 128, 0, 64), VT[:, base:base + 64],
                                     PT[0][:, kt, :], start=(kt == 0), stop=(kt == nk - 1))
                    mm = nc.tensor.matmul(bB(0, 128, 64, 128), VT[:, base + 64:base + 128],
                                          PT[1][:, kt, :], start=(kt == 0),
                                          stop=(kt == nk - 1), tile_position=(0, 64))
                inc(mm, sPE, 1)
                pt_free = sPE.v
                guard("B1")
                pe.wait(sVE, recl_at)
                inc(nc.tensor.transpose(PS_B[0:2, 512:640], recl2[:, 0:2], c_ident[:, :]),
                    sPE, 1)
                ve.wait(sPE, sPE.v)
                inc(ve.e.tensor_copy(reclT[:, :], PS_B[0:2, 512:640]), sVE, 1)
                release("B1", sVE)
                guard("B2")
                pe.wait(sVE, sVE.v)
                inc(nc.tensor.matmul(bB(2, 128), c_e2tf[:, :], reclT[:, :],
                                     start=True, stop=True), sPE, 1)
                ve.wait(sPE, sPE.v)
                inc(ve.e.tensor_copy(mtile[:, 0:128], bB(2, 128)), sVE, 1)
                release("B2", sVE)
                ve.wait(sVE, sVE.v)
                inc(ve.e.tensor_tensor(X1BF[:, g, qt * 128:(qt + 1) * 128],
                                       bB(0, 128), mtile[:, 0:128], op=OP.mult), sVE, 1)
                release("B0", sVE)

    XRES = E(nc.sbuf_tensor("XRES", [128, NCH, T], F32))
    wo_stack = ExitStack()
    XTF = wo_stack.enter_context(nc.sbuf_tensor("XTF", [128, 1024], F32))
    wsl = wo_stack.enter_context(nc.sbuf_tensor("wsl", [128, NCH, 256], BF16))
    fence(ve, pe, sy, gp, ac)

    for t in range(NQT):
        sy.wait(sVE, sVE.v)
        sy.wait(sDL, sDL.v)
        inc(sy.e.dma_start(out=XTF[:, :], in_=xq[t * 128:(t + 1) * 128, :]), sDL, 16)
        pe.wait(sDL, sDL.v)
        for j in range(NCH):
            key = f"B{j % 2}"
            guard(key)
            inc(nc.tensor.transpose(bB(j % 2, 128), XTF[:, j * 128:(j + 1) * 128],
                                    c_ident[:, :]), sPE, 1)
            ve.wait(sPE, sPE.v)
            inc(ve.e.tensor_copy(XRES[:, j, t * 128:(t + 1) * 128], bB(j % 2, 128)), sVE, 1)
            release(key, sVE)
    gbar()
    for g4 in range(4):
        gp.wait(sPE, sPE.v)
        wload(gp, sGP, wsl, l0_woT, 8, g4 * 256, 256)
        wo_at = sGP.v
        for jj in range(2):
            j = g4 * 2 + jj
            for qb in range(4):
                key = f"A{qb % 4}"
                guard(key)
                pe.wait(sGP, wo_at)
                pe.wait(sVE, sVE.v)
                for dch in range(NCH):
                    mm = nc.tensor.matmul(bA(qb % 4), wsl[:, dch, jj * 128:(jj + 1) * 128],
                                          X1BF[:, dch, qb * 512:(qb + 1) * 512],
                                          start=(dch == 0), stop=(dch == NCH - 1))
                inc(mm, sPE, 1)
                dst = XRES[:, j, qb * 512:(qb + 1) * 512]
                ve.wait(sPE, sPE.v)
                inc(ve.e.scalar_tensor_tensor(dst, bA(qb % 4), c_bo[:, j:j + 1], dst,
                                              op0=OP.add, op1=OP.add), sVE, 1)
                release(key, sVE)
    postln(XRES, 0, T)
    wo_stack.close()
    if debug:
        sy.wait(sVE, sVE.v)
        inc(sy.e.dma_start(out=dbgo["d_x1"], in_=XRES[:, :, :]), sST, 16)

    with ExitStack() as f0s:
        L = f0s.enter_context
        u1b = L(nc.sbuf_tensor("u1b", [128, NCH, 256], BF16))
        v1b = L(nc.sbuf_tensor("v1b", [128, 2, 4096], BF16))
        u2b = L(nc.sbuf_tensor("u2b", [128, 32, 256], BF16))
        v2b = L(nc.sbuf_tensor("v2b", [128, 2, 1024], BF16))
        pqb = L(nc.sbuf_tensor("pqb", [128, 2, 512], BF16))
        qqb = L(nc.sbuf_tensor("qqb", [128, 2, 512], BF16))
        fence(gp)
        wload(gp, sGP, u1b, l0_u1, 8, 0, 256)
        wload(gp, sGP, v1b, l0_v1, 2, 0, 4096)
        wload(gp, sGP, u2b, l0_u2, 32, 0, 256)
        wload(gp, sGP, v2b, l0_v2, 2, 0, 1024)
        ffw_at = sGP.v
        gbar()
        for tb in range(4):
            tc = tb * 512
            for rc in range(2):
                key = f"A{rc}"
                guard(key)
                pe.wait(sGP, ffw_at)
                for j in range(NCH):
                    mm = nc.tensor.matmul(bA(rc), u1b[:, j, rc * 128:(rc + 1) * 128],
                                          X1BF[:, j, tc:tc + 512],
                                          start=(j == 0), stop=(j == NCH - 1))
                inc(mm, sPE, 1)
                ve.wait(sPE, sPE.v)
                inc(ve.e.tensor_copy(pqb[:, rc, :], bA(rc)), sVE, 1)
                release(key, sVE)
            guard("B0")
            guard("B1")
            p_at = sVE.v
            for cc in range(32):
                key = f"A{2 + cc % 2}"
                guard(key)
                pe.wait(sVE, p_at)
                for rc in range(2):
                    mm = nc.tensor.matmul(bA(2 + cc % 2), v1b[:, rc, cc * 128:(cc + 1) * 128],
                                          pqb[:, rc, :], start=(rc == 0), stop=(rc == 1))
                inc(mm, sPE, 1)
                hbuf = stg if cc % 2 == 0 else stg2
                hkey = f"H{cc % 2}"
                ac.wait(sPE, sPE.v)
                if hkey in bg:
                    ac.wait(bg[hkey][0], bg[hkey][1])
                inc(ac.e.activation(hbuf[:, :], bA(2 + cc % 2), AF.Gelu_apprx_tanh,
                                    bias=c_b1[:, cc:cc + 1]), sAC, 1)
                release(key, sAC)
                pe.wait(sAC, sAC.v)
                for rc in range(2):
                    mm = nc.tensor.matmul(bB(rc), u2b[:, cc, rc * 128:(rc + 1) * 128],
                                          hbuf[:, :], start=(cc == 0), stop=(cc == 31))
                inc(mm, sPE, 1)
                bg[hkey] = (sPE, sPE.v)
            ve.wait(sPE, sPE.v)
            inc(ve.e.tensor_copy(qqb[:, 0, :], bB(0)), sVE, 1)
            inc(ve.e.tensor_copy(qqb[:, 1, :], bB(1)), sVE, 1)
            release("B0", sVE)
            release("B1", sVE)
            q_at = sVE.v
            for j in range(NCH):
                key = f"A{j % 2}"
                guard(key)
                pe.wait(sVE, q_at)
                for rc in range(2):
                    mm = nc.tensor.matmul(bA(j % 2), v2b[:, rc, j * 128:(j + 1) * 128],
                                          qqb[:, rc, :], start=(rc == 0), stop=(rc == 1))
                inc(mm, sPE, 1)
                dst = XRES[:, j, tc:tc + 512]
                ve.wait(sPE, sPE.v)
                inc(ve.e.scalar_tensor_tensor(dst, bA(j % 2), c_b2[:, j:j + 1], dst,
                                              op0=OP.add, op1=OP.add), sVE, 1)
                release(key, sVE)
        gbar()
        postln(XRES, 0, T)
        if debug:
            sy.wait(sVE, sVE.v)
            inc(sy.e.dma_start(out=dbgo["d_x2"], in_=XRES[:, :, :]), sST, 16)

    with ExitStack() as l1s:
        L = l1s.enter_context
        wq1 = L(nc.sbuf_tensor("wq1", [128, NCH, 512], BF16))
        wk1 = L(nc.sbuf_tensor("wk1", [128, NCH, 128], BF16))
        wv1 = L(nc.sbuf_tensor("wv1", [128, NCH, 64], BF16))
        K2 = L(nc.sbuf_tensor("K2", [128, S], BF16))
        VT1 = L(nc.sbuf_tensor("VT1", [128, NKT * VW], BF16))
        QG = L(nc.sbuf_tensor("QG", [128, NCH, 256], BF16))
        PT0 = L(nc.sbuf_tensor("PX0", [128, NKT * 256], BF16))
        PT1 = L(nc.sbuf_tensor("PX1", [128, NKT * 256], BF16))
        OB1 = L(nc.sbuf_tensor("OB1", [128, NCH, 256], BF16))
        fence(gp)
        wload(gp, sGP, wk1, l1_wkT2, 8, 0, 128)
        wload(gp, sGP, wv1, l1_wvT, 8, 0, 64)
        w1_at = sGP.v
        for tb in range(4):
            key = f"A{tb % 2}"
            guard(key)
            pe.wait(sGP, w1_at)
            for j in range(NCH):
                mm = nc.tensor.matmul(bA(tb % 2), wk1[:, j, :],
                                      X1BF[:, j, tb * 512:(tb + 1) * 512],
                                      start=(j == 0), stop=(j == NCH - 1))
            inc(mm, sPE, 1)
            ve.wait(sPE, sPE.v)
            inc(ve.e.tensor_copy(K2[:, tb * 512:(tb + 1) * 512], bA(tb % 2)), sVE, 1)
            release(key, sVE)
        for kt in range(16):
            key = f"B{kt % 2}"
            guard(key)
            for j in range(NCH):
                mm = nc.tensor.matmul(bB(kt % 2, 64), X1BF[:, j, kt * 128:(kt + 1) * 128],
                                      wv1[:, j, :], start=(j == 0), stop=(j == NCH - 1))
            inc(mm, sPE, 1)
            ve.wait(sPE, sPE.v)
            inc(ve.e.tensor_copy(VT1[:, kt * VW:kt * VW + 64], bB(kt % 2, 64)), sVE, 1)
            release(key, sVE)
            inc(ve.e.memset(VT1[:, kt * VW + 64:kt * VW + 65], 1.0), sVE, 1)
        kv_at = sVE.v
        gp.wait(sVE, kv_at)
        inc(gp.e.dma_start(out=cc_kin.ap(), in_=K2[0:64, 0:T]), sGP, 16)
        gp.wait(sGP, sGP.v)
        inc(gp.e.collective_compute(
            "AllGather", OP.bypass, replica_groups=pairs,
            ins=[cc_kin.ap().opt()], outs=[cc_kout.ap().opt()]), sCC, 1)
        gp.wait(sCC, sCC.v)
        inc(gp.e.dma_start(out=cc_vin.ap(), in_=VT1[:, 0:16 * VW]), sGP, 16)
        gp.wait(sGP, sGP.v)
        inc(gp.e.collective_compute(
            "AllGather", OP.bypass, replica_groups=pairs,
            ins=[cc_vin.ap().opt()], outs=[cc_vout.ap().opt()]), sCC, 1)
        gp.wait(sCC, sCC.v)
        inc(gp.e.dma_start(out=K2[0:64, 0:T], in_=cc_kout.ap()[0:64, :]), sGP, 16)
        gp.wait(sGP, sGP.v)
        inc(gp.e.dma_start(out=K2[0:64, T:S], in_=cc_kout.ap()[64:128, :]), sGP, 16)
        gp.wait(sGP, sGP.v)
        inc(gp.e.dma_start(out=VT1[:, 0:16 * VW], in_=cc_vout.ap()[0:128, :]), sGP, 16)
        gp.wait(sGP, sGP.v)
        inc(gp.e.dma_start(out=VT1[:, 16 * VW:32 * VW], in_=cc_vout.ap()[128:256, :]), sGP, 16)
        gp.wait(sGP, sGP.v)
        inc(gp.e.dma_start(out=K2[64:128, 0:S], in_=K2[0:64, 0:S]), sGP, 16)
        kg_at = sGP.v
        for cb2 in range(8):
            sl = K2[0:64, cb2 * 512:(cb2 + 1) * 512]
            ve.wait(sGP, kg_at)
            ve.wait(sPE, sPE.v)
            inc(ve.e.tensor_tensor(stg[0:64, :], sl, sl, op=OP.mult), sVE, 1)
            guard("B3")
            pe.wait(sVE, sVE.v)
            inc(nc.tensor.matmul(bB(3, 512, 0, 1), c_o64[:, :], stg[0:64, :],
                                 start=True, stop=True), sPE, 1)
            ve.wait(sPE, sPE.v)
            if cb2 == 0:
                inc(ve.e.tensor_reduce(kmax2[:, :], bB(3, 512, 0, 1), axis=AX.X, op=OP.max),
                    sVE, 1)
            else:
                inc(ve.e.tensor_reduce(kmtmp[:, :], bB(3, 512, 0, 1), axis=AX.X, op=OP.max),
                    sVE, 1)
                ve.wait(sVE, sVE.v)
                inc(ve.e.tensor_tensor(kmax2[:, :], kmax2[:, :], kmtmp[:, :], op=OP.max),
                    sVE, 1)
            release("B3", sVE)
        kmax_at = sVE.v
        for qg in range(8):
            qc = qg * 256
            for j in range(NCH):
                if j % 4 == 0:
                    hh = j // 4
                    gp.wait(sPE, sPE.v)
                    wload(gp, sGP, wq1, l1_wqT, 8, hh * 512, 512)
                    wq_at = sGP.v
                key = f"A{j % 2}"
                guard(key)
                pe.wait(sGP, wq_at)
                for dch in range(NCH):
                    mm = nc.tensor.matmul(bA(j % 2, 256),
                                          wq1[:, dch, (j % 4) * 128:(j % 4 + 1) * 128],
                                          X1BF[:, dch, qc:qc + 256],
                                          start=(dch == 0), stop=(dch == NCH - 1))
                inc(mm, sPE, 1)
                ve.wait(sPE, sPE.v)
                inc(ve.e.tensor_copy(QG[:, j, :], bA(j % 2, 256)), sVE, 1)
                release(key, sVE)
                qsl = QG[:, j, :]
                ve.wait(sVE, sVE.v)
                ve.wait(sPE, sPE.v)
                inc(ve.e.tensor_tensor(stg[:, 0:256], qsl, qsl, op=OP.mult), sVE, 1)
                guard("B3")
                pe.wait(sVE, sVE.v)
                inc(nc.tensor.matmul(bB(3, 256, 0, 1), c_oc128[:, :], stg[:, 0:256],
                                     start=True, stop=True), sPE, 1)
                ve.wait(sPE, sPE.v)
                if j == 0:
                    inc(ve.e.tensor_reduce(qmaxs[:, :], bB(3, 256, 0, 1), axis=AX.X,
                                           op=OP.max), sVE, 1)
                else:
                    inc(ve.e.tensor_reduce(qmtmp[:, :], bB(3, 256, 0, 1), axis=AX.X,
                                           op=OP.max), sVE, 1)
                    ve.wait(sVE, sVE.v)
                    inc(ve.e.tensor_tensor(qmaxs[:, :], qmaxs[:, :], qmtmp[:, :],
                                           op=OP.max), sVE, 1)
                release("B3", sVE)
            ve.wait(sVE, sVE.v)
            ve.wait(sVE, kmax_at)
            inc(ve.e.tensor_tensor(bsc[:, :], qmaxs[:, :], kmax2[:, :], op=OP.mult), sVE, 1)
            ac.wait(sVE, sVE.v)
            inc(ac.e.activation(bsc[:, :], bsc[:, :], AF.Ln), sAC, 1)
            ac.wait(sAC, sAC.v)
            inc(ac.e.activation(bsc[:, :], bsc[:, :], AF.Exp, scale=np.float32(0.5),
                                bias=np.float32(-2.0794415416798357)), sAC, 1)
            ve.wait(sAC, sAC.v)
            ve.wait(sPE, sPE.v)
            inc(ve.e.tensor_scalar(rowb[:, :], c_ones1[:, :], bsc[0:1, :], None,
                                   op0=OP.mult), sVE, 1)
            guard("B3")
            pe.wait(sVE, sVE.v)
            inc(nc.tensor.transpose(PS_B[0:128, 1536:1537], rowb[:, :], c_ones1[0:1, 0:1]),
                sPE, 1)
            ve.wait(sPE, sPE.v)
            inc(ve.e.scalar_tensor_tensor(negB[:, :], c_zero[:, :], c_zero[:, :],
                                          PS_B[0:128, 1536:1537],
                                          op0=OP.add, op1=OP.subtract), sVE, 1)
            release("B3", sVE)
            negb_at = sVE.v
            for pr in range(NCH):
                q_lo = QG[0:64, pr, :]
                q_hi = QG[64:128, pr, :]
                for ktg in range(8):
                    for bi in range(4):
                        guard(f"A{bi}")
                    pe.wait(sGP, kg_at)
                    pe.wait(sVE, negb_at)
                    for kk in range(4):
                        kt = ktg * 4 + kk
                        nc.tensor.matmul(PS_A[:, kk * 256:kk * 256 + 256],
                                         K2[0:64, kt * 128:(kt + 1) * 128], q_lo,
                                         start=True, stop=True)
                        mm = nc.tensor.matmul(PS_A[:, 1024 + kk * 256:1024 + kk * 256 + 256],
                                              K2[64:128, kt * 128:(kt + 1) * 128], q_hi,
                                              start=True, stop=True, tile_position=(64, 0))
                    inc(mm, sPE, 1)
                    ac.wait(sPE, sPE.v)
                    ac.wait(sVE, negb_at)
                    inc(ac.e.activation(PT0[:, ktg * 1024:(ktg + 1) * 1024], PS_A[:, 0:1024],
                                        AF.Exp, bias=negB[:, :], scale=np.float32(0.125)),
                        sAC, 1)
                    inc(ac.e.activation(PT1[:, ktg * 1024:(ktg + 1) * 1024],
                                        PS_A[:, 1024:2048], AF.Exp, bias=negB[:, :],
                                        scale=np.float32(0.125)), sAC, 1)
                    for bi in range(4):
                        release(f"A{bi}", sAC)
                for hl, PTx in ((0, PT0), (1, PT1)):
                    key = f"B{hl}"
                    guard(key)
                    pe.wait(sAC, sAC.v)
                    for kt in range(NKT):
                        mm = nc.tensor.matmul(bB(hl, 256, 0, 65), VT1[:, kt * VW:kt * VW + 65],
                                              PTx[:, kt * 256:(kt + 1) * 256],
                                              start=(kt == 0), stop=(kt == NKT - 1))
                    inc(mm, sPE, 1)
                    ve.wait(sPE, sPE.v)
                    inc(ve.e.reciprocal(mtile[0:1, 0:256],
                                        PS_B[64:65, hl * 512:hl * 512 + 256]), sVE, 1)
                    guard("B2")
                    pe.wait(sVE, sVE.v)
                    inc(nc.tensor.matmul(bB(2, 256), c_ones1[:, :], mtile[0:1, 0:256],
                                         start=True, stop=True), sPE, 1)
                    ve.wait(sPE, sPE.v)
                    inc(ve.e.tensor_copy(mtile[:, 0:256], bB(2, 256)), sVE, 1)
                    release("B2", sVE)
                    ve.wait(sVE, sVE.v)
                    inc(ve.e.tensor_tensor(OB1[hl * 64:(hl + 1) * 64, pr, :],
                                           PS_B[0:64, hl * 512:hl * 512 + 256],
                                           mtile[0:64, 0:256], op=OP.mult), sVE, 1)
                    release(key, sVE)
            gbar()
            fence(ve, ac, pe)
            for j in range(NCH):
                dst = XRES[:, j, qc:qc + 256]
                ve.wait(sVE, sVE.v)
                inc(ve.e.scalar_tensor_tensor(dst, OB1[:, j, :], c_zero[:, :], dst,
                                              op0=OP.add, op1=OP.add), sVE, 1)
            postln(XRES, qc, 256)
        if debug:
            sy.wait(sVE, sVE.v)
            inc(sy.e.dma_start(out=dbgo["d_x3"], in_=XRES[:, :, :]), sST, 16)

    with ExitStack() as f1s:
        L = f1s.enter_context
        T1b = L(nc.sbuf_tensor("T1b", [128, NCH, 256], BF16))
        T2b = L(nc.sbuf_tensor("T2b", [128, NCH, 256], BF16))
        wrb = L(nc.sbuf_tensor("wrb", [128, NCH, 512], BF16))
        wib = L(nc.sbuf_tensor("wib", [128, NCH, 512], BF16))
        csl = L(nc.sbuf_tensor("csl", [128, NCH, 256], BF16))
        ssl = L(nc.sbuf_tensor("ssl", [128, NCH, 256], BF16))
        snl = L(nc.sbuf_tensor("snl", [128, NCH, 256], BF16))
        cdo = L(nc.sbuf_tensor("cdo", [128, NCH, 128], BF16))
        sdo = L(nc.sbuf_tensor("sdo", [128, NCH, 128], BF16))
        WEF = L(nc.sbuf_tensor("WEF", [128, NCH, 256], BF16))
        fence(gp, sy)
        for cs in range(4):
            sy.wait(sPE, sPE.v)
            for dst, src_ in ((csl, cCb), (ssl, cSb), (snl, cSnb)):
                wload(sy, sDL, dst, src_, 8, cs * 256, 256)
            cs_at = sDL.v
            for hh in range(2):
                gp.wait(sPE, sPE.v)
                wload(gp, sGP, wrb, wrT_i, 8, hh * 512, 512)
                wload(gp, sGP, wib, wiT_i, 8, hh * 512, 512)
                wr_at = sGP.v
                for fq in range(4):
                    fo = hh * 4 + fq
                    key = f"A{fo % 4}"
                    guard(key)
                    pe.wait(sGP, wr_at)
                    pe.wait(sDL, cs_at)
                    for j in range(NCH):
                        nc.tensor.matmul(bA(fo % 4, 256),
                                         wrb[:, j, fq * 128:(fq + 1) * 128],
                                         csl[:, j, :], start=(j == 0), stop=False)
                    for j in range(NCH):
                        mm = nc.tensor.matmul(bA(fo % 4, 256),
                                              wib[:, j, fq * 128:(fq + 1) * 128],
                                              snl[:, j, :], start=False,
                                              stop=(j == NCH - 1))
                    inc(mm, sPE, 1)
                    ve.wait(sPE, sPE.v)
                    inc(ve.e.tensor_copy(T1b[:, fo, :], bA(fo % 4, 256)), sVE, 1)
                    release(key, sVE)
                    key = f"B{fo % 4}"
                    guard(key)
                    for j in range(NCH):
                        nc.tensor.matmul(bB(fo % 4, 256),
                                         wib[:, j, fq * 128:(fq + 1) * 128],
                                         csl[:, j, :], start=(j == 0), stop=False)
                    for j in range(NCH):
                        mm = nc.tensor.matmul(bB(fo % 4, 256),
                                              wrb[:, j, fq * 128:(fq + 1) * 128],
                                              ssl[:, j, :], start=False,
                                              stop=(j == NCH - 1))
                    inc(mm, sPE, 1)
                    ve.wait(sPE, sPE.v)
                    inc(ve.e.tensor_copy(T2b[:, fo, :], bB(fo % 4, 256)), sVE, 1)
                    release(key, sVE)
            t12_at = sVE.v
            for do in range(NCH):
                sy.wait(sPE, sPE.v)
                wload(sy, sDL, cdo, cCb, 8, do * 128, 128)
                wload(sy, sDL, sdo, cSb, 8, do * 128, 128)
                do_at = sDL.v
                key = f"A{do % 4}"
                guard(key)
                pe.wait(sDL, do_at)
                pe.wait(sVE, t12_at)
                for f_ in range(NCH):
                    nc.tensor.matmul(bA(do % 4, 256), cdo[:, f_, :], T1b[:, f_, :],
                                     start=(f_ == 0), stop=False)
                for f_ in range(NCH):
                    mm = nc.tensor.matmul(bA(do % 4, 256), sdo[:, f_, :], T2b[:, f_, :],
                                          start=False, stop=(f_ == NCH - 1))
                inc(mm, sPE, 1)
                ve.wait(sPE, sPE.v)
                inc(ve.e.tensor_scalar(WEF[:, do, :], bA(do % 4, 256),
                                       c_iD128[:, :], None, op0=OP.mult), sVE, 1)
                release(key, sVE)
            wef_at = sVE.v
            for jj in range(2):
                j = cs * 2 + jj
                for qb in range(4):
                    key = f"B{qb % 2}"
                    guard(key)
                    pe.wait(sVE, wef_at)
                    for dch in range(NCH):
                        mm = nc.tensor.matmul(
                            bB(qb % 2), WEF[:, dch, jj * 128:(jj + 1) * 128],
                            X1BF[:, dch, qb * 512:(qb + 1) * 512],
                            start=(dch == 0), stop=(dch == NCH - 1))
                    inc(mm, sPE, 1)
                    dst = XRES[:, j, qb * 512:(qb + 1) * 512]
                    ve.wait(sPE, sPE.v)
                    inc(ve.e.scalar_tensor_tensor(dst, bB(qb % 2), c_l1b[:, j:j + 1], dst,
                                                  op0=OP.add, op1=OP.add), sVE, 1)
                    release(key, sVE)
        postln(XRES, 0, T)

    XOT = [E(nc.sbuf_tensor(f"XOT{i}", [128, 1024], F32)) for i in range(2)]
    xot_free = [0, 0]
    for t in range(NQT):
        b = t % 2
        for j in range(NCH):
            key = f"B{2 + j % 2}"
            guard(key)
            pe.wait(sVE, sVE.v)
            inc(nc.tensor.transpose(bB(2 + j % 2, 128), XRES[:, j, t * 128:(t + 1) * 128],
                                    c_ident[:, :]), sPE, 1)
            ve.wait(sPE, sPE.v)
            ve.wait(sST, xot_free[b])
            inc(ve.e.tensor_copy(XOT[b][:, j * 128:(j + 1) * 128], bB(2 + j % 2, 128)), sVE, 1)
            release(key, sVE)
        sy.wait(sVE, sVE.v)
        sy.wait(sST, sST.v)
        inc(sy.e.dma_start(out=xout[t * 128:(t + 1) * 128, :], in_=XOT[b][:, :]), sST, 16)
        xot_free[b] = sST.v
    gp.wait(sST, sST.v)
    gp.wait(sDT, sDT.v)
    gp.wait(sDL, sDL.v)
    gp.wait(sGP, sGP.v)

    est.close()
    return nc


def _mk_consts():
    d = np.arange(D, dtype=np.int64)
    prod = np.outer(d, d) % D
    ang = 2.0 * np.pi * prod.astype(np.float64) / D
    return np.cos(ang), np.sin(ang)


def kernel(
    L0_attn_x, L0_attn_mask, L0_attn_Wq, L0_attn_Wk, L0_attn_Wv, L0_attn_gq, L0_attn_bq,
    L0_attn_gk, L0_attn_bk, L0_attn_Wo, L0_attn_bo, L0_ffn_U1, L0_ffn_V1, L0_ffn_b1,
    L0_ffn_U2, L0_ffn_V2, L0_ffn_b2, L1_attn_Wq, L1_attn_Wk, L1_attn_Wv,
    L1_ffn_w_freq_real, L1_ffn_w_freq_imag, L1_ffn_bias,
):
    f32 = np.float32
    bf16 = ml_dtypes.bfloat16
    x = np.ascontiguousarray(np.asarray(L0_attn_x, f32))

    def t2(a):
        return np.ascontiguousarray(np.asarray(a, f32).T)

    def col(a, n=128):
        a = np.asarray(a, f32).reshape(-1)
        return np.ascontiguousarray(a.reshape(-1, n).T)

    def rep2(a):
        a = np.asarray(a, f32).reshape(-1)
        return np.ascontiguousarray(np.concatenate([a, a])[:, None])

    C, Sm = _mk_consts()
    e2 = np.zeros((128, 2), f32)
    e2[0:64, 0] = 1.0
    e2[64:128, 1] = 1.0
    shared = {
        "l0_wqT": t2(L0_attn_Wq), "l0_wkT": t2(L0_attn_Wk), "l0_wvT": t2(L0_attn_Wv),
        "l0_woT": t2(L0_attn_Wo), "l0_bo": col(L0_attn_bo),
        "l0_gq": rep2(L0_attn_gq), "l0_bq": rep2(L0_attn_bq),
        "l0_gk": rep2(L0_attn_gk), "l0_bk": rep2(L0_attn_bk),
        "l0_u1": np.ascontiguousarray(np.asarray(L0_ffn_U1, f32)),
        "l0_v1": np.ascontiguousarray(np.asarray(L0_ffn_V1, f32)),
        "l0_b1": col(L0_ffn_b1),
        "l0_u2": np.ascontiguousarray(np.asarray(L0_ffn_U2, f32)),
        "l0_v2": np.ascontiguousarray(np.asarray(L0_ffn_V2, f32)),
        "l0_b2": col(L0_ffn_b2),
        "l1_wqT": t2(L1_attn_Wq),
        "l1_wkT2": np.ascontiguousarray(
            np.concatenate([np.asarray(L1_attn_Wk, f32).T] * 2, axis=1)),
        "l1_wvT": t2(L1_attn_Wv),
        "l1_bias": col(L1_ffn_bias),
        "wrT": t2(L1_ffn_w_freq_real), "wiT": t2(L1_ffn_w_freq_imag),
        "cCb": C.astype(bf16), "cSb": Sm.astype(bf16), "cSnb": (-Sm).astype(bf16),
        "e2bf": e2.astype(bf16),
        "e2tf": np.ascontiguousarray(e2.T),
        "ones1": np.ones((1, 128), f32),
        "neg1": -np.ones((1, 128), f32),
        "onesc": np.ones((128, 1), f32),
        "identf": np.eye(128, dtype=f32),
        "o64": np.ones((64, 1), bf16),
        "oc128": np.ones((128, 1), bf16),
    }
    trilm = np.where(np.tril(np.ones((128, 128), bool)), 0.0, -1e30).astype(f32)
    fullm = np.full((128, 128), -1e30, f32)
    zerom = np.zeros((128, 128), f32)

    nc = build()
    in_maps = []
    for c in range(8):
        n, f = c // 2, c % 2
        tiles = [2 * i + f for i in range(16)]
        xq_rows = np.ascontiguousarray(
            np.concatenate([x[n, t * 128:(t + 1) * 128, :] for t in tiles], axis=0))
        im = dict(shared)
        im["xq"] = xq_rows
        im["xb"] = np.ascontiguousarray(x[n])
        im["mask_a"] = trilm if f == 0 else zerom
        im["mask_b"] = fullm if f == 0 else trilm
        in_maps.append(im)

    res = run_bass_kernel_spmd(nc, in_maps, core_ids=list(range(8)))
    out = np.empty((4, 4096, 1024), f32)
    for c in range(8):
        n, f = c // 2, c % 2
        r = np.asarray(res.results[c]["xout"], f32)
        for i in range(16):
            t = 2 * i + f
            out[n, t * 128:(t + 1) * 128, :] = r[i * 128:(i + 1) * 128, :]
    return out


# version 06
# speedup vs baseline: 1.0073x; 1.0073x over previous
import sys

sys.path.insert(0, "/opt/trn_rl_repo")
from contextlib import ExitStack

import numpy as np
import ml_dtypes
import concourse.bass as bass
import concourse.mybir as mybir
from concourse.bass_utils import run_bass_kernel_spmd
from concourse.alu_op_type import AluOpType as OP

F32 = mybir.dt.float32
BF16 = mybir.dt.bfloat16
AF = mybir.ActivationFunctionType
AX = mybir.AxisListType

D = 1024
S = 4096
NCH = 8
T = 2048
NQT = 16
NKT = 32
VW = 72


class Sem:
    __slots__ = ("h", "v")

    def __init__(self, h):
        self.h = h
        self.v = 0


class Wr:
    def __init__(self, e):
        self.e = e
        self.seen = {}

    def wait(self, sem, val=None):
        if val is None:
            val = sem.v
        if val <= 0 or self.seen.get(id(sem), 0) >= val:
            return
        self.e.wait_ge(sem.h, val)
        self.seen[id(sem)] = val


def inc(instr, sem, n=1):
    instr.then_inc(sem.h, n)
    sem.v += n


DBGQG = 0


def build(debug=0):
    nc = bass.Bass("TRN2", target_bir_lowering=False)

    def din(name, shape, dt=F32):
        return nc.dram_tensor(name, shape, dt, kind="ExternalInput").ap()

    xq = din("xq", [T, D])
    xb = din("xb", [S, D])
    mask_a = din("mask_a", [128, 128])
    mask_b = din("mask_b", [128, 128])
    l0_wqT = din("l0_wqT", [D, D])
    l0_wkT = din("l0_wkT", [D, D])
    l0_wvT = din("l0_wvT", [D, D])
    l0_woT = din("l0_woT", [D, D])
    l0_bo = din("l0_bo", [128, 8])
    l0_gq = din("l0_gq", [128, 1])
    l0_bq = din("l0_bq", [128, 1])
    l0_gk = din("l0_gk", [128, 1])
    l0_bk = din("l0_bk", [128, 1])
    l0_u1 = din("l0_u1", [D, 256])
    l0_v1 = din("l0_v1", [256, 4096])
    l0_b1 = din("l0_b1", [128, 32])
    l0_u2 = din("l0_u2", [4096, 256])
    l0_v2 = din("l0_v2", [256, D])
    l0_b2 = din("l0_b2", [128, 8])
    l1_wqT = din("l1_wqT", [D, D])
    l1_wkT2 = din("l1_wkT2", [D, 128])
    l1_wvT = din("l1_wvT", [D, 64])
    l1_bias = din("l1_bias", [128, 8])
    wrT_i = din("wrT", [D, D])
    wiT_i = din("wiT", [D, D])
    cCb = din("cCb", [D, D], BF16)
    cSb = din("cSb", [D, D], BF16)
    cSnb = din("cSnb", [D, D], BF16)
    e2bf_i = din("e2bf", [128, 2], BF16)
    e2tf_i = din("e2tf", [2, 128])
    ones1_i = din("ones1", [1, 128])
    neg1_i = din("neg1", [1, 128])
    onesc_i = din("onesc", [128, 1])
    identf_i = din("identf", [128, 128])
    o64_i = din("o64", [64, 1], BF16)
    oc128_i = din("oc128", [128, 1], BF16)

    xout = nc.dram_tensor("xout", [T, D], F32, kind="ExternalOutput").ap()
    dbgo = {}
    if debug:
        for nm in ("d_x1", "d_x2", "d_x3"):
            dbgo[nm] = nc.dram_tensor(nm, [128, NCH, T], F32, kind="ExternalOutput").ap()
        dbgo["d_k2"] = nc.dram_tensor("d_k2", [128, S], F32, kind="ExternalOutput").ap()
        dbgo["d_vt1"] = nc.dram_tensor("d_vt1", [128, NKT * VW], F32,
                                       kind="ExternalOutput").ap()
        dbgo["d_negb"] = nc.dram_tensor("d_negb", [128, 1], F32, kind="ExternalOutput").ap()
        dbgo["d_qg"] = nc.dram_tensor("d_qg", [128, NCH, 256], F32,
                                      kind="ExternalOutput").ap()
        dbgo["d_pt0"] = nc.dram_tensor("d_pt0", [128, 1024], F32, kind="ExternalOutput").ap()
        dbgo["d_ob1"] = nc.dram_tensor("d_ob1", [128, NCH, 256], F32,
                                       kind="ExternalOutput").ap()
        dbgo["d_scal"] = nc.dram_tensor("d_scal", [1, 16], F32, kind="ExternalOutput").ap()
        dbgo["d_xr"] = nc.dram_tensor("d_xr", [128, NCH, 512], F32,
                                      kind="ExternalOutput").ap()
        dbgo["d_xr0"] = nc.dram_tensor("d_xr0", [128, NCH, 256], F32,
                                       kind="ExternalOutput").ap()
        dbgo["d_stm"] = nc.dram_tensor("d_stm", [2, 2048], F32, kind="ExternalOutput").ap()
        dbgo["d_sts"] = nc.dram_tensor("d_sts", [2, 2048], F32, kind="ExternalOutput").ap()
        dbgo["d_str"] = nc.dram_tensor("d_str", [2, 2048], F32, kind="ExternalOutput").ap()

    cc_kin = nc.dram_tensor("cc_kin", [64, T], BF16)
    cc_kout = nc.dram_tensor("cc_kout", [128, T], BF16)
    cc_vin = nc.dram_tensor("cc_vin", [128, NQT * VW], BF16)
    cc_vout = nc.dram_tensor("cc_vout", [256, NQT * VW], BF16)
    pairs = [[0, 1], [2, 3], [4, 5], [6, 7]]

    est = ExitStack()
    E = est.enter_context

    sDL = Sem(E(nc.semaphore("sDL")))
    sDT = Sem(E(nc.semaphore("sDT")))
    sST = Sem(E(nc.semaphore("sST")))
    sPE = Sem(E(nc.semaphore("sPE")))
    sVE = Sem(E(nc.semaphore("sVE")))
    sAC = Sem(E(nc.semaphore("sAC")))
    sGP = Sem(E(nc.semaphore("sGP")))
    sCC = Sem(E(nc.semaphore("sCC")))
    sX0 = Sem(E(nc.semaphore("sX0")))
    sX1 = Sem(E(nc.semaphore("sX1")))
    sX = [sX0, sX1]
    sGB = Sem(E(nc.semaphore("sGB")))
    ALLSEMS = [sDL, sDT, sST, sPE, sVE, sAC, sGP, sCC]

    pe = Wr(nc.tensor)
    ve = Wr(nc.vector)
    ac = Wr(nc.scalar)
    gp = Wr(nc.gpsimd)
    sy = Wr(nc.sync)

    def fence(*engs):
        for e_ in engs:
            for sm_ in ALLSEMS:
                e_.wait(sm_, sm_.v)

    def gbar():
        gp.wait(sVE, sVE.v)
        gp.wait(sAC, sAC.v)
        inc(gp.e.nop(), sGB, 1)
        ve.wait(sGB, sGB.v)
        ac.wait(sGB, sGB.v)
        pe.wait(sGB, sGB.v)

    def wload(engw, sem, dst3, src2d, nj, c0, w):
        engw.wait(sem, sem.v)
        for j_ in range(nj):
            inc(engw.e.dma_start(out=dst3[:, j_, :],
                                 in_=src2d[j_ * 128:(j_ + 1) * 128, c0:c0 + w]), sem, 16)

    c_e2bf = E(nc.sbuf_tensor("c_e2bf", [128, 2], BF16))
    c_e2tf = E(nc.sbuf_tensor("c_e2tf", [2, 128], F32))
    c_ones1 = E(nc.sbuf_tensor("c_ones1", [1, 128], F32))
    c_neg1 = E(nc.sbuf_tensor("c_neg1", [1, 128], F32))
    c_onesc = E(nc.sbuf_tensor("c_onesc", [128, 1], F32))
    c_ident = E(nc.sbuf_tensor("c_ident", [128, 128], F32))
    c_maska = E(nc.sbuf_tensor("c_maska", [128, 128], F32))
    c_maskb = E(nc.sbuf_tensor("c_maskb", [128, 128], F32))
    c_o64 = E(nc.sbuf_tensor("c_o64", [64, 1], BF16))
    c_oc128 = E(nc.sbuf_tensor("c_oc128", [128, 1], BF16))
    rowb = E(nc.sbuf_tensor("rowb", [1, 128], F32))
    qmtmp = E(nc.sbuf_tensor("qmtmp", [1, 1], F32))
    qmaxs = E(nc.sbuf_tensor("qmaxs", [1, 1], F32))
    dscal = E(nc.sbuf_tensor("dscal", [1, 16], F32))
    c_gq = E(nc.sbuf_tensor("c_gq", [128, 1], F32))
    c_bq = E(nc.sbuf_tensor("c_bq", [128, 1], F32))
    c_gk = E(nc.sbuf_tensor("c_gk", [128, 1], F32))
    c_bk = E(nc.sbuf_tensor("c_bk", [128, 1], F32))
    c_bo = E(nc.sbuf_tensor("c_bo", [128, 8], F32))
    c_b1 = E(nc.sbuf_tensor("c_b1", [128, 32], F32))
    c_b2 = E(nc.sbuf_tensor("c_b2", [128, 8], F32))
    c_l1b = E(nc.sbuf_tensor("c_l1b", [128, 8], F32))
    c_zero = E(nc.sbuf_tensor("c_zero", [128, 1], F32))
    c_i64 = E(nc.sbuf_tensor("c_i64", [2, 1], F32))
    c_iD = E(nc.sbuf_tensor("c_iD", [1, 1], F32))
    c_iD128 = E(nc.sbuf_tensor("c_iD128", [128, 1], F32))
    c_eps2 = E(nc.sbuf_tensor("c_eps2", [2, 1], F32))

    amax = E(nc.sbuf_tensor("amax", [128, 16], F32))
    negm2 = E(nc.sbuf_tensor("negm2", [128, 2], F32))
    ellp = E(nc.sbuf_tensor("ellp", [128, 8], F32))
    recl2 = E(nc.sbuf_tensor("recl2", [128, 2], F32))
    reclT = E(nc.sbuf_tensor("reclT", [2, 128], F32))
    mtile = E(nc.sbuf_tensor("mtile", [128, 256], F32))
    st_m = E(nc.sbuf_tensor("st_m", [2, 2048], F32))
    st_s = E(nc.sbuf_tensor("st_s", [2, 2048], F32))
    st_r = E(nc.sbuf_tensor("st_r", [2, 2048], F32))
    st_t = E(nc.sbuf_tensor("st_t", [2, 2048], F32))
    kmax2 = E(nc.sbuf_tensor("kmax2", [1, 1], F32))
    kmtmp = E(nc.sbuf_tensor("kmtmp", [1, 1], F32))
    qmax2 = E(nc.sbuf_tensor("qmax2", [2, 8], F32))
    qmaxr = E(nc.sbuf_tensor("qmaxr", [2, 1], F32))
    qmaxf = E(nc.sbuf_tensor("qmaxf", [1, 2], F32))
    bsc = E(nc.sbuf_tensor("bsc", [1, 1], F32))
    negB = E(nc.sbuf_tensor("negB", [128, 1], F32))
    tmpa = E(nc.sbuf_tensor("tmpa", [128, 512], F32))
    tmpb = E(nc.sbuf_tensor("tmpb", [128, 512], F32))
    stg = E(nc.sbuf_tensor("stg", [128, 512], BF16))
    stg2 = E(nc.sbuf_tensor("stg2", [128, 512], BF16))

    X1BF = E(nc.sbuf_tensor("X1BF", [128, NCH, T], BF16))
    XTM = [E(nc.sbuf_tensor(f"XTM{i}", [128, 1024], BF16)) for i in range(2)]

    PS_A = E(nc.psum_tensor("PS_A", [128, 2048], F32))
    PS_B = E(nc.psum_tensor("PS_B", [128, 2048], F32))

    def bA(i, w=512, p0=0, p1=128):
        return PS_A[p0:p1, i * 512:i * 512 + w]

    def bB(i, w=512, p0=0, p1=128):
        return PS_B[p0:p1, i * 512:i * 512 + w]

    bg = {}

    def guard(key):
        if key in bg:
            sem, val = bg[key]
            pe.wait(sem, val)

    def release(key, sem):
        bg[key] = (sem, sem.v)

    for dst, src in [
        (c_e2bf, e2bf_i), (c_e2tf, e2tf_i), (c_ones1, ones1_i), (c_neg1, neg1_i),
        (c_onesc, onesc_i), (c_ident, identf_i), (c_maska, mask_a), (c_maskb, mask_b),
        (c_o64, o64_i), (c_oc128, oc128_i),
        (c_gq, l0_gq), (c_bq, l0_bq), (c_gk, l0_gk), (c_bk, l0_bk),
        (c_bo, l0_bo), (c_b1, l0_b1), (c_b2, l0_b2), (c_l1b, l1_bias),
    ]:
        inc(sy.e.dma_start(out=dst[:, :], in_=src), sDL, 16)
    nc.vector.memset(c_zero[:, :], 0.0)
    nc.vector.memset(c_i64[:, :], 1.0 / 64.0)
    nc.vector.memset(c_iD[:, :], 1.0 / 1024.0)
    nc.vector.memset(c_iD128[:, :], 1.0 / 1024.0)
    inc(nc.vector.memset(c_eps2[:, :], 1e-5), sVE, 1)
    consts_d = sDL.v
    nc.sync.wait_ge(sDL.h, consts_d)

    xtm_free = [0, 0]

    def load_fm(dst3, src2d, nt, coff=0):
        sy.wait(sDT, sDT.v)
        for t in range(nt):
            b = t % 2
            sb = sX[b]
            gp.wait(sDT, sDT.v)
            inc(gp.e.dma_start(out=XTM[b][:, :], in_=src2d[t * 128:(t + 1) * 128, :]), sb, 16)
            sy.wait(sb, sb.v)
            inc(sy.e.dma_start_transpose(
                out=dst3[:, :, coff + t * 128:coff + (t + 1) * 128], in_=XTM[b][:, :]), sDT, 16)
            xtm_free[b] = sDT.v

    def rstd_from_stats(n, rows=2):
        ve.wait(sVE, sVE.v)
        ve.wait(sAC, sAC.v)
        inc(ve.e.tensor_tensor(st_t[0:rows, :n], st_m[0:rows, :n], st_m[0:rows, :n],
                               op=OP.mult), sVE, 1)
        ve.wait(sVE, sVE.v)
        inc(ve.e.tensor_tensor(st_t[0:rows, :n], st_s[0:rows, :n], st_t[0:rows, :n],
                               op=OP.subtract), sVE, 1)
        ac.wait(sVE, sVE.v)
        ac.wait(sPE, sPE.v)
        inc(ac.e.activation(st_r[0:rows, :n], st_t[0:rows, :n], AF.Ln,
                            bias=c_eps2[0:rows, :]), sAC, 1)
        ac.wait(sAC, sAC.v)
        inc(ac.e.activation(st_r[0:rows, :n], st_r[0:rows, :n], AF.Exp,
                            scale=np.float32(-0.5)), sAC, 1)
        return sAC.v

    def postln(XRES, c0, ct):
        nqb = max(1, ct // 512)
        bw = min(512, ct)
        for qb in range(nqb):
            cs = c0 + qb * bw
            lo = qb * bw
            guard("A0")
            guard("A1")
            for j in range(NCH):
                src = XRES[:, j, cs:cs + bw]
                ve.wait(sPE, sPE.v)
                inc(ve.e.tensor_tensor(tmpa[:, :bw], src, src, op=OP.mult), sVE, 1)
                pe.wait(sVE, sVE.v)
                nc.tensor.matmul(bA(0, bw, 0, 1), c_onesc[:, :], src,
                                 start=(j == 0), stop=(j == NCH - 1))
                mm = nc.tensor.matmul(bA(1, bw, 0, 1), c_onesc[:, :], tmpa[:, :bw],
                                      start=(j == 0), stop=(j == NCH - 1))
                inc(mm, sPE, 1)
            ve.wait(sPE, sPE.v)
            ve.wait(sAC, sAC.v)
            inc(ve.e.tensor_scalar(st_m[0:1, lo:lo + bw], bA(0, bw, 0, 1),
                                   c_iD[0:1, :], None, op0=OP.mult), sVE, 1)
            inc(ve.e.tensor_scalar(st_s[0:1, lo:lo + bw], bA(1, bw, 0, 1),
                                   c_iD[0:1, :], None, op0=OP.mult), sVE, 1)
            release("A0", sVE)
            release("A1", sVE)
        rstd_at = rstd_from_stats(ct, rows=1)
        for qb in range(nqb):
            cs = c0 + qb * bw
            lo = qb * bw
            guard("A2")
            guard("A3")
            pe.wait(sAC, rstd_at)
            inc(nc.tensor.matmul(bA(2, bw), c_ones1[:, :], st_m[0:1, lo:lo + bw],
                                 start=True, stop=True), sPE, 1)
            inc(nc.tensor.matmul(bA(3, bw), c_ones1[:, :], st_r[0:1, lo:lo + bw],
                                 start=True, stop=True), sPE, 1)
            ve.wait(sPE, sPE.v)
            for j in range(NCH):
                src = XRES[:, j, cs:cs + bw]
                inc(ve.e.scalar_tensor_tensor(tmpa[:, :bw], src, c_zero[:, :], bA(2, bw),
                                              op0=OP.add, op1=OP.subtract), sVE, 1)
                ve.wait(sVE, sVE.v)
                inc(ve.e.tensor_tensor(src, tmpa[:, :bw], bA(3, bw), op=OP.mult), sVE, 1)
                ve.wait(sVE, sVE.v)
                inc(ve.e.tensor_copy(X1BF[:, j, cs:cs + bw], src), sVE, 1)
            release("A2", sVE)
            release("A3", sVE)

    def kqstats(buf_slice, m_dst, s_dst):
        ve.wait(sPE, sPE.v)
        inc(ve.e.tensor_tensor(stg[:, :], buf_slice, buf_slice, op=OP.mult), sVE, 1)
        guard("B2")
        guard("B3")
        pe.wait(sVE, sVE.v)
        nc.tensor.matmul(bB(2, 512, 0, 2), c_e2bf[:, :], buf_slice, start=True, stop=True)
        inc(nc.tensor.matmul(bB(3, 512, 0, 2), c_e2bf[:, :], stg[:, :],
                             start=True, stop=True), sPE, 1)
        ve.wait(sPE, sPE.v)
        inc(ve.e.tensor_scalar(m_dst, bB(2, 512, 0, 2), c_i64[:, :], None, op0=OP.mult),
            sVE, 1)
        inc(ve.e.tensor_scalar(s_dst, bB(3, 512, 0, 2), c_i64[:, :], None, op0=OP.mult),
            sVE, 1)
        release("B2", sVE)
        release("B3", sVE)

    def lnapply(buf3, cl, c0, n, rstd_at, gv, bv):
        for cb_ in range(n // 512):
            col = c0 + cb_ * 512
            guard("A2")
            guard("A3")
            pe.wait(sAC, rstd_at)
            pe.wait(sVE, sVE.v)
            inc(nc.tensor.matmul(bA(2), c_e2tf[:, :], st_m[0:2, cb_ * 512:(cb_ + 1) * 512],
                                 start=True, stop=True), sPE, 1)
            inc(nc.tensor.matmul(bA(3), c_e2tf[:, :], st_r[0:2, cb_ * 512:(cb_ + 1) * 512],
                                 start=True, stop=True), sPE, 1)
            dst = buf3[:, cl, col:col + 512] if cl is not None else buf3[:, col:col + 512]
            ve.wait(sPE, sPE.v)
            inc(ve.e.scalar_tensor_tensor(tmpa[:, :], dst, c_zero[:, :], bA(2),
                                          op0=OP.add, op1=OP.subtract), sVE, 1)
            ve.wait(sVE, sVE.v)
            inc(ve.e.tensor_tensor(tmpb[:, :], tmpa[:, :], bA(3), op=OP.mult), sVE, 1)
            ve.wait(sVE, sVE.v)
            inc(ve.e.tensor_scalar(dst, tmpb[:, :], gv[:, :], bv[:, :],
                                   op0=OP.mult, op1=OP.add), sVE, 1)
            release("A2", sVE)
            release("A3", sVE)

    sy.wait(sDL, consts_d)
    with ExitStack() as l0s:
        L = l0s.enter_context
        XQBF = L(nc.sbuf_tensor("XQBF", [128, NCH, T], BF16))
        XBF = L(nc.sbuf_tensor("XBF", [128, NCH, T], BF16))
        KH = L(nc.sbuf_tensor("KH", [128, S], BF16))
        VT = L(nc.sbuf_tensor("VT", [128, NKT * 128], BF16))
        QH = L(nc.sbuf_tensor("QH", [128, T], BF16))
        PB = [L(nc.sbuf_tensor(f"PB{i}", [128, 4096], BF16)) for i in range(2)]
        PT = [L(nc.sbuf_tensor(f"PT{i}", [128, NKT, 128], BF16)) for i in range(2)]
        wq_g = L(nc.sbuf_tensor("wq_g", [128, NCH, 128], BF16))
        wk_g = L(nc.sbuf_tensor("wk_g", [128, NCH, 128], BF16))
        wv_g = L(nc.sbuf_tensor("wv_g", [128, NCH, 128], BF16))

        load_fm(XQBF, xq, 16)
        xq_fm_at = sDT.v
        pb_free = 0
        pt_free = 0

        for g in range(8):
            gp.wait(sPE, sPE.v)
            for dst, srcw in [(wq_g, l0_wqT), (wk_g, l0_wkT), (wv_g, l0_wvT)]:
                wload(gp, sGP, dst, srcw, 8, g * 128, 128)
            wg_at = sGP.v

            for half in range(2):
                sy.wait(sPE, sPE.v)
                load_fm(XBF, xb[half * T:(half + 1) * T, :], 16)
                xbf_at = sDT.v
                for tb in range(4):
                    key = f"A{tb % 2}"
                    guard(key)
                    pe.wait(sGP, wg_at)
                    pe.wait(sDT, xbf_at)
                    for j in range(NCH):
                        mm = nc.tensor.matmul(bA(tb % 2), wk_g[:, j, :],
                                              XBF[:, j, tb * 512:(tb + 1) * 512],
                                              start=(j == 0), stop=(j == NCH - 1))
                    inc(mm, sPE, 1)
                    khs = KH[:, half * T + tb * 512:half * T + (tb + 1) * 512]
                    ve.wait(sPE, sPE.v)
                    inc(ve.e.tensor_copy(khs, bA(tb % 2)), sVE, 1)
                    release(key, sVE)
                    ve.wait(sVE, sVE.v)
                    kqstats(khs, st_m[:, tb * 512:(tb + 1) * 512],
                            st_s[:, tb * 512:(tb + 1) * 512])
                r_at = rstd_from_stats(T)
                lnapply(KH, None, half * T, T, r_at, c_gk, c_bk)
                for kt in range(16):
                    key = f"B{kt % 2}"
                    guard(key)
                    for j in range(NCH):
                        mm = nc.tensor.matmul(bB(kt % 2, 128), XBF[:, j, kt * 128:(kt + 1) * 128],
                                              wv_g[:, j, :], start=(j == 0), stop=(j == NCH - 1))
                    inc(mm, sPE, 1)
                    gkt = half * 16 + kt
                    ve.wait(sPE, sPE.v)
                    inc(ve.e.tensor_copy(VT[:, gkt * 128:(gkt + 1) * 128], bB(kt % 2, 128)),
                        sVE, 1)
                    release(key, sVE)
            for tb in range(4):
                key = f"A{tb % 2}"
                guard(key)
                pe.wait(sDT, xq_fm_at)
                for j in range(NCH):
                    mm = nc.tensor.matmul(bA(tb % 2), wq_g[:, j, :],
                                          XQBF[:, j, tb * 512:(tb + 1) * 512],
                                          start=(j == 0), stop=(j == NCH - 1))
                inc(mm, sPE, 1)
                qhs = QH[:, tb * 512:(tb + 1) * 512]
                ve.wait(sPE, sPE.v)
                inc(ve.e.tensor_copy(qhs, bA(tb % 2)), sVE, 1)
                release(key, sVE)
                ve.wait(sVE, sVE.v)
                kqstats(qhs, st_m[:, tb * 512:(tb + 1) * 512],
                        st_s[:, tb * 512:(tb + 1) * 512])
            r_at = rstd_from_stats(T)
            lnapply(QH, None, 0, T, r_at, c_gq, c_bq)
            gbar()
            lnk_at = sVE.v
            for qt in range(NQT):
                nk = 2 * qt + 2
                qlh = QH[0:64, qt * 128:(qt + 1) * 128]
                qhh = QH[64:128, qt * 128:(qt + 1) * 128]
                ca = (nk + 3) // 4
                for c in range(ca):
                    w = min(4, nk - 4 * c) * 128
                    k0 = c * 512
                    b1_, b2_ = c % 2, 2 + c % 2
                    guard(f"A{b1_}")
                    guard(f"A{b2_}")
                    pe.wait(sVE, lnk_at)
                    nc.tensor.matmul(bA(b1_, w), qlh, KH[0:64, k0:k0 + w],
                                     start=True, stop=True)
                    inc(nc.tensor.matmul(bA(b2_, w), qhh, KH[64:128, k0:k0 + w],
                                         start=True, stop=True, tile_position=(64, 0)),
                        sPE, 1)
                    ve.wait(sPE, sPE.v)
                    if c == ca - 1:
                        for bi in (b1_, b2_):
                            inc(ve.e.tensor_tensor(bA(bi, 512)[:, w - 256:w - 128],
                                                   bA(bi, 512)[:, w - 256:w - 128],
                                                   c_maska[:, :], op=OP.add), sVE, 1)
                            inc(ve.e.tensor_tensor(bA(bi, 512)[:, w - 128:w],
                                                   bA(bi, 512)[:, w - 128:w],
                                                   c_maskb[:, :], op=OP.add), sVE, 1)
                        ve.wait(sVE, sVE.v)
                    inc(ve.e.tensor_reduce(amax[:, c:c + 1], bA(b1_, w), axis=AX.X,
                                           op=OP.max), sVE, 1)
                    inc(ve.e.tensor_reduce(amax[:, 8 + c:9 + c], bA(b2_, w), axis=AX.X,
                                           op=OP.max), sVE, 1)
                    release(f"A{b1_}", sVE)
                    release(f"A{b2_}", sVE)
                ve.wait(sVE, sVE.v)
                ve.wait(sAC, sAC.v)
                inc(ve.e.tensor_reduce(negm2[:, 0:1], amax[:, 0:ca], axis=AX.X,
                                       op=OP.max, negate=True), sVE, 1)
                inc(ve.e.tensor_reduce(negm2[:, 1:2], amax[:, 8:8 + ca], axis=AX.X,
                                       op=OP.max, negate=True), sVE, 1)
                negm_at = sVE.v
                cb = (nk + 7) // 8
                for c in range(cb):
                    w = min(8, nk - 8 * c) * 128
                    k0 = c * 1024
                    PSX = PS_A if c % 2 == 0 else PS_B
                    kyb = "A" if c % 2 == 0 else "B"
                    for bi in range(4):
                        guard(f"{kyb}{bi}")
                    for sub in range((w + 511) // 512):
                        ws = min(512, w - sub * 512)
                        o_ = sub * 512
                        nc.tensor.matmul(PSX[:, o_:o_ + ws], qlh,
                                         KH[0:64, k0 + o_:k0 + o_ + ws],
                                         start=True, stop=True)
                        mm = nc.tensor.matmul(PSX[:, 1024 + o_:1024 + o_ + ws], qhh,
                                              KH[64:128, k0 + o_:k0 + o_ + ws],
                                              start=True, stop=True, tile_position=(64, 0))
                    inc(mm, sPE, 1)
                    if c == cb - 1:
                        ve.wait(sPE, sPE.v)
                        for off in (0, 1024):
                            inc(ve.e.tensor_tensor(PSX[:, off + w - 256:off + w - 128],
                                                   PSX[:, off + w - 256:off + w - 128],
                                                   c_maska[:, :], op=OP.add), sVE, 1)
                            inc(ve.e.tensor_tensor(PSX[:, off + w - 128:off + w],
                                                   PSX[:, off + w - 128:off + w],
                                                   c_maskb[:, :], op=OP.add), sVE, 1)
                    ac.wait(sPE, sPE.v)
                    ac.wait(sVE, sVE.v)
                    ac.wait(sDT, pb_free)
                    inc(ac.e.activation(PB[0][:, k0:k0 + w], PSX[:, 0:w], AF.Exp,
                                        bias=negm2[:, 0:1], accum_out=ellp[:, c:c + 1]),
                        sAC, 1)
                    inc(ac.e.activation(PB[1][:, k0:k0 + w], PSX[:, 1024:1024 + w], AF.Exp,
                                        bias=negm2[:, 1:2], accum_out=ellp[:, 4 + c:5 + c]),
                        sAC, 1)
                    for bi in range(4):
                        release(f"{kyb}{bi}", sAC)
                exp_at = sAC.v
                ve.wait(sAC, exp_at)
                ve.wait(sPE, sPE.v)
                inc(ve.e.tensor_reduce(recl2[:, 0:1], ellp[:, 0:cb], axis=AX.X, op=OP.add),
                    sVE, 1)
                inc(ve.e.tensor_reduce(recl2[:, 1:2], ellp[:, 4:4 + cb], axis=AX.X,
                                       op=OP.add), sVE, 1)
                ve.wait(sVE, sVE.v)
                inc(ve.e.reciprocal(recl2[:, 0:2], recl2[:, 0:2]), sVE, 1)
                recl_at = sVE.v
                sy.wait(sAC, exp_at)
                sy.wait(sPE, pt_free)
                sy.wait(sDT, sDT.v)
                inc(sy.e.dma_start_transpose(out=PT[0][:, 0:nk, :],
                                             in_=PB[0][:, 0:nk * 128]), sDT, 16)
                inc(sy.e.dma_start_transpose(out=PT[1][:, 0:nk, :],
                                             in_=PB[1][:, 0:nk * 128]), sDT, 16)
                pb_free = sDT.v
                guard("B0")
                pe.wait(sDT, sDT.v)
                for kt in range(nk):
                    base = kt * 128
                    nc.tensor.matmul(bB(0, 128, 0, 64), VT[:, base:base + 64],
                                     PT[0][:, kt, :], start=(kt == 0), stop=(kt == nk - 1))
                    mm = nc.tensor.matmul(bB(0, 128, 64, 128), VT[:, base + 64:base + 128],
                                          PT[1][:, kt, :], start=(kt == 0),
                                          stop=(kt == nk - 1), tile_position=(0, 64))
                inc(mm, sPE, 1)
                pt_free = sPE.v
                guard("B1")
                pe.wait(sVE, recl_at)
                inc(nc.tensor.transpose(PS_B[0:2, 512:640], recl2[:, 0:2], c_ident[:, :]),
                    sPE, 1)
                ve.wait(sPE, sPE.v)
                inc(ve.e.tensor_copy(reclT[:, :], PS_B[0:2, 512:640]), sVE, 1)
                release("B1", sVE)
                guard("B2")
                pe.wait(sVE, sVE.v)
                inc(nc.tensor.matmul(bB(2, 128), c_e2tf[:, :], reclT[:, :],
                                     start=True, stop=True), sPE, 1)
                ve.wait(sPE, sPE.v)
                inc(ve.e.tensor_copy(mtile[:, 0:128], bB(2, 128)), sVE, 1)
                release("B2", sVE)
                ve.wait(sVE, sVE.v)
                inc(ve.e.tensor_tensor(X1BF[:, g, qt * 128:(qt + 1) * 128],
                                       bB(0, 128), mtile[:, 0:128], op=OP.mult), sVE, 1)
                release("B0", sVE)

    XRES = E(nc.sbuf_tensor("XRES", [128, NCH, T], F32))
    wo_stack = ExitStack()
    XTF = wo_stack.enter_context(nc.sbuf_tensor("XTF", [128, 1024], F32))
    wsl = wo_stack.enter_context(nc.sbuf_tensor("wsl", [128, NCH, 256], BF16))
    fence(ve, pe, sy, gp, ac)

    for t in range(NQT):
        sy.wait(sVE, sVE.v)
        sy.wait(sDL, sDL.v)
        inc(sy.e.dma_start(out=XTF[:, :], in_=xq[t * 128:(t + 1) * 128, :]), sDL, 16)
        pe.wait(sDL, sDL.v)
        for j in range(NCH):
            key = f"B{j % 2}"
            guard(key)
            inc(nc.tensor.transpose(bB(j % 2, 128), XTF[:, j * 128:(j + 1) * 128],
                                    c_ident[:, :]), sPE, 1)
            ve.wait(sPE, sPE.v)
            inc(ve.e.tensor_copy(XRES[:, j, t * 128:(t + 1) * 128], bB(j % 2, 128)), sVE, 1)
            release(key, sVE)
    gbar()
    for g4 in range(4):
        gp.wait(sPE, sPE.v)
        wload(gp, sGP, wsl, l0_woT, 8, g4 * 256, 256)
        wo_at = sGP.v
        for jj in range(2):
            j = g4 * 2 + jj
            for qb in range(4):
                key = f"A{qb % 4}"
                guard(key)
                pe.wait(sGP, wo_at)
                pe.wait(sVE, sVE.v)
                for dch in range(NCH):
                    mm = nc.tensor.matmul(bA(qb % 4), wsl[:, dch, jj * 128:(jj + 1) * 128],
                                          X1BF[:, dch, qb * 512:(qb + 1) * 512],
                                          start=(dch == 0), stop=(dch == NCH - 1))
                inc(mm, sPE, 1)
                dst = XRES[:, j, qb * 512:(qb + 1) * 512]
                ve.wait(sPE, sPE.v)
                inc(ve.e.scalar_tensor_tensor(dst, bA(qb % 4), c_bo[:, j:j + 1], dst,
                                              op0=OP.add, op1=OP.add), sVE, 1)
                release(key, sVE)
    postln(XRES, 0, T)
    wo_stack.close()
    if debug:
        sy.wait(sVE, sVE.v)
        inc(sy.e.dma_start(out=dbgo["d_x1"], in_=XRES[:, :, :]), sST, 16)

    with ExitStack() as f0s:
        L = f0s.enter_context
        u1b = L(nc.sbuf_tensor("u1b", [128, NCH, 256], BF16))
        v1b = L(nc.sbuf_tensor("v1b", [128, 2, 4096], BF16))
        u2b = L(nc.sbuf_tensor("u2b", [128, 32, 256], BF16))
        v2b = L(nc.sbuf_tensor("v2b", [128, 2, 1024], BF16))
        pqb = L(nc.sbuf_tensor("pqb", [128, 2, 512], BF16))
        qqb = L(nc.sbuf_tensor("qqb", [128, 2, 512], BF16))
        fence(gp)
        wload(gp, sGP, u1b, l0_u1, 8, 0, 256)
        wload(gp, sGP, v1b, l0_v1, 2, 0, 4096)
        wload(gp, sGP, u2b, l0_u2, 32, 0, 256)
        wload(gp, sGP, v2b, l0_v2, 2, 0, 1024)
        ffw_at = sGP.v
        gbar()
        for tb in range(4):
            tc = tb * 512
            for rc in range(2):
                key = f"A{rc}"
                guard(key)
                pe.wait(sGP, ffw_at)
                for j in range(NCH):
                    mm = nc.tensor.matmul(bA(rc), u1b[:, j, rc * 128:(rc + 1) * 128],
                                          X1BF[:, j, tc:tc + 512],
                                          start=(j == 0), stop=(j == NCH - 1))
                inc(mm, sPE, 1)
                ve.wait(sPE, sPE.v)
                inc(ve.e.tensor_copy(pqb[:, rc, :], bA(rc)), sVE, 1)
                release(key, sVE)
            guard("B0")
            guard("B1")
            p_at = sVE.v
            for cc in range(32):
                key = f"A{2 + cc % 2}"
                guard(key)
                pe.wait(sVE, p_at)
                for rc in range(2):
                    mm = nc.tensor.matmul(bA(2 + cc % 2), v1b[:, rc, cc * 128:(cc + 1) * 128],
                                          pqb[:, rc, :], start=(rc == 0), stop=(rc == 1))
                inc(mm, sPE, 1)
                hbuf = stg if cc % 2 == 0 else stg2
                hkey = f"H{cc % 2}"
                ac.wait(sPE, sPE.v)
                if hkey in bg:
                    ac.wait(bg[hkey][0], bg[hkey][1])
                inc(ac.e.activation(hbuf[:, :], bA(2 + cc % 2), AF.Gelu_apprx_tanh,
                                    bias=c_b1[:, cc:cc + 1]), sAC, 1)
                release(key, sAC)
                pe.wait(sAC, sAC.v)
                for rc in range(2):
                    mm = nc.tensor.matmul(bB(rc), u2b[:, cc, rc * 128:(rc + 1) * 128],
                                          hbuf[:, :], start=(cc == 0), stop=(cc == 31))
                inc(mm, sPE, 1)
                bg[hkey] = (sPE, sPE.v)
            ve.wait(sPE, sPE.v)
            inc(ve.e.tensor_copy(qqb[:, 0, :], bB(0)), sVE, 1)
            inc(ve.e.tensor_copy(qqb[:, 1, :], bB(1)), sVE, 1)
            release("B0", sVE)
            release("B1", sVE)
            q_at = sVE.v
            for j in range(NCH):
                key = f"A{j % 2}"
                guard(key)
                pe.wait(sVE, q_at)
                for rc in range(2):
                    mm = nc.tensor.matmul(bA(j % 2), v2b[:, rc, j * 128:(j + 1) * 128],
                                          qqb[:, rc, :], start=(rc == 0), stop=(rc == 1))
                inc(mm, sPE, 1)
                dst = XRES[:, j, tc:tc + 512]
                ve.wait(sPE, sPE.v)
                inc(ve.e.scalar_tensor_tensor(dst, bA(j % 2), c_b2[:, j:j + 1], dst,
                                              op0=OP.add, op1=OP.add), sVE, 1)
                release(key, sVE)
        gbar()
        postln(XRES, 0, T)
        if debug:
            sy.wait(sVE, sVE.v)
            inc(sy.e.dma_start(out=dbgo["d_x2"], in_=XRES[:, :, :]), sST, 16)

    with ExitStack() as l1s:
        L = l1s.enter_context
        wq1 = L(nc.sbuf_tensor("wq1", [128, NCH, 512], BF16))
        wk1 = L(nc.sbuf_tensor("wk1", [128, NCH, 128], BF16))
        wv1 = L(nc.sbuf_tensor("wv1", [128, NCH, 64], BF16))
        K2 = L(nc.sbuf_tensor("K2", [128, S], BF16))
        VT1 = L(nc.sbuf_tensor("VT1", [128, NKT * VW], BF16))
        QG = L(nc.sbuf_tensor("QG", [128, NCH, 256], BF16))
        PT0 = L(nc.sbuf_tensor("PX0", [128, NKT * 256], BF16))
        PT1 = L(nc.sbuf_tensor("PX1", [128, NKT * 256], BF16))
        OB1 = L(nc.sbuf_tensor("OB1", [128, NCH, 256], BF16))
        fence(gp)
        wload(gp, sGP, wk1, l1_wkT2, 8, 0, 128)
        wload(gp, sGP, wv1, l1_wvT, 8, 0, 64)
        w1_at = sGP.v
        for tb in range(4):
            key = f"A{tb % 2}"
            guard(key)
            pe.wait(sGP, w1_at)
            for j in range(NCH):
                mm = nc.tensor.matmul(bA(tb % 2), wk1[:, j, :],
                                      X1BF[:, j, tb * 512:(tb + 1) * 512],
                                      start=(j == 0), stop=(j == NCH - 1))
            inc(mm, sPE, 1)
            ve.wait(sPE, sPE.v)
            inc(ve.e.tensor_copy(K2[:, tb * 512:(tb + 1) * 512], bA(tb % 2)), sVE, 1)
            release(key, sVE)
        for kt in range(16):
            key = f"B{kt % 2}"
            guard(key)
            for j in range(NCH):
                mm = nc.tensor.matmul(bB(kt % 2, 64), X1BF[:, j, kt * 128:(kt + 1) * 128],
                                      wv1[:, j, :], start=(j == 0), stop=(j == NCH - 1))
            inc(mm, sPE, 1)
            ve.wait(sPE, sPE.v)
            inc(ve.e.tensor_copy(VT1[:, kt * VW:kt * VW + 64], bB(kt % 2, 64)), sVE, 1)
            release(key, sVE)
            inc(ve.e.memset(VT1[:, kt * VW + 64:kt * VW + 65], 1.0), sVE, 1)
        kv_at = sVE.v
        gp.wait(sVE, kv_at)
        inc(gp.e.dma_start(out=cc_kin.ap(), in_=K2[0:64, 0:T]), sGP, 16)
        gp.wait(sGP, sGP.v)
        inc(gp.e.collective_compute(
            "AllGather", OP.bypass, replica_groups=pairs,
            ins=[cc_kin.ap().opt()], outs=[cc_kout.ap().opt()]), sCC, 1)
        gp.wait(sCC, sCC.v)
        inc(gp.e.dma_start(out=cc_vin.ap(), in_=VT1[:, 0:16 * VW]), sGP, 16)
        gp.wait(sGP, sGP.v)
        inc(gp.e.collective_compute(
            "AllGather", OP.bypass, replica_groups=pairs,
            ins=[cc_vin.ap().opt()], outs=[cc_vout.ap().opt()]), sCC, 1)
        gp.wait(sCC, sCC.v)
        inc(gp.e.dma_start(out=K2[0:64, 0:T], in_=cc_kout.ap()[0:64, :]), sGP, 16)
        gp.wait(sGP, sGP.v)
        inc(gp.e.dma_start(out=K2[0:64, T:S], in_=cc_kout.ap()[64:128, :]), sGP, 16)
        gp.wait(sGP, sGP.v)
        inc(gp.e.dma_start(out=VT1[:, 0:16 * VW], in_=cc_vout.ap()[0:128, :]), sGP, 16)
        gp.wait(sGP, sGP.v)
        inc(gp.e.dma_start(out=VT1[:, 16 * VW:32 * VW], in_=cc_vout.ap()[128:256, :]), sGP, 16)
        gp.wait(sGP, sGP.v)
        inc(gp.e.dma_start(out=K2[64:128, 0:S], in_=K2[0:64, 0:S]), sGP, 16)
        kg_at = sGP.v
        for cb2 in range(8):
            sl = K2[0:64, cb2 * 512:(cb2 + 1) * 512]
            ve.wait(sGP, kg_at)
            ve.wait(sPE, sPE.v)
            inc(ve.e.tensor_tensor(stg[0:64, :], sl, sl, op=OP.mult), sVE, 1)
            guard("B3")
            pe.wait(sVE, sVE.v)
            inc(nc.tensor.matmul(bB(3, 512, 0, 1), c_o64[:, :], stg[0:64, :],
                                 start=True, stop=True), sPE, 1)
            ve.wait(sPE, sPE.v)
            if cb2 == 0:
                inc(ve.e.tensor_reduce(kmax2[:, :], bB(3, 512, 0, 1), axis=AX.X, op=OP.max),
                    sVE, 1)
            else:
                inc(ve.e.tensor_reduce(kmtmp[:, :], bB(3, 512, 0, 1), axis=AX.X, op=OP.max),
                    sVE, 1)
                ve.wait(sVE, sVE.v)
                inc(ve.e.tensor_tensor(kmax2[:, :], kmax2[:, :], kmtmp[:, :], op=OP.max),
                    sVE, 1)
            release("B3", sVE)
        kmax_at = sVE.v
        for qg in range(8):
            qc = qg * 256
            for j in range(NCH):
                if j % 4 == 0:
                    hh = j // 4
                    gp.wait(sPE, sPE.v)
                    wload(gp, sGP, wq1, l1_wqT, 8, hh * 512, 512)
                    wq_at = sGP.v
                key = f"A{j % 2}"
                guard(key)
                pe.wait(sGP, wq_at)
                for dch in range(NCH):
                    mm = nc.tensor.matmul(bA(j % 2, 256),
                                          wq1[:, dch, (j % 4) * 128:(j % 4 + 1) * 128],
                                          X1BF[:, dch, qc:qc + 256],
                                          start=(dch == 0), stop=(dch == NCH - 1))
                inc(mm, sPE, 1)
                ve.wait(sPE, sPE.v)
                inc(ve.e.tensor_copy(QG[:, j, :], bA(j % 2, 256)), sVE, 1)
                release(key, sVE)
                qsl = QG[:, j, :]
                ve.wait(sVE, sVE.v)
                ve.wait(sPE, sPE.v)
                inc(ve.e.tensor_tensor(stg[:, 0:256], qsl, qsl, op=OP.mult), sVE, 1)
                guard("B3")
                pe.wait(sVE, sVE.v)
                inc(nc.tensor.matmul(bB(3, 256, 0, 1), c_oc128[:, :], stg[:, 0:256],
                                     start=True, stop=True), sPE, 1)
                ve.wait(sPE, sPE.v)
                if j == 0:
                    inc(ve.e.tensor_reduce(qmaxs[:, :], bB(3, 256, 0, 1), axis=AX.X,
                                           op=OP.max), sVE, 1)
                else:
                    inc(ve.e.tensor_reduce(qmtmp[:, :], bB(3, 256, 0, 1), axis=AX.X,
                                           op=OP.max), sVE, 1)
                    ve.wait(sVE, sVE.v)
                    inc(ve.e.tensor_tensor(qmaxs[:, :], qmaxs[:, :], qmtmp[:, :],
                                           op=OP.max), sVE, 1)
                release("B3", sVE)
            ve.wait(sVE, sVE.v)
            ve.wait(sVE, kmax_at)
            inc(ve.e.tensor_tensor(bsc[:, :], qmaxs[:, :], kmax2[:, :], op=OP.mult), sVE, 1)
            ac.wait(sVE, sVE.v)
            inc(ac.e.activation(bsc[:, :], bsc[:, :], AF.Ln), sAC, 1)
            ac.wait(sAC, sAC.v)
            inc(ac.e.activation(bsc[:, :], bsc[:, :], AF.Exp, scale=np.float32(0.5),
                                bias=np.float32(-2.0794415416798357)), sAC, 1)
            ve.wait(sAC, sAC.v)
            ve.wait(sPE, sPE.v)
            inc(ve.e.tensor_scalar(rowb[:, :], c_ones1[:, :], bsc[0:1, :], None,
                                   op0=OP.mult), sVE, 1)
            guard("B3")
            pe.wait(sVE, sVE.v)
            inc(nc.tensor.transpose(PS_B[0:128, 1536:1537], rowb[:, :], c_ones1[0:1, 0:1]),
                sPE, 1)
            ve.wait(sPE, sPE.v)
            inc(ve.e.scalar_tensor_tensor(negB[:, :], c_zero[:, :], c_zero[:, :],
                                          PS_B[0:128, 1536:1537],
                                          op0=OP.add, op1=OP.subtract), sVE, 1)
            release("B3", sVE)
            negb_at = sVE.v
            for pr in range(NCH):
                q_lo = QG[0:64, pr, :]
                q_hi = QG[64:128, pr, :]
                for ktg in range(8):
                    for bi in range(4):
                        guard(f"A{bi}")
                    pe.wait(sGP, kg_at)
                    pe.wait(sVE, negb_at)
                    for kk in range(4):
                        kt = ktg * 4 + kk
                        nc.tensor.matmul(PS_A[:, kk * 256:kk * 256 + 256],
                                         K2[0:64, kt * 128:(kt + 1) * 128], q_lo,
                                         start=True, stop=True)
                        mm = nc.tensor.matmul(PS_A[:, 1024 + kk * 256:1024 + kk * 256 + 256],
                                              K2[64:128, kt * 128:(kt + 1) * 128], q_hi,
                                              start=True, stop=True, tile_position=(64, 0))
                    inc(mm, sPE, 1)
                    ac.wait(sPE, sPE.v)
                    ac.wait(sVE, negb_at)
                    inc(ac.e.activation(PT0[:, ktg * 1024:(ktg + 1) * 1024], PS_A[:, 0:1024],
                                        AF.Exp, bias=negB[:, :], scale=np.float32(0.125)),
                        sAC, 1)
                    inc(ac.e.activation(PT1[:, ktg * 1024:(ktg + 1) * 1024],
                                        PS_A[:, 1024:2048], AF.Exp, bias=negB[:, :],
                                        scale=np.float32(0.125)), sAC, 1)
                    for bi in range(4):
                        release(f"A{bi}", sAC)
                for hl, PTx in ((0, PT0), (1, PT1)):
                    key = f"B{hl}"
                    guard(key)
                    pe.wait(sAC, sAC.v)
                    for kt in range(NKT):
                        mm = nc.tensor.matmul(bB(hl, 256, 0, 65), VT1[:, kt * VW:kt * VW + 65],
                                              PTx[:, kt * 256:(kt + 1) * 256],
                                              start=(kt == 0), stop=(kt == NKT - 1))
                    inc(mm, sPE, 1)
                    ve.wait(sPE, sPE.v)
                    inc(ve.e.reciprocal(mtile[0:1, 0:256],
                                        PS_B[64:65, hl * 512:hl * 512 + 256]), sVE, 1)
                    guard("B2")
                    pe.wait(sVE, sVE.v)
                    inc(nc.tensor.matmul(bB(2, 256), c_ones1[:, :], mtile[0:1, 0:256],
                                         start=True, stop=True), sPE, 1)
                    ve.wait(sPE, sPE.v)
                    inc(ve.e.tensor_copy(mtile[:, 0:256], bB(2, 256)), sVE, 1)
                    release("B2", sVE)
                    ve.wait(sVE, sVE.v)
                    inc(ve.e.tensor_tensor(OB1[hl * 64:(hl + 1) * 64, pr, :],
                                           PS_B[0:64, hl * 512:hl * 512 + 256],
                                           mtile[0:64, 0:256], op=OP.mult), sVE, 1)
                    release(key, sVE)
            gbar()
            fence(ve, ac, pe)
            for j in range(NCH):
                dst = XRES[:, j, qc:qc + 256]
                ve.wait(sVE, sVE.v)
                inc(ve.e.scalar_tensor_tensor(dst, OB1[:, j, :], c_zero[:, :], dst,
                                              op0=OP.add, op1=OP.add), sVE, 1)
            postln(XRES, qc, 256)
        if debug:
            sy.wait(sVE, sVE.v)
            inc(sy.e.dma_start(out=dbgo["d_x3"], in_=XRES[:, :, :]), sST, 16)

    with ExitStack() as f1s:
        L = f1s.enter_context
        T1b = L(nc.sbuf_tensor("T1b", [128, NCH, 256], BF16))
        T2b = L(nc.sbuf_tensor("T2b", [128, NCH, 256], BF16))
        wrb = L(nc.sbuf_tensor("wrb", [128, NCH, 512], BF16))
        wib = L(nc.sbuf_tensor("wib", [128, NCH, 512], BF16))
        csl = L(nc.sbuf_tensor("csl", [128, NCH, 256], BF16))
        ssl = L(nc.sbuf_tensor("ssl", [128, NCH, 256], BF16))
        snl = L(nc.sbuf_tensor("snl", [128, NCH, 256], BF16))
        cdo = L(nc.sbuf_tensor("cdo", [128, NCH, 128], BF16))
        sdo = L(nc.sbuf_tensor("sdo", [128, NCH, 128], BF16))
        WEF = L(nc.sbuf_tensor("WEF", [128, NCH, 256], BF16))
        fence(gp, sy)
        for cs in range(4):
            sy.wait(sPE, sPE.v)
            for dst, src_ in ((csl, cCb), (ssl, cSb), (snl, cSnb)):
                wload(sy, sDL, dst, src_, 8, cs * 256, 256)
            cs_at = sDL.v
            for hh in range(2):
                gp.wait(sPE, sPE.v)
                wload(gp, sGP, wrb, wrT_i, 8, hh * 512, 512)
                wload(gp, sGP, wib, wiT_i, 8, hh * 512, 512)
                wr_at = sGP.v
                for fq in range(4):
                    fo = hh * 4 + fq
                    key = f"A{fo % 4}"
                    guard(key)
                    pe.wait(sGP, wr_at)
                    pe.wait(sDL, cs_at)
                    for j in range(NCH):
                        nc.tensor.matmul(bA(fo % 4, 256),
                                         wrb[:, j, fq * 128:(fq + 1) * 128],
                                         csl[:, j, :], start=(j == 0), stop=False)
                    for j in range(NCH):
                        mm = nc.tensor.matmul(bA(fo % 4, 256),
                                              wib[:, j, fq * 128:(fq + 1) * 128],
                                              snl[:, j, :], start=False,
                                              stop=(j == NCH - 1))
                    inc(mm, sPE, 1)
                    ve.wait(sPE, sPE.v)
                    inc(ve.e.tensor_copy(T1b[:, fo, :], bA(fo % 4, 256)), sVE, 1)
                    release(key, sVE)
                    key = f"B{fo % 4}"
                    guard(key)
                    for j in range(NCH):
                        nc.tensor.matmul(bB(fo % 4, 256),
                                         wib[:, j, fq * 128:(fq + 1) * 128],
                                         csl[:, j, :], start=(j == 0), stop=False)
                    for j in range(NCH):
                        mm = nc.tensor.matmul(bB(fo % 4, 256),
                                              wrb[:, j, fq * 128:(fq + 1) * 128],
                                              ssl[:, j, :], start=False,
                                              stop=(j == NCH - 1))
                    inc(mm, sPE, 1)
                    ve.wait(sPE, sPE.v)
                    inc(ve.e.tensor_copy(T2b[:, fo, :], bB(fo % 4, 256)), sVE, 1)
                    release(key, sVE)
            t12_at = sVE.v
            for do in range(NCH):
                sy.wait(sPE, sPE.v)
                wload(sy, sDL, cdo, cCb, 8, do * 128, 128)
                wload(sy, sDL, sdo, cSb, 8, do * 128, 128)
                do_at = sDL.v
                key = f"A{do % 4}"
                guard(key)
                pe.wait(sDL, do_at)
                pe.wait(sVE, t12_at)
                for f_ in range(NCH):
                    nc.tensor.matmul(bA(do % 4, 256), cdo[:, f_, :], T1b[:, f_, :],
                                     start=(f_ == 0), stop=False)
                for f_ in range(NCH):
                    mm = nc.tensor.matmul(bA(do % 4, 256), sdo[:, f_, :], T2b[:, f_, :],
                                          start=False, stop=(f_ == NCH - 1))
                inc(mm, sPE, 1)
                ve.wait(sPE, sPE.v)
                inc(ve.e.tensor_scalar(WEF[:, do, :], bA(do % 4, 256),
                                       c_iD128[:, :], None, op0=OP.mult), sVE, 1)
                release(key, sVE)
            wef_at = sVE.v
            for jj in range(2):
                j = cs * 2 + jj
                for qb in range(4):
                    key = f"B{qb % 2}"
                    guard(key)
                    pe.wait(sVE, wef_at)
                    for dch in range(NCH):
                        mm = nc.tensor.matmul(
                            bB(qb % 2), WEF[:, dch, jj * 128:(jj + 1) * 128],
                            X1BF[:, dch, qb * 512:(qb + 1) * 512],
                            start=(dch == 0), stop=(dch == NCH - 1))
                    inc(mm, sPE, 1)
                    dst = XRES[:, j, qb * 512:(qb + 1) * 512]
                    ve.wait(sPE, sPE.v)
                    inc(ve.e.scalar_tensor_tensor(dst, bB(qb % 2), c_l1b[:, j:j + 1], dst,
                                                  op0=OP.add, op1=OP.add), sVE, 1)
                    release(key, sVE)
        postln(XRES, 0, T)

    XOT = [E(nc.sbuf_tensor(f"XOT{i}", [128, 1024], F32)) for i in range(2)]
    xot_free = [0, 0]
    for t in range(NQT):
        b = t % 2
        for j in range(NCH):
            key = f"B{2 + j % 2}"
            guard(key)
            pe.wait(sVE, sVE.v)
            inc(nc.tensor.transpose(bB(2 + j % 2, 128), XRES[:, j, t * 128:(t + 1) * 128],
                                    c_ident[:, :]), sPE, 1)
            ve.wait(sPE, sPE.v)
            ve.wait(sST, xot_free[b])
            inc(ve.e.tensor_copy(XOT[b][:, j * 128:(j + 1) * 128], bB(2 + j % 2, 128)), sVE, 1)
            release(key, sVE)
        sy.wait(sVE, sVE.v)
        sy.wait(sST, sST.v)
        inc(sy.e.dma_start(out=xout[t * 128:(t + 1) * 128, :], in_=XOT[b][:, :]), sST, 16)
        xot_free[b] = sST.v
    gp.wait(sST, sST.v)
    gp.wait(sDT, sDT.v)
    gp.wait(sDL, sDL.v)
    gp.wait(sGP, sGP.v)

    est.close()
    return nc


def _mk_consts():
    d = np.arange(D, dtype=np.int64)
    prod = np.outer(d, d) % D
    ang = 2.0 * np.pi * prod.astype(np.float64) / D
    return np.cos(ang), np.sin(ang)


def kernel(
    L0_attn_x, L0_attn_mask, L0_attn_Wq, L0_attn_Wk, L0_attn_Wv, L0_attn_gq, L0_attn_bq,
    L0_attn_gk, L0_attn_bk, L0_attn_Wo, L0_attn_bo, L0_ffn_U1, L0_ffn_V1, L0_ffn_b1,
    L0_ffn_U2, L0_ffn_V2, L0_ffn_b2, L1_attn_Wq, L1_attn_Wk, L1_attn_Wv,
    L1_ffn_w_freq_real, L1_ffn_w_freq_imag, L1_ffn_bias,
):
    f32 = np.float32
    bf16 = ml_dtypes.bfloat16
    x = np.ascontiguousarray(np.asarray(L0_attn_x, f32))

    def t2(a):
        return np.ascontiguousarray(np.asarray(a, f32).T)

    def col(a, n=128):
        a = np.asarray(a, f32).reshape(-1)
        return np.ascontiguousarray(a.reshape(-1, n).T)

    def rep2(a):
        a = np.asarray(a, f32).reshape(-1)
        return np.ascontiguousarray(np.concatenate([a, a])[:, None])

    C, Sm = _mk_consts()
    e2 = np.zeros((128, 2), f32)
    e2[0:64, 0] = 1.0
    e2[64:128, 1] = 1.0
    shared = {
        "l0_wqT": t2(L0_attn_Wq), "l0_wkT": t2(L0_attn_Wk), "l0_wvT": t2(L0_attn_Wv),
        "l0_woT": t2(L0_attn_Wo), "l0_bo": col(L0_attn_bo),
        "l0_gq": rep2(L0_attn_gq), "l0_bq": rep2(L0_attn_bq),
        "l0_gk": rep2(L0_attn_gk), "l0_bk": rep2(L0_attn_bk),
        "l0_u1": np.ascontiguousarray(np.asarray(L0_ffn_U1, f32)),
        "l0_v1": np.ascontiguousarray(np.asarray(L0_ffn_V1, f32)),
        "l0_b1": col(L0_ffn_b1),
        "l0_u2": np.ascontiguousarray(np.asarray(L0_ffn_U2, f32)),
        "l0_v2": np.ascontiguousarray(np.asarray(L0_ffn_V2, f32)),
        "l0_b2": col(L0_ffn_b2),
        "l1_wqT": t2(L1_attn_Wq),
        "l1_wkT2": np.ascontiguousarray(
            np.concatenate([np.asarray(L1_attn_Wk, f32).T] * 2, axis=1)),
        "l1_wvT": t2(L1_attn_Wv),
        "l1_bias": col(L1_ffn_bias),
        "wrT": t2(L1_ffn_w_freq_real), "wiT": t2(L1_ffn_w_freq_imag),
        "cCb": C.astype(bf16), "cSb": Sm.astype(bf16), "cSnb": (-Sm).astype(bf16),
        "e2bf": e2.astype(bf16),
        "e2tf": np.ascontiguousarray(e2.T),
        "ones1": np.ones((1, 128), f32),
        "neg1": -np.ones((1, 128), f32),
        "onesc": np.ones((128, 1), f32),
        "identf": np.eye(128, dtype=f32),
        "o64": np.ones((64, 1), bf16),
        "oc128": np.ones((128, 1), bf16),
    }
    trilm = np.where(np.tril(np.ones((128, 128), bool)), 0.0, -1e30).astype(f32)
    fullm = np.full((128, 128), -1e30, f32)
    zerom = np.zeros((128, 128), f32)

    nc = build()
    in_maps = []
    for c in range(8):
        n, f = c // 2, c % 2
        tiles = [2 * i + f for i in range(16)]
        xq_rows = np.ascontiguousarray(
            np.concatenate([x[n, t * 128:(t + 1) * 128, :] for t in tiles], axis=0))
        im = dict(shared)
        im["xq"] = xq_rows
        im["xb"] = np.ascontiguousarray(x[n])
        im["mask_a"] = trilm if f == 0 else zerom
        im["mask_b"] = fullm if f == 0 else trilm
        in_maps.append(im)

    res = run_bass_kernel_spmd(nc, in_maps, core_ids=list(range(8)))
    out = np.empty((4, 4096, 1024), f32)
    for c in range(8):
        n, f = c // 2, c % 2
        r = np.asarray(res.results[c]["xout"], f32)
        for i in range(16):
            t = 2 * i + f
            out[n, t * 128:(t + 1) * 128, :] = r[i * 128:(i + 1) * 128, :]
    return out


# version 08
# speedup vs baseline: 1.0996x; 1.0917x over previous
import sys

sys.path.insert(0, "/opt/trn_rl_repo")
from contextlib import ExitStack

import numpy as np
import ml_dtypes
import concourse.bass as bass
import concourse.mybir as mybir
from concourse.bass_utils import run_bass_kernel_spmd
from concourse.alu_op_type import AluOpType as OP

F32 = mybir.dt.float32
BF16 = mybir.dt.bfloat16
AF = mybir.ActivationFunctionType
AX = mybir.AxisListType

D = 1024
S = 4096
NCH = 8
T = 2048
NQT = 16
NKT = 32
VW = 72


class Sem:
    __slots__ = ("h", "v")

    def __init__(self, h):
        self.h = h
        self.v = 0


class Wr:
    def __init__(self, e):
        self.e = e
        self.seen = {}

    def wait(self, sem, val=None):
        if val is None:
            val = sem.v
        if val <= 0 or self.seen.get(id(sem), 0) >= val:
            return
        self.e.wait_ge(sem.h, val)
        self.seen[id(sem)] = val


def inc(instr, sem, n=1):
    instr.then_inc(sem.h, n)
    sem.v += n


DBGQG = 0


def build(debug=0):
    nc = bass.Bass("TRN2", target_bir_lowering=False)

    def din(name, shape, dt=F32):
        return nc.dram_tensor(name, shape, dt, kind="ExternalInput").ap()

    xq = din("xq", [T, D])
    xb = din("xb", [S, D])
    mask_a = din("mask_a", [128, 128])
    mask_b = din("mask_b", [128, 128])
    l0_wqT = din("l0_wqT", [D, D])
    l0_wkT = din("l0_wkT", [D, D])
    l0_wvT = din("l0_wvT", [D, D])
    l0_woT = din("l0_woT", [D, D])
    l0_bo = din("l0_bo", [128, 8])
    l0_gq = din("l0_gq", [128, 1])
    l0_bq = din("l0_bq", [128, 1])
    l0_gk = din("l0_gk", [128, 1])
    l0_bk = din("l0_bk", [128, 1])
    l0_u1 = din("l0_u1", [D, 256])
    l0_v1 = din("l0_v1", [256, 4096])
    l0_b1 = din("l0_b1", [128, 32])
    l0_u2 = din("l0_u2", [4096, 256])
    l0_v2 = din("l0_v2", [256, D])
    l0_b2 = din("l0_b2", [128, 8])
    l1_wqT = din("l1_wqT", [D, D])
    l1_wkT2 = din("l1_wkT2", [D, 128])
    l1_wvT = din("l1_wvT", [D, 64])
    l1_bias = din("l1_bias", [128, 8])
    wrT_i = din("wrT", [D, D])
    wiT_i = din("wiT", [D, D])
    cCb = din("cCb", [D, D], BF16)
    cSb = din("cSb", [D, D], BF16)
    cSnb = din("cSnb", [D, D], BF16)
    e2bf_i = din("e2bf", [128, 2], BF16)
    e2tf_i = din("e2tf", [2, 128])
    ones1_i = din("ones1", [1, 128])
    neg1_i = din("neg1", [1, 128])
    onesc_i = din("onesc", [128, 1])
    identf_i = din("identf", [128, 128])
    o64_i = din("o64", [64, 1], BF16)
    oc128_i = din("oc128", [128, 1], BF16)

    xout = nc.dram_tensor("xout", [T, D], F32, kind="ExternalOutput").ap()
    dbgo = {}
    if debug:
        for nm in ("d_x1", "d_x2", "d_x3"):
            dbgo[nm] = nc.dram_tensor(nm, [128, NCH, T], F32, kind="ExternalOutput").ap()
        dbgo["d_k2"] = nc.dram_tensor("d_k2", [128, S], F32, kind="ExternalOutput").ap()
        dbgo["d_vt1"] = nc.dram_tensor("d_vt1", [128, NKT * VW], F32,
                                       kind="ExternalOutput").ap()
        dbgo["d_negb"] = nc.dram_tensor("d_negb", [128, 1], F32, kind="ExternalOutput").ap()
        dbgo["d_qg"] = nc.dram_tensor("d_qg", [128, NCH, 256], F32,
                                      kind="ExternalOutput").ap()
        dbgo["d_pt0"] = nc.dram_tensor("d_pt0", [128, 1024], F32, kind="ExternalOutput").ap()
        dbgo["d_ob1"] = nc.dram_tensor("d_ob1", [128, NCH, 256], F32,
                                       kind="ExternalOutput").ap()
        dbgo["d_scal"] = nc.dram_tensor("d_scal", [1, 16], F32, kind="ExternalOutput").ap()
        dbgo["d_xr"] = nc.dram_tensor("d_xr", [128, NCH, 512], F32,
                                      kind="ExternalOutput").ap()
        dbgo["d_xr0"] = nc.dram_tensor("d_xr0", [128, NCH, 256], F32,
                                       kind="ExternalOutput").ap()
        dbgo["d_stm"] = nc.dram_tensor("d_stm", [2, 2048], F32, kind="ExternalOutput").ap()
        dbgo["d_sts"] = nc.dram_tensor("d_sts", [2, 2048], F32, kind="ExternalOutput").ap()
        dbgo["d_str"] = nc.dram_tensor("d_str", [2, 2048], F32, kind="ExternalOutput").ap()

    cc_kin = nc.dram_tensor("cc_kin", [64, T], BF16)
    cc_kout = nc.dram_tensor("cc_kout", [128, T], BF16)
    cc_vin = nc.dram_tensor("cc_vin", [128, NQT * VW], BF16)
    cc_vout = nc.dram_tensor("cc_vout", [256, NQT * VW], BF16)
    pairs = [[0, 1], [2, 3], [4, 5], [6, 7]]

    est = ExitStack()
    E = est.enter_context

    sDL = Sem(E(nc.semaphore("sDL")))
    sDT = Sem(E(nc.semaphore("sDT")))
    sST = Sem(E(nc.semaphore("sST")))
    sPE = Sem(E(nc.semaphore("sPE")))
    sVE = Sem(E(nc.semaphore("sVE")))
    sAC = Sem(E(nc.semaphore("sAC")))
    sGP = Sem(E(nc.semaphore("sGP")))
    sCC = Sem(E(nc.semaphore("sCC")))
    sX0 = Sem(E(nc.semaphore("sX0")))
    sX1 = Sem(E(nc.semaphore("sX1")))
    sX = [sX0, sX1]
    sGB = Sem(E(nc.semaphore("sGB")))
    ALLSEMS = [sDL, sDT, sST, sPE, sVE, sAC, sGP, sCC]

    pe = Wr(nc.tensor)
    ve = Wr(nc.vector)
    ac = Wr(nc.scalar)
    gp = Wr(nc.gpsimd)
    sy = Wr(nc.sync)

    def fence(*engs):
        for e_ in engs:
            for sm_ in ALLSEMS:
                e_.wait(sm_, sm_.v)

    def gbar():
        gp.wait(sVE, sVE.v)
        gp.wait(sAC, sAC.v)
        inc(gp.e.nop(), sGB, 1)
        ve.wait(sGB, sGB.v)
        ac.wait(sGB, sGB.v)
        pe.wait(sGB, sGB.v)

    def wload(engw, sem, dst3, src2d, nj, c0, w):
        engw.wait(sem, sem.v)
        for j_ in range(nj):
            inc(engw.e.dma_start(out=dst3[:, j_, :],
                                 in_=src2d[j_ * 128:(j_ + 1) * 128, c0:c0 + w]), sem, 16)

    c_e2bf = E(nc.sbuf_tensor("c_e2bf", [128, 2], BF16))
    c_e2tf = E(nc.sbuf_tensor("c_e2tf", [2, 128], F32))
    c_ones1 = E(nc.sbuf_tensor("c_ones1", [1, 128], F32))
    c_neg1 = E(nc.sbuf_tensor("c_neg1", [1, 128], F32))
    c_onesc = E(nc.sbuf_tensor("c_onesc", [128, 1], F32))
    c_ident = E(nc.sbuf_tensor("c_ident", [128, 128], F32))
    c_maska = E(nc.sbuf_tensor("c_maska", [128, 128], F32))
    c_maskb = E(nc.sbuf_tensor("c_maskb", [128, 128], F32))
    c_o64 = E(nc.sbuf_tensor("c_o64", [64, 1], BF16))
    c_oc128 = E(nc.sbuf_tensor("c_oc128", [128, 1], BF16))
    rowb = E(nc.sbuf_tensor("rowb", [1, 128], F32))
    qmtmp = E(nc.sbuf_tensor("qmtmp", [1, 1], F32))
    qmaxs = E(nc.sbuf_tensor("qmaxs", [1, 1], F32))
    dscal = E(nc.sbuf_tensor("dscal", [1, 16], F32))
    c_gq = E(nc.sbuf_tensor("c_gq", [128, 1], F32))
    c_bq = E(nc.sbuf_tensor("c_bq", [128, 1], F32))
    c_gk = E(nc.sbuf_tensor("c_gk", [128, 1], F32))
    c_bk = E(nc.sbuf_tensor("c_bk", [128, 1], F32))
    c_bo = E(nc.sbuf_tensor("c_bo", [128, 8], F32))
    c_b1 = E(nc.sbuf_tensor("c_b1", [128, 32], F32))
    c_b2 = E(nc.sbuf_tensor("c_b2", [128, 8], F32))
    c_l1b = E(nc.sbuf_tensor("c_l1b", [128, 8], F32))
    c_zero = E(nc.sbuf_tensor("c_zero", [128, 1], F32))
    c_i64 = E(nc.sbuf_tensor("c_i64", [2, 1], F32))
    c_iD = E(nc.sbuf_tensor("c_iD", [1, 1], F32))
    c_iD128 = E(nc.sbuf_tensor("c_iD128", [128, 1], F32))
    c_eps2 = E(nc.sbuf_tensor("c_eps2", [2, 1], F32))

    amax = E(nc.sbuf_tensor("amax", [128, 16], F32))
    negm2 = E(nc.sbuf_tensor("negm2", [128, 2], F32))
    ellp = E(nc.sbuf_tensor("ellp", [128, 8], F32))
    recl2 = E(nc.sbuf_tensor("recl2", [128, 2], F32))
    reclT = E(nc.sbuf_tensor("reclT", [2, 128], F32))
    mtile = E(nc.sbuf_tensor("mtile", [128, 256], F32))
    st_m = E(nc.sbuf_tensor("st_m", [2, 2048], F32))
    st_s = E(nc.sbuf_tensor("st_s", [2, 2048], F32))
    st_r = E(nc.sbuf_tensor("st_r", [2, 2048], F32))
    st_t = E(nc.sbuf_tensor("st_t", [2, 2048], F32))
    kmax2 = E(nc.sbuf_tensor("kmax2", [1, 1], F32))
    kmtmp = E(nc.sbuf_tensor("kmtmp", [1, 1], F32))
    qmax2 = E(nc.sbuf_tensor("qmax2", [2, 8], F32))
    qmaxr = E(nc.sbuf_tensor("qmaxr", [2, 1], F32))
    qmaxf = E(nc.sbuf_tensor("qmaxf", [1, 2], F32))
    bsc = E(nc.sbuf_tensor("bsc", [1, 1], F32))
    negB = E(nc.sbuf_tensor("negB", [128, 1], F32))
    tmpa = E(nc.sbuf_tensor("tmpa", [128, 512], F32))
    tmpb = E(nc.sbuf_tensor("tmpb", [128, 512], F32))
    stg = E(nc.sbuf_tensor("stg", [128, 512], BF16))
    stg2 = E(nc.sbuf_tensor("stg2", [128, 512], BF16))

    X1BF = E(nc.sbuf_tensor("X1BF", [128, NCH, T], BF16))
    XTM = [E(nc.sbuf_tensor(f"XTM{i}", [128, 1024], BF16)) for i in range(2)]

    PS_A = E(nc.psum_tensor("PS_A", [128, 2048], F32))
    PS_B = E(nc.psum_tensor("PS_B", [128, 2048], F32))

    def bA(i, w=512, p0=0, p1=128):
        return PS_A[p0:p1, i * 512:i * 512 + w]

    def bB(i, w=512, p0=0, p1=128):
        return PS_B[p0:p1, i * 512:i * 512 + w]

    bg = {}

    def guard(key):
        if key in bg:
            sem, val = bg[key]
            pe.wait(sem, val)

    def release(key, sem):
        bg[key] = (sem, sem.v)

    for dst, src in [
        (c_e2bf, e2bf_i), (c_e2tf, e2tf_i), (c_ones1, ones1_i), (c_neg1, neg1_i),
        (c_onesc, onesc_i), (c_ident, identf_i), (c_maska, mask_a), (c_maskb, mask_b),
        (c_o64, o64_i), (c_oc128, oc128_i),
        (c_gq, l0_gq), (c_bq, l0_bq), (c_gk, l0_gk), (c_bk, l0_bk),
        (c_bo, l0_bo), (c_b1, l0_b1), (c_b2, l0_b2), (c_l1b, l1_bias),
    ]:
        inc(sy.e.dma_start(out=dst[:, :], in_=src), sDL, 16)
    nc.vector.memset(c_zero[:, :], 0.0)
    nc.vector.memset(c_i64[:, :], 1.0 / 64.0)
    nc.vector.memset(c_iD[:, :], 1.0 / 1024.0)
    nc.vector.memset(c_iD128[:, :], 1.0 / 1024.0)
    inc(nc.vector.memset(c_eps2[:, :], 1e-5), sVE, 1)
    consts_d = sDL.v
    nc.sync.wait_ge(sDL.h, consts_d)

    xtm_free = [0, 0]

    def load_fm(dst3, src2d, nt, coff=0):
        sy.wait(sDT, sDT.v)
        for t in range(nt):
            b = t % 2
            sb = sX[b]
            gp.wait(sDT, sDT.v)
            inc(gp.e.dma_start(out=XTM[b][:, :], in_=src2d[t * 128:(t + 1) * 128, :]), sb, 16)
            sy.wait(sb, sb.v)
            inc(sy.e.dma_start_transpose(
                out=dst3[:, :, coff + t * 128:coff + (t + 1) * 128], in_=XTM[b][:, :]), sDT, 16)
            xtm_free[b] = sDT.v

    def rstd_from_stats(n, rows=2):
        ve.wait(sVE, sVE.v)
        ve.wait(sAC, sAC.v)
        inc(ve.e.tensor_tensor(st_t[0:rows, :n], st_m[0:rows, :n], st_m[0:rows, :n],
                               op=OP.mult), sVE, 1)
        ve.wait(sVE, sVE.v)
        inc(ve.e.tensor_tensor(st_t[0:rows, :n], st_s[0:rows, :n], st_t[0:rows, :n],
                               op=OP.subtract), sVE, 1)
        ac.wait(sVE, sVE.v)
        ac.wait(sPE, sPE.v)
        inc(ac.e.activation(st_r[0:rows, :n], st_t[0:rows, :n], AF.Ln,
                            bias=c_eps2[0:rows, :]), sAC, 1)
        ac.wait(sAC, sAC.v)
        inc(ac.e.activation(st_r[0:rows, :n], st_r[0:rows, :n], AF.Exp,
                            scale=np.float32(-0.5)), sAC, 1)
        return sAC.v

    def postln(XRES, c0, ct):
        nqb = max(1, ct // 512)
        bw = min(512, ct)
        for qb in range(nqb):
            cs = c0 + qb * bw
            lo = qb * bw
            guard("A0")
            guard("A1")
            for j in range(NCH):
                src = XRES[:, j, cs:cs + bw]
                ve.wait(sPE, sPE.v)
                inc(ve.e.tensor_tensor(tmpa[:, :bw], src, src, op=OP.mult), sVE, 1)
                pe.wait(sVE, sVE.v)
                nc.tensor.matmul(bA(0, bw, 0, 1), c_onesc[:, :], src,
                                 start=(j == 0), stop=(j == NCH - 1))
                mm = nc.tensor.matmul(bA(1, bw, 0, 1), c_onesc[:, :], tmpa[:, :bw],
                                      start=(j == 0), stop=(j == NCH - 1))
                inc(mm, sPE, 1)
            ve.wait(sPE, sPE.v)
            ve.wait(sAC, sAC.v)
            inc(ve.e.tensor_scalar(st_m[0:1, lo:lo + bw], bA(0, bw, 0, 1),
                                   c_iD[0:1, :], None, op0=OP.mult), sVE, 1)
            inc(ve.e.tensor_scalar(st_s[0:1, lo:lo + bw], bA(1, bw, 0, 1),
                                   c_iD[0:1, :], None, op0=OP.mult), sVE, 1)
            release("A0", sVE)
            release("A1", sVE)
        rstd_at = rstd_from_stats(ct, rows=1)
        for qb in range(nqb):
            cs = c0 + qb * bw
            lo = qb * bw
            guard("A2")
            guard("A3")
            pe.wait(sAC, rstd_at)
            inc(nc.tensor.matmul(bA(2, bw), c_ones1[:, :], st_m[0:1, lo:lo + bw],
                                 start=True, stop=True), sPE, 1)
            inc(nc.tensor.matmul(bA(3, bw), c_ones1[:, :], st_r[0:1, lo:lo + bw],
                                 start=True, stop=True), sPE, 1)
            ve.wait(sPE, sPE.v)
            for j in range(NCH):
                src = XRES[:, j, cs:cs + bw]
                inc(ve.e.scalar_tensor_tensor(tmpa[:, :bw], src, c_zero[:, :], bA(2, bw),
                                              op0=OP.add, op1=OP.subtract), sVE, 1)
                ve.wait(sVE, sVE.v)
                inc(ve.e.tensor_tensor(src, tmpa[:, :bw], bA(3, bw), op=OP.mult), sVE, 1)
                ve.wait(sVE, sVE.v)
                inc(ve.e.tensor_copy(X1BF[:, j, cs:cs + bw], src), sVE, 1)
            release("A2", sVE)
            release("A3", sVE)

    def kqstats(buf_slice, m_dst, s_dst):
        ve.wait(sPE, sPE.v)
        inc(ve.e.tensor_tensor(stg[:, :], buf_slice, buf_slice, op=OP.mult), sVE, 1)
        guard("B2")
        guard("B3")
        pe.wait(sVE, sVE.v)
        nc.tensor.matmul(bB(2, 512, 0, 2), c_e2bf[:, :], buf_slice, start=True, stop=True)
        inc(nc.tensor.matmul(bB(3, 512, 0, 2), c_e2bf[:, :], stg[:, :],
                             start=True, stop=True), sPE, 1)
        ve.wait(sPE, sPE.v)
        inc(ve.e.tensor_scalar(m_dst, bB(2, 512, 0, 2), c_i64[:, :], None, op0=OP.mult),
            sVE, 1)
        inc(ve.e.tensor_scalar(s_dst, bB(3, 512, 0, 2), c_i64[:, :], None, op0=OP.mult),
            sVE, 1)
        release("B2", sVE)
        release("B3", sVE)

    def lnapply(buf3, cl, c0, n, rstd_at, gv, bv):
        for cb_ in range(n // 512):
            col = c0 + cb_ * 512
            guard("A2")
            guard("A3")
            pe.wait(sAC, rstd_at)
            pe.wait(sVE, sVE.v)
            inc(nc.tensor.matmul(bA(2), c_e2tf[:, :], st_m[0:2, cb_ * 512:(cb_ + 1) * 512],
                                 start=True, stop=True), sPE, 1)
            inc(nc.tensor.matmul(bA(3), c_e2tf[:, :], st_r[0:2, cb_ * 512:(cb_ + 1) * 512],
                                 start=True, stop=True), sPE, 1)
            dst = buf3[:, cl, col:col + 512] if cl is not None else buf3[:, col:col + 512]
            ve.wait(sPE, sPE.v)
            inc(ve.e.scalar_tensor_tensor(tmpa[:, :], dst, c_zero[:, :], bA(2),
                                          op0=OP.add, op1=OP.subtract), sVE, 1)
            ve.wait(sVE, sVE.v)
            inc(ve.e.tensor_tensor(tmpb[:, :], tmpa[:, :], bA(3), op=OP.mult), sVE, 1)
            ve.wait(sVE, sVE.v)
            inc(ve.e.tensor_scalar(dst, tmpb[:, :], gv[:, :], bv[:, :],
                                   op0=OP.mult, op1=OP.add), sVE, 1)
            release("A2", sVE)
            release("A3", sVE)

    sy.wait(sDL, consts_d)
    with ExitStack() as l0s:
        L = l0s.enter_context
        XQBF = L(nc.sbuf_tensor("XQBF", [128, NCH, T], BF16))
        XB0 = L(nc.sbuf_tensor("XB0", [128, NCH, 1024], BF16))
        XB1 = L(nc.sbuf_tensor("XB1", [128, NCH, 1024], BF16))
        XB = [XB0, XB1]
        xb_at = [0, 0]
        KH = L(nc.sbuf_tensor("KH", [128, S], BF16))
        VT = L(nc.sbuf_tensor("VT", [128, NKT * 128], BF16))
        QH = L(nc.sbuf_tensor("QH", [128, T], BF16))
        PB = [L(nc.sbuf_tensor(f"PB{i}", [128, 4096], BF16)) for i in range(2)]
        PT = [L(nc.sbuf_tensor(f"PT{i}", [128, NKT, 128], BF16)) for i in range(2)]
        wq_g = L(nc.sbuf_tensor("wq_g", [128, NCH, 128], BF16))
        wk_g = L(nc.sbuf_tensor("wk_g", [128, NCH, 128], BF16))
        wv_g = L(nc.sbuf_tensor("wv_g", [128, NCH, 128], BF16))

        load_fm(XQBF, xq, 16)
        xq_fm_at = sDT.v
        pb_free = 0
        pt_free = 0

        for g in range(8):
            gp.wait(sPE, sPE.v)
            for dst, srcw in [(wq_g, l0_wqT), (wk_g, l0_wkT), (wv_g, l0_wvT)]:
                wload(gp, sGP, dst, srcw, 8, g * 128, 128)
            wg_at = sGP.v

            if g == 0:
                sy.wait(sPE, sPE.v)
                load_fm(XB[0], xb[0:1024, :], 8)
                xb_at[0] = sDT.v
            for half in range(2):
                for qq in range(2):
                    qr = half * 2 + qq
                    nq = qr + 1
                    sy.wait(sPE, sPE.v)
                    if nq < 4:
                        load_fm(XB[nq % 2], xb[nq * 1024:(nq + 1) * 1024, :], 8)
                        xb_at[nq % 2] = sDT.v
                    elif g < 7:
                        load_fm(XB[0], xb[0:1024, :], 8)
                        xb_at[0] = sDT.v
                    XBF = XB[qr % 2]
                    for tb2 in range(2):
                        tb = qq * 2 + tb2
                        key = f"A{tb % 2}"
                        guard(key)
                        pe.wait(sGP, wg_at)
                        pe.wait(sDT, xb_at[qr % 2])
                        for j in range(NCH):
                            mm = nc.tensor.matmul(bA(tb % 2), wk_g[:, j, :],
                                                  XBF[:, j, tb2 * 512:(tb2 + 1) * 512],
                                                  start=(j == 0), stop=(j == NCH - 1))
                        inc(mm, sPE, 1)
                        khs = KH[:, half * T + tb * 512:half * T + (tb + 1) * 512]
                        ve.wait(sPE, sPE.v)
                        inc(ve.e.tensor_copy(khs, bA(tb % 2)), sVE, 1)
                        release(key, sVE)
                        ve.wait(sVE, sVE.v)
                        kqstats(khs, st_m[:, tb * 512:(tb + 1) * 512],
                                st_s[:, tb * 512:(tb + 1) * 512])
                    for kt2 in range(8):
                        kt = qq * 8 + kt2
                        key = f"B{kt % 2}"
                        guard(key)
                        for j in range(NCH):
                            mm = nc.tensor.matmul(bB(kt % 2, 128),
                                                  XBF[:, j, kt2 * 128:(kt2 + 1) * 128],
                                                  wv_g[:, j, :], start=(j == 0),
                                                  stop=(j == NCH - 1))
                        inc(mm, sPE, 1)
                        gkt = half * 16 + kt
                        ve.wait(sPE, sPE.v)
                        inc(ve.e.tensor_copy(VT[:, gkt * 128:(gkt + 1) * 128],
                                             bB(kt % 2, 128)), sVE, 1)
                        release(key, sVE)
                r_at = rstd_from_stats(T)
                lnapply(KH, None, half * T, T, r_at, c_gk, c_bk)
            for tb in range(4):
                key = f"A{tb % 2}"
                guard(key)
                pe.wait(sDT, xq_fm_at)
                for j in range(NCH):
                    mm = nc.tensor.matmul(bA(tb % 2), wq_g[:, j, :],
                                          XQBF[:, j, tb * 512:(tb + 1) * 512],
                                          start=(j == 0), stop=(j == NCH - 1))
                inc(mm, sPE, 1)
                qhs = QH[:, tb * 512:(tb + 1) * 512]
                ve.wait(sPE, sPE.v)
                inc(ve.e.tensor_copy(qhs, bA(tb % 2)), sVE, 1)
                release(key, sVE)
                ve.wait(sVE, sVE.v)
                kqstats(qhs, st_m[:, tb * 512:(tb + 1) * 512],
                        st_s[:, tb * 512:(tb + 1) * 512])
            r_at = rstd_from_stats(T)
            lnapply(QH, None, 0, T, r_at, c_gq, c_bq)
            gbar()
            lnk_at = sVE.v
            for qt in range(NQT):
                nk = 2 * qt + 2
                qlh = QH[0:64, qt * 128:(qt + 1) * 128]
                qhh = QH[64:128, qt * 128:(qt + 1) * 128]
                ca = (nk + 3) // 4
                for c in range(ca):
                    w = min(4, nk - 4 * c) * 128
                    k0 = c * 512
                    b1_, b2_ = c % 2, 2 + c % 2
                    guard(f"A{b1_}")
                    guard(f"A{b2_}")
                    pe.wait(sVE, lnk_at)
                    nc.tensor.matmul(bA(b1_, w), qlh, KH[0:64, k0:k0 + w],
                                     start=True, stop=True)
                    inc(nc.tensor.matmul(bA(b2_, w), qhh, KH[64:128, k0:k0 + w],
                                         start=True, stop=True, tile_position=(64, 0)),
                        sPE, 1)
                    ve.wait(sPE, sPE.v)
                    if c == ca - 1:
                        for bi in (b1_, b2_):
                            inc(ve.e.tensor_tensor(bA(bi, 512)[:, w - 256:w - 128],
                                                   bA(bi, 512)[:, w - 256:w - 128],
                                                   c_maska[:, :], op=OP.add), sVE, 1)
                            inc(ve.e.tensor_tensor(bA(bi, 512)[:, w - 128:w],
                                                   bA(bi, 512)[:, w - 128:w],
                                                   c_maskb[:, :], op=OP.add), sVE, 1)
                        ve.wait(sVE, sVE.v)
                    inc(ve.e.tensor_reduce(amax[:, c:c + 1], bA(b1_, w), axis=AX.X,
                                           op=OP.max), sVE, 1)
                    inc(ve.e.tensor_reduce(amax[:, 8 + c:9 + c], bA(b2_, w), axis=AX.X,
                                           op=OP.max), sVE, 1)
                    release(f"A{b1_}", sVE)
                    release(f"A{b2_}", sVE)
                ve.wait(sVE, sVE.v)
                ve.wait(sAC, sAC.v)
                inc(ve.e.tensor_reduce(negm2[:, 0:1], amax[:, 0:ca], axis=AX.X,
                                       op=OP.max, negate=True), sVE, 1)
                inc(ve.e.tensor_reduce(negm2[:, 1:2], amax[:, 8:8 + ca], axis=AX.X,
                                       op=OP.max, negate=True), sVE, 1)
                negm_at = sVE.v
                cb = (nk + 7) // 8
                for c in range(cb):
                    w = min(8, nk - 8 * c) * 128
                    k0 = c * 1024
                    PSX = PS_A if c % 2 == 0 else PS_B
                    kyb = "A" if c % 2 == 0 else "B"
                    for bi in range(4):
                        guard(f"{kyb}{bi}")
                    for sub in range((w + 511) // 512):
                        ws = min(512, w - sub * 512)
                        o_ = sub * 512
                        nc.tensor.matmul(PSX[:, o_:o_ + ws], qlh,
                                         KH[0:64, k0 + o_:k0 + o_ + ws],
                                         start=True, stop=True)
                        mm = nc.tensor.matmul(PSX[:, 1024 + o_:1024 + o_ + ws], qhh,
                                              KH[64:128, k0 + o_:k0 + o_ + ws],
                                              start=True, stop=True, tile_position=(64, 0))
                    inc(mm, sPE, 1)
                    if c == cb - 1:
                        ve.wait(sPE, sPE.v)
                        for off in (0, 1024):
                            inc(ve.e.tensor_tensor(PSX[:, off + w - 256:off + w - 128],
                                                   PSX[:, off + w - 256:off + w - 128],
                                                   c_maska[:, :], op=OP.add), sVE, 1)
                            inc(ve.e.tensor_tensor(PSX[:, off + w - 128:off + w],
                                                   PSX[:, off + w - 128:off + w],
                                                   c_maskb[:, :], op=OP.add), sVE, 1)
                    ac.wait(sPE, sPE.v)
                    ac.wait(sVE, sVE.v)
                    ac.wait(sDT, pb_free)
                    inc(ac.e.activation(PB[0][:, k0:k0 + w], PSX[:, 0:w], AF.Exp,
                                        bias=negm2[:, 0:1], accum_out=ellp[:, c:c + 1]),
                        sAC, 1)
                    inc(ac.e.activation(PB[1][:, k0:k0 + w], PSX[:, 1024:1024 + w], AF.Exp,
                                        bias=negm2[:, 1:2], accum_out=ellp[:, 4 + c:5 + c]),
                        sAC, 1)
                    for bi in range(4):
                        release(f"{kyb}{bi}", sAC)
                exp_at = sAC.v
                ve.wait(sAC, exp_at)
                ve.wait(sPE, sPE.v)
                inc(ve.e.tensor_reduce(recl2[:, 0:1], ellp[:, 0:cb], axis=AX.X, op=OP.add),
                    sVE, 1)
                inc(ve.e.tensor_reduce(recl2[:, 1:2], ellp[:, 4:4 + cb], axis=AX.X,
                                       op=OP.add), sVE, 1)
                ve.wait(sVE, sVE.v)
                inc(ve.e.reciprocal(recl2[:, 0:2], recl2[:, 0:2]), sVE, 1)
                recl_at = sVE.v
                sy.wait(sAC, exp_at)
                sy.wait(sPE, pt_free)
                sy.wait(sDT, sDT.v)
                inc(sy.e.dma_start_transpose(out=PT[0][:, 0:nk, :],
                                             in_=PB[0][:, 0:nk * 128]), sDT, 16)
                inc(sy.e.dma_start_transpose(out=PT[1][:, 0:nk, :],
                                             in_=PB[1][:, 0:nk * 128]), sDT, 16)
                pb_free = sDT.v
                guard("B0")
                pe.wait(sDT, sDT.v)
                for kt in range(nk):
                    base = kt * 128
                    nc.tensor.matmul(bB(0, 128, 0, 64), VT[:, base:base + 64],
                                     PT[0][:, kt, :], start=(kt == 0), stop=(kt == nk - 1),
                                     skip_group_check=True)
                    mm = nc.tensor.matmul(bB(0, 128, 64, 128), VT[:, base + 64:base + 128],
                                          PT[1][:, kt, :], start=(kt == 0),
                                          stop=(kt == nk - 1), tile_position=(0, 64),
                                          skip_group_check=True)
                inc(mm, sPE, 1)
                pt_free = sPE.v
                guard("B1")
                pe.wait(sVE, recl_at)
                inc(nc.tensor.transpose(PS_B[0:2, 512:640], recl2[:, 0:2], c_ident[:, :]),
                    sPE, 1)
                ve.wait(sPE, sPE.v)
                inc(ve.e.tensor_copy(reclT[:, :], PS_B[0:2, 512:640]), sVE, 1)
                release("B1", sVE)
                guard("B2")
                pe.wait(sVE, sVE.v)
                inc(nc.tensor.matmul(bB(2, 128), c_e2tf[:, :], reclT[:, :],
                                     start=True, stop=True), sPE, 1)
                ve.wait(sPE, sPE.v)
                inc(ve.e.tensor_copy(mtile[:, 0:128], bB(2, 128)), sVE, 1)
                release("B2", sVE)
                ve.wait(sVE, sVE.v)
                inc(ve.e.tensor_tensor(X1BF[:, g, qt * 128:(qt + 1) * 128],
                                       bB(0, 128), mtile[:, 0:128], op=OP.mult), sVE, 1)
                release("B0", sVE)

    XRES = E(nc.sbuf_tensor("XRES", [128, NCH, T], F32))
    wo_stack = ExitStack()
    XTF = wo_stack.enter_context(nc.sbuf_tensor("XTF", [128, 1024], F32))
    wsl = wo_stack.enter_context(nc.sbuf_tensor("wsl", [128, NCH, 256], BF16))
    fence(ve, pe, sy, gp, ac)

    for t in range(NQT):
        sy.wait(sVE, sVE.v)
        sy.wait(sDL, sDL.v)
        inc(sy.e.dma_start(out=XTF[:, :], in_=xq[t * 128:(t + 1) * 128, :]), sDL, 16)
        pe.wait(sDL, sDL.v)
        for j in range(NCH):
            key = f"B{j % 2}"
            guard(key)
            inc(nc.tensor.transpose(bB(j % 2, 128), XTF[:, j * 128:(j + 1) * 128],
                                    c_ident[:, :]), sPE, 1)
            ve.wait(sPE, sPE.v)
            inc(ve.e.tensor_copy(XRES[:, j, t * 128:(t + 1) * 128], bB(j % 2, 128)), sVE, 1)
            release(key, sVE)
    gbar()
    for g4 in range(4):
        gp.wait(sPE, sPE.v)
        wload(gp, sGP, wsl, l0_woT, 8, g4 * 256, 256)
        wo_at = sGP.v
        for jj in range(2):
            j = g4 * 2 + jj
            for qb in range(4):
                key = f"A{qb % 4}"
                guard(key)
                pe.wait(sGP, wo_at)
                pe.wait(sVE, sVE.v)
                for dch in range(NCH):
                    mm = nc.tensor.matmul(bA(qb % 4), wsl[:, dch, jj * 128:(jj + 1) * 128],
                                          X1BF[:, dch, qb * 512:(qb + 1) * 512],
                                          start=(dch == 0), stop=(dch == NCH - 1))
                inc(mm, sPE, 1)
                dst = XRES[:, j, qb * 512:(qb + 1) * 512]
                ve.wait(sPE, sPE.v)
                inc(ve.e.scalar_tensor_tensor(dst, bA(qb % 4), c_bo[:, j:j + 1], dst,
                                              op0=OP.add, op1=OP.add), sVE, 1)
                release(key, sVE)
    postln(XRES, 0, T)
    wo_stack.close()
    if debug:
        sy.wait(sVE, sVE.v)
        inc(sy.e.dma_start(out=dbgo["d_x1"], in_=XRES[:, :, :]), sST, 16)

    with ExitStack() as f0s:
        L = f0s.enter_context
        u1b = L(nc.sbuf_tensor("u1b", [128, NCH, 256], BF16))
        v1b = L(nc.sbuf_tensor("v1b", [128, 2, 4096], BF16))
        u2b = L(nc.sbuf_tensor("u2b", [128, 32, 256], BF16))
        v2b = L(nc.sbuf_tensor("v2b", [128, 2, 1024], BF16))
        pqb = L(nc.sbuf_tensor("pqb", [128, 2, 512], BF16))
        qqb = L(nc.sbuf_tensor("qqb", [128, 2, 512], BF16))
        fence(gp)
        wload(gp, sGP, u1b, l0_u1, 8, 0, 256)
        wload(gp, sGP, v1b, l0_v1, 2, 0, 4096)
        wload(gp, sGP, u2b, l0_u2, 32, 0, 256)
        wload(gp, sGP, v2b, l0_v2, 2, 0, 1024)
        ffw_at = sGP.v
        gbar()
        for tb in range(4):
            tc = tb * 512
            for rc in range(2):
                key = f"A{rc}"
                guard(key)
                pe.wait(sGP, ffw_at)
                for j in range(NCH):
                    mm = nc.tensor.matmul(bA(rc), u1b[:, j, rc * 128:(rc + 1) * 128],
                                          X1BF[:, j, tc:tc + 512],
                                          start=(j == 0), stop=(j == NCH - 1))
                inc(mm, sPE, 1)
                ve.wait(sPE, sPE.v)
                inc(ve.e.tensor_copy(pqb[:, rc, :], bA(rc)), sVE, 1)
                release(key, sVE)
            guard("B0")
            guard("B1")
            p_at = sVE.v
            for cc in range(32):
                key = f"A{2 + cc % 2}"
                guard(key)
                pe.wait(sVE, p_at)
                for rc in range(2):
                    mm = nc.tensor.matmul(bA(2 + cc % 2), v1b[:, rc, cc * 128:(cc + 1) * 128],
                                          pqb[:, rc, :], start=(rc == 0), stop=(rc == 1))
                inc(mm, sPE, 1)
                hbuf = stg if cc % 2 == 0 else stg2
                hkey = f"H{cc % 2}"
                ac.wait(sPE, sPE.v)
                if hkey in bg:
                    ac.wait(bg[hkey][0], bg[hkey][1])
                inc(ac.e.activation(hbuf[:, :], bA(2 + cc % 2), AF.Gelu_apprx_tanh,
                                    bias=c_b1[:, cc:cc + 1]), sAC, 1)
                release(key, sAC)
                pe.wait(sAC, sAC.v)
                for rc in range(2):
                    mm = nc.tensor.matmul(bB(rc), u2b[:, cc, rc * 128:(rc + 1) * 128],
                                          hbuf[:, :], start=(cc == 0), stop=(cc == 31))
                inc(mm, sPE, 1)
                bg[hkey] = (sPE, sPE.v)
            ve.wait(sPE, sPE.v)
            inc(ve.e.tensor_copy(qqb[:, 0, :], bB(0)), sVE, 1)
            inc(ve.e.tensor_copy(qqb[:, 1, :], bB(1)), sVE, 1)
            release("B0", sVE)
            release("B1", sVE)
            q_at = sVE.v
            for j in range(NCH):
                key = f"A{j % 2}"
                guard(key)
                pe.wait(sVE, q_at)
                for rc in range(2):
                    mm = nc.tensor.matmul(bA(j % 2), v2b[:, rc, j * 128:(j + 1) * 128],
                                          qqb[:, rc, :], start=(rc == 0), stop=(rc == 1))
                inc(mm, sPE, 1)
                dst = XRES[:, j, tc:tc + 512]
                ve.wait(sPE, sPE.v)
                inc(ve.e.scalar_tensor_tensor(dst, bA(j % 2), c_b2[:, j:j + 1], dst,
                                              op0=OP.add, op1=OP.add), sVE, 1)
                release(key, sVE)
        gbar()
        postln(XRES, 0, T)
        if debug:
            sy.wait(sVE, sVE.v)
            inc(sy.e.dma_start(out=dbgo["d_x2"], in_=XRES[:, :, :]), sST, 16)

    with ExitStack() as l1s:
        L = l1s.enter_context
        wq1 = L(nc.sbuf_tensor("wq1", [128, NCH, 512], BF16))
        wk1 = L(nc.sbuf_tensor("wk1", [128, NCH, 128], BF16))
        wv1 = L(nc.sbuf_tensor("wv1", [128, NCH, 64], BF16))
        K2 = L(nc.sbuf_tensor("K2", [128, S], BF16))
        VT1 = L(nc.sbuf_tensor("VT1", [128, NKT * VW], BF16))
        QG = L(nc.sbuf_tensor("QG", [128, NCH, 256], BF16))
        PT0 = L(nc.sbuf_tensor("PX0", [128, NKT * 256], BF16))
        PT1 = L(nc.sbuf_tensor("PX1", [128, NKT * 256], BF16))
        OB1 = L(nc.sbuf_tensor("OB1", [128, NCH, 256], BF16))
        fence(gp)
        wload(gp, sGP, wk1, l1_wkT2, 8, 0, 128)
        wload(gp, sGP, wv1, l1_wvT, 8, 0, 64)
        w1_at = sGP.v
        for tb in range(4):
            key = f"A{tb % 2}"
            guard(key)
            pe.wait(sGP, w1_at)
            for j in range(NCH):
                mm = nc.tensor.matmul(bA(tb % 2), wk1[:, j, :],
                                      X1BF[:, j, tb * 512:(tb + 1) * 512],
                                      start=(j == 0), stop=(j == NCH - 1))
            inc(mm, sPE, 1)
            ve.wait(sPE, sPE.v)
            inc(ve.e.tensor_copy(K2[:, tb * 512:(tb + 1) * 512], bA(tb % 2)), sVE, 1)
            release(key, sVE)
        for kt in range(16):
            key = f"B{kt % 2}"
            guard(key)
            for j in range(NCH):
                mm = nc.tensor.matmul(bB(kt % 2, 64), X1BF[:, j, kt * 128:(kt + 1) * 128],
                                      wv1[:, j, :], start=(j == 0), stop=(j == NCH - 1))
            inc(mm, sPE, 1)
            ve.wait(sPE, sPE.v)
            inc(ve.e.tensor_copy(VT1[:, kt * VW:kt * VW + 64], bB(kt % 2, 64)), sVE, 1)
            release(key, sVE)
            inc(ve.e.memset(VT1[:, kt * VW + 64:kt * VW + 65], 1.0), sVE, 1)
        kv_at = sVE.v
        gp.wait(sVE, kv_at)
        inc(gp.e.dma_start(out=cc_kin.ap(), in_=K2[0:64, 0:T]), sGP, 16)
        gp.wait(sGP, sGP.v)
        inc(gp.e.collective_compute(
            "AllGather", OP.bypass, replica_groups=pairs,
            ins=[cc_kin.ap().opt()], outs=[cc_kout.ap().opt()]), sCC, 1)
        gp.wait(sCC, sCC.v)
        inc(gp.e.dma_start(out=cc_vin.ap(), in_=VT1[:, 0:16 * VW]), sGP, 16)
        gp.wait(sGP, sGP.v)
        inc(gp.e.collective_compute(
            "AllGather", OP.bypass, replica_groups=pairs,
            ins=[cc_vin.ap().opt()], outs=[cc_vout.ap().opt()]), sCC, 1)
        gp.wait(sCC, sCC.v)
        inc(gp.e.dma_start(out=K2[0:64, 0:T], in_=cc_kout.ap()[0:64, :]), sGP, 16)
        gp.wait(sGP, sGP.v)
        inc(gp.e.dma_start(out=K2[0:64, T:S], in_=cc_kout.ap()[64:128, :]), sGP, 16)
        gp.wait(sGP, sGP.v)
        inc(gp.e.dma_start(out=VT1[:, 0:16 * VW], in_=cc_vout.ap()[0:128, :]), sGP, 16)
        gp.wait(sGP, sGP.v)
        inc(gp.e.dma_start(out=VT1[:, 16 * VW:32 * VW], in_=cc_vout.ap()[128:256, :]), sGP, 16)
        gp.wait(sGP, sGP.v)
        inc(gp.e.dma_start(out=K2[64:128, 0:S], in_=K2[0:64, 0:S]), sGP, 16)
        kg_at = sGP.v
        for cb2 in range(8):
            sl = K2[0:64, cb2 * 512:(cb2 + 1) * 512]
            ve.wait(sGP, kg_at)
            ve.wait(sPE, sPE.v)
            inc(ve.e.tensor_tensor(stg[0:64, :], sl, sl, op=OP.mult), sVE, 1)
            guard("B3")
            pe.wait(sVE, sVE.v)
            inc(nc.tensor.matmul(bB(3, 512, 0, 1), c_o64[:, :], stg[0:64, :],
                                 start=True, stop=True), sPE, 1)
            ve.wait(sPE, sPE.v)
            if cb2 == 0:
                inc(ve.e.tensor_reduce(kmax2[:, :], bB(3, 512, 0, 1), axis=AX.X, op=OP.max),
                    sVE, 1)
            else:
                inc(ve.e.tensor_reduce(kmtmp[:, :], bB(3, 512, 0, 1), axis=AX.X, op=OP.max),
                    sVE, 1)
                ve.wait(sVE, sVE.v)
                inc(ve.e.tensor_tensor(kmax2[:, :], kmax2[:, :], kmtmp[:, :], op=OP.max),
                    sVE, 1)
            release("B3", sVE)
        kmax_at = sVE.v
        for qg in range(8):
            qc = qg * 256
            for j in range(NCH):
                if j % 4 == 0:
                    hh = j // 4
                    gp.wait(sPE, sPE.v)
                    wload(gp, sGP, wq1, l1_wqT, 8, hh * 512, 512)
                    wq_at = sGP.v
                key = f"A{j % 2}"
                guard(key)
                pe.wait(sGP, wq_at)
                for dch in range(NCH):
                    mm = nc.tensor.matmul(bA(j % 2, 256),
                                          wq1[:, dch, (j % 4) * 128:(j % 4 + 1) * 128],
                                          X1BF[:, dch, qc:qc + 256],
                                          start=(dch == 0), stop=(dch == NCH - 1))
                inc(mm, sPE, 1)
                ve.wait(sPE, sPE.v)
                inc(ve.e.tensor_copy(QG[:, j, :], bA(j % 2, 256)), sVE, 1)
                release(key, sVE)
                qsl = QG[:, j, :]
                ve.wait(sVE, sVE.v)
                ve.wait(sPE, sPE.v)
                inc(ve.e.tensor_tensor(stg[:, 0:256], qsl, qsl, op=OP.mult), sVE, 1)
                guard("B3")
                pe.wait(sVE, sVE.v)
                inc(nc.tensor.matmul(bB(3, 256, 0, 1), c_oc128[:, :], stg[:, 0:256],
                                     start=True, stop=True), sPE, 1)
                ve.wait(sPE, sPE.v)
                if j == 0:
                    inc(ve.e.tensor_reduce(qmaxs[:, :], bB(3, 256, 0, 1), axis=AX.X,
                                           op=OP.max), sVE, 1)
                else:
                    inc(ve.e.tensor_reduce(qmtmp[:, :], bB(3, 256, 0, 1), axis=AX.X,
                                           op=OP.max), sVE, 1)
                    ve.wait(sVE, sVE.v)
                    inc(ve.e.tensor_tensor(qmaxs[:, :], qmaxs[:, :], qmtmp[:, :],
                                           op=OP.max), sVE, 1)
                release("B3", sVE)
            ve.wait(sVE, sVE.v)
            ve.wait(sVE, kmax_at)
            inc(ve.e.tensor_tensor(bsc[:, :], qmaxs[:, :], kmax2[:, :], op=OP.mult), sVE, 1)
            ac.wait(sVE, sVE.v)
            inc(ac.e.activation(bsc[:, :], bsc[:, :], AF.Ln), sAC, 1)
            ac.wait(sAC, sAC.v)
            inc(ac.e.activation(bsc[:, :], bsc[:, :], AF.Exp, scale=np.float32(0.5),
                                bias=np.float32(-2.0794415416798357)), sAC, 1)
            ve.wait(sAC, sAC.v)
            ve.wait(sPE, sPE.v)
            inc(ve.e.tensor_scalar(rowb[:, :], c_ones1[:, :], bsc[0:1, :], None,
                                   op0=OP.mult), sVE, 1)
            guard("B3")
            pe.wait(sVE, sVE.v)
            inc(nc.tensor.transpose(PS_B[0:128, 1536:1537], rowb[:, :], c_ones1[0:1, 0:1]),
                sPE, 1)
            ve.wait(sPE, sPE.v)
            inc(ve.e.scalar_tensor_tensor(negB[:, :], c_zero[:, :], c_zero[:, :],
                                          PS_B[0:128, 1536:1537],
                                          op0=OP.add, op1=OP.subtract), sVE, 1)
            release("B3", sVE)
            negb_at = sVE.v
            for pr in range(NCH):
                q_lo = QG[0:64, pr, :]
                q_hi = QG[64:128, pr, :]
                for ktg in range(8):
                    for bi in range(4):
                        guard(f"A{bi}")
                    pe.wait(sGP, kg_at)
                    pe.wait(sVE, negb_at)
                    for kk in range(4):
                        kt = ktg * 4 + kk
                        nc.tensor.matmul(PS_A[:, kk * 256:kk * 256 + 256],
                                         K2[0:64, kt * 128:(kt + 1) * 128], q_lo,
                                         start=True, stop=True)
                        mm = nc.tensor.matmul(PS_A[:, 1024 + kk * 256:1024 + kk * 256 + 256],
                                              K2[64:128, kt * 128:(kt + 1) * 128], q_hi,
                                              start=True, stop=True, tile_position=(64, 0))
                    inc(mm, sPE, 1)
                    ac.wait(sPE, sPE.v)
                    ac.wait(sVE, negb_at)
                    inc(ac.e.activation(PT0[:, ktg * 1024:(ktg + 1) * 1024], PS_A[:, 0:1024],
                                        AF.Exp, bias=negB[:, :], scale=np.float32(0.125)),
                        sAC, 1)
                    inc(ac.e.activation(PT1[:, ktg * 1024:(ktg + 1) * 1024],
                                        PS_A[:, 1024:2048], AF.Exp, bias=negB[:, :],
                                        scale=np.float32(0.125)), sAC, 1)
                    for bi in range(4):
                        release(f"A{bi}", sAC)
                for hl, PTx in ((0, PT0), (1, PT1)):
                    key = f"B{hl}"
                    guard(key)
                    pe.wait(sAC, sAC.v)
                    for kt in range(NKT):
                        mm = nc.tensor.matmul(bB(hl, 256, 0, 65), VT1[:, kt * VW:kt * VW + 65],
                                              PTx[:, kt * 256:(kt + 1) * 256],
                                              start=(kt == 0), stop=(kt == NKT - 1))
                    inc(mm, sPE, 1)
                    ve.wait(sPE, sPE.v)
                    inc(ve.e.reciprocal(mtile[0:1, 0:256],
                                        PS_B[64:65, hl * 512:hl * 512 + 256]), sVE, 1)
                    guard("B2")
                    pe.wait(sVE, sVE.v)
                    inc(nc.tensor.matmul(bB(2, 256), c_ones1[:, :], mtile[0:1, 0:256],
                                         start=True, stop=True), sPE, 1)
                    ve.wait(sPE, sPE.v)
                    inc(ve.e.tensor_copy(mtile[:, 0:256], bB(2, 256)), sVE, 1)
                    release("B2", sVE)
                    ve.wait(sVE, sVE.v)
                    inc(ve.e.tensor_tensor(OB1[hl * 64:(hl + 1) * 64, pr, :],
                                           PS_B[0:64, hl * 512:hl * 512 + 256],
                                           mtile[0:64, 0:256], op=OP.mult), sVE, 1)
                    release(key, sVE)
            gbar()
            fence(ve, ac, pe)
            for j in range(NCH):
                dst = XRES[:, j, qc:qc + 256]
                ve.wait(sVE, sVE.v)
                inc(ve.e.scalar_tensor_tensor(dst, OB1[:, j, :], c_zero[:, :], dst,
                                              op0=OP.add, op1=OP.add), sVE, 1)
            postln(XRES, qc, 256)
        if debug:
            sy.wait(sVE, sVE.v)
            inc(sy.e.dma_start(out=dbgo["d_x3"], in_=XRES[:, :, :]), sST, 16)

    with ExitStack() as f1s:
        L = f1s.enter_context
        T1b = L(nc.sbuf_tensor("T1b", [128, NCH, 256], BF16))
        T2b = L(nc.sbuf_tensor("T2b", [128, NCH, 256], BF16))
        wrb = L(nc.sbuf_tensor("wrb", [128, NCH, 512], BF16))
        wib = L(nc.sbuf_tensor("wib", [128, NCH, 512], BF16))
        csl = L(nc.sbuf_tensor("csl", [128, NCH, 256], BF16))
        ssl = L(nc.sbuf_tensor("ssl", [128, NCH, 256], BF16))
        snl = L(nc.sbuf_tensor("snl", [128, NCH, 256], BF16))
        cdo = L(nc.sbuf_tensor("cdo", [128, NCH, 128], BF16))
        sdo = L(nc.sbuf_tensor("sdo", [128, NCH, 128], BF16))
        WEF = L(nc.sbuf_tensor("WEF", [128, NCH, 256], BF16))
        fence(gp, sy)
        for cs in range(4):
            sy.wait(sPE, sPE.v)
            for dst, src_ in ((csl, cCb), (ssl, cSb), (snl, cSnb)):
                wload(sy, sDL, dst, src_, 8, cs * 256, 256)
            cs_at = sDL.v
            for hh in range(2):
                gp.wait(sPE, sPE.v)
                wload(gp, sGP, wrb, wrT_i, 8, hh * 512, 512)
                wload(gp, sGP, wib, wiT_i, 8, hh * 512, 512)
                wr_at = sGP.v
                for fq in range(4):
                    fo = hh * 4 + fq
                    key = f"A{fo % 4}"
                    guard(key)
                    pe.wait(sGP, wr_at)
                    pe.wait(sDL, cs_at)
                    for j in range(NCH):
                        nc.tensor.matmul(bA(fo % 4, 256),
                                         wrb[:, j, fq * 128:(fq + 1) * 128],
                                         csl[:, j, :], start=(j == 0), stop=False)
                    for j in range(NCH):
                        mm = nc.tensor.matmul(bA(fo % 4, 256),
                                              wib[:, j, fq * 128:(fq + 1) * 128],
                                              snl[:, j, :], start=False,
                                              stop=(j == NCH - 1))
                    inc(mm, sPE, 1)
                    ve.wait(sPE, sPE.v)
                    inc(ve.e.tensor_copy(T1b[:, fo, :], bA(fo % 4, 256)), sVE, 1)
                    release(key, sVE)
                    key = f"B{fo % 4}"
                    guard(key)
                    for j in range(NCH):
                        nc.tensor.matmul(bB(fo % 4, 256),
                                         wib[:, j, fq * 128:(fq + 1) * 128],
                                         csl[:, j, :], start=(j == 0), stop=False)
                    for j in range(NCH):
                        mm = nc.tensor.matmul(bB(fo % 4, 256),
                                              wrb[:, j, fq * 128:(fq + 1) * 128],
                                              ssl[:, j, :], start=False,
                                              stop=(j == NCH - 1))
                    inc(mm, sPE, 1)
                    ve.wait(sPE, sPE.v)
                    inc(ve.e.tensor_copy(T2b[:, fo, :], bB(fo % 4, 256)), sVE, 1)
                    release(key, sVE)
            t12_at = sVE.v
            for do in range(NCH):
                sy.wait(sPE, sPE.v)
                wload(sy, sDL, cdo, cCb, 8, do * 128, 128)
                wload(sy, sDL, sdo, cSb, 8, do * 128, 128)
                do_at = sDL.v
                key = f"A{do % 4}"
                guard(key)
                pe.wait(sDL, do_at)
                pe.wait(sVE, t12_at)
                for f_ in range(NCH):
                    nc.tensor.matmul(bA(do % 4, 256), cdo[:, f_, :], T1b[:, f_, :],
                                     start=(f_ == 0), stop=False)
                for f_ in range(NCH):
                    mm = nc.tensor.matmul(bA(do % 4, 256), sdo[:, f_, :], T2b[:, f_, :],
                                          start=False, stop=(f_ == NCH - 1))
                inc(mm, sPE, 1)
                ve.wait(sPE, sPE.v)
                inc(ve.e.tensor_scalar(WEF[:, do, :], bA(do % 4, 256),
                                       c_iD128[:, :], None, op0=OP.mult), sVE, 1)
                release(key, sVE)
            wef_at = sVE.v
            for jj in range(2):
                j = cs * 2 + jj
                for qb in range(4):
                    key = f"B{qb % 2}"
                    guard(key)
                    pe.wait(sVE, wef_at)
                    for dch in range(NCH):
                        mm = nc.tensor.matmul(
                            bB(qb % 2), WEF[:, dch, jj * 128:(jj + 1) * 128],
                            X1BF[:, dch, qb * 512:(qb + 1) * 512],
                            start=(dch == 0), stop=(dch == NCH - 1))
                    inc(mm, sPE, 1)
                    dst = XRES[:, j, qb * 512:(qb + 1) * 512]
                    ve.wait(sPE, sPE.v)
                    inc(ve.e.scalar_tensor_tensor(dst, bB(qb % 2), c_l1b[:, j:j + 1], dst,
                                                  op0=OP.add, op1=OP.add), sVE, 1)
                    release(key, sVE)
        postln(XRES, 0, T)

    XOT = [E(nc.sbuf_tensor(f"XOT{i}", [128, 1024], F32)) for i in range(2)]
    xot_free = [0, 0]
    for t in range(NQT):
        b = t % 2
        for j in range(NCH):
            key = f"B{2 + j % 2}"
            guard(key)
            pe.wait(sVE, sVE.v)
            inc(nc.tensor.transpose(bB(2 + j % 2, 128), XRES[:, j, t * 128:(t + 1) * 128],
                                    c_ident[:, :]), sPE, 1)
            ve.wait(sPE, sPE.v)
            ve.wait(sST, xot_free[b])
            inc(ve.e.tensor_copy(XOT[b][:, j * 128:(j + 1) * 128], bB(2 + j % 2, 128)), sVE, 1)
            release(key, sVE)
        sy.wait(sVE, sVE.v)
        sy.wait(sST, sST.v)
        inc(sy.e.dma_start(out=xout[t * 128:(t + 1) * 128, :], in_=XOT[b][:, :]), sST, 16)
        xot_free[b] = sST.v
    gp.wait(sST, sST.v)
    gp.wait(sDT, sDT.v)
    gp.wait(sDL, sDL.v)
    gp.wait(sGP, sGP.v)

    est.close()
    return nc


def _mk_consts():
    d = np.arange(D, dtype=np.int64)
    prod = np.outer(d, d) % D
    ang = 2.0 * np.pi * prod.astype(np.float64) / D
    return np.cos(ang), np.sin(ang)


def kernel(
    L0_attn_x, L0_attn_mask, L0_attn_Wq, L0_attn_Wk, L0_attn_Wv, L0_attn_gq, L0_attn_bq,
    L0_attn_gk, L0_attn_bk, L0_attn_Wo, L0_attn_bo, L0_ffn_U1, L0_ffn_V1, L0_ffn_b1,
    L0_ffn_U2, L0_ffn_V2, L0_ffn_b2, L1_attn_Wq, L1_attn_Wk, L1_attn_Wv,
    L1_ffn_w_freq_real, L1_ffn_w_freq_imag, L1_ffn_bias,
):
    f32 = np.float32
    bf16 = ml_dtypes.bfloat16
    x = np.ascontiguousarray(np.asarray(L0_attn_x, f32))

    def t2(a):
        return np.ascontiguousarray(np.asarray(a, f32).T)

    def col(a, n=128):
        a = np.asarray(a, f32).reshape(-1)
        return np.ascontiguousarray(a.reshape(-1, n).T)

    def rep2(a):
        a = np.asarray(a, f32).reshape(-1)
        return np.ascontiguousarray(np.concatenate([a, a])[:, None])

    C, Sm = _mk_consts()
    e2 = np.zeros((128, 2), f32)
    e2[0:64, 0] = 1.0
    e2[64:128, 1] = 1.0
    shared = {
        "l0_wqT": t2(L0_attn_Wq), "l0_wkT": t2(L0_attn_Wk), "l0_wvT": t2(L0_attn_Wv),
        "l0_woT": t2(L0_attn_Wo), "l0_bo": col(L0_attn_bo),
        "l0_gq": rep2(L0_attn_gq), "l0_bq": rep2(L0_attn_bq),
        "l0_gk": rep2(L0_attn_gk), "l0_bk": rep2(L0_attn_bk),
        "l0_u1": np.ascontiguousarray(np.asarray(L0_ffn_U1, f32)),
        "l0_v1": np.ascontiguousarray(np.asarray(L0_ffn_V1, f32)),
        "l0_b1": col(L0_ffn_b1),
        "l0_u2": np.ascontiguousarray(np.asarray(L0_ffn_U2, f32)),
        "l0_v2": np.ascontiguousarray(np.asarray(L0_ffn_V2, f32)),
        "l0_b2": col(L0_ffn_b2),
        "l1_wqT": t2(L1_attn_Wq),
        "l1_wkT2": np.ascontiguousarray(
            np.concatenate([np.asarray(L1_attn_Wk, f32).T] * 2, axis=1)),
        "l1_wvT": t2(L1_attn_Wv),
        "l1_bias": col(L1_ffn_bias),
        "wrT": t2(L1_ffn_w_freq_real), "wiT": t2(L1_ffn_w_freq_imag),
        "cCb": C.astype(bf16), "cSb": Sm.astype(bf16), "cSnb": (-Sm).astype(bf16),
        "e2bf": e2.astype(bf16),
        "e2tf": np.ascontiguousarray(e2.T),
        "ones1": np.ones((1, 128), f32),
        "neg1": -np.ones((1, 128), f32),
        "onesc": np.ones((128, 1), f32),
        "identf": np.eye(128, dtype=f32),
        "o64": np.ones((64, 1), bf16),
        "oc128": np.ones((128, 1), bf16),
    }
    trilm = np.where(np.tril(np.ones((128, 128), bool)), 0.0, -1e30).astype(f32)
    fullm = np.full((128, 128), -1e30, f32)
    zerom = np.zeros((128, 128), f32)

    nc = build()
    in_maps = []
    for c in range(8):
        n, f = c // 2, c % 2
        tiles = [2 * i + f for i in range(16)]
        xq_rows = np.ascontiguousarray(
            np.concatenate([x[n, t * 128:(t + 1) * 128, :] for t in tiles], axis=0))
        im = dict(shared)
        im["xq"] = xq_rows
        im["xb"] = np.ascontiguousarray(x[n])
        im["mask_a"] = trilm if f == 0 else zerom
        im["mask_b"] = fullm if f == 0 else trilm
        in_maps.append(im)

    res = run_bass_kernel_spmd(nc, in_maps, core_ids=list(range(8)))
    out = np.empty((4, 4096, 1024), f32)
    for c in range(8):
        n, f = c // 2, c % 2
        r = np.asarray(res.results[c]["xout"], f32)
        for i in range(16):
            t = 2 * i + f
            out[n, t * 128:(t + 1) * 128, :] = r[i * 128:(i + 1) * 128, :]
    return out


# version 09
# speedup vs baseline: 1.1433x; 1.0397x over previous
import sys

sys.path.insert(0, "/opt/trn_rl_repo")
from contextlib import ExitStack

import numpy as np
import ml_dtypes
import concourse.bass as bass
import concourse.mybir as mybir
from concourse.bass_utils import run_bass_kernel_spmd
from concourse.alu_op_type import AluOpType as OP

F32 = mybir.dt.float32
BF16 = mybir.dt.bfloat16
AF = mybir.ActivationFunctionType
AX = mybir.AxisListType

D = 1024
S = 4096
NCH = 8
T = 2048
NQT = 16
NKT = 32
VW = 72


class Sem:
    __slots__ = ("h", "v")

    def __init__(self, h):
        self.h = h
        self.v = 0


class Wr:
    def __init__(self, e):
        self.e = e
        self.seen = {}

    def wait(self, sem, val=None):
        if val is None:
            val = sem.v
        if val <= 0 or self.seen.get(id(sem), 0) >= val:
            return
        self.e.wait_ge(sem.h, val)
        self.seen[id(sem)] = val


def inc(instr, sem, n=1):
    instr.then_inc(sem.h, n)
    sem.v += n


DBGQG = 0


def build(debug=0):
    nc = bass.Bass("TRN2", target_bir_lowering=False)

    def din(name, shape, dt=F32):
        return nc.dram_tensor(name, shape, dt, kind="ExternalInput").ap()

    xq = din("xq", [T, D])
    xb = din("xb", [S, D])
    mask_a = din("mask_a", [128, 128])
    mask_b = din("mask_b", [128, 128])
    l0_wqT = din("l0_wqT", [D, D])
    l0_wkT = din("l0_wkT", [D, D])
    l0_wvT = din("l0_wvT", [D, D])
    l0_woT = din("l0_woT", [D, D])
    l0_bo = din("l0_bo", [128, 8])
    l0_gq = din("l0_gq", [128, 1])
    l0_bq = din("l0_bq", [128, 1])
    l0_gk = din("l0_gk", [128, 1])
    l0_bk = din("l0_bk", [128, 1])
    l0_u1 = din("l0_u1", [D, 256])
    l0_v1 = din("l0_v1", [256, 4096])
    l0_b1 = din("l0_b1", [128, 32])
    l0_u2 = din("l0_u2", [4096, 256])
    l0_v2 = din("l0_v2", [256, D])
    l0_b2 = din("l0_b2", [128, 8])
    l1_wqT = din("l1_wqT", [D, D])
    l1_wkT2 = din("l1_wkT2", [D, 128])
    l1_wvT = din("l1_wvT", [D, 64])
    l1_bias = din("l1_bias", [128, 8])
    wrT_i = din("wrT", [D, D])
    wiT_i = din("wiT", [D, D])
    cCb = din("cCb", [D, D], BF16)
    cSb = din("cSb", [D, D], BF16)
    cSnb = din("cSnb", [D, D], BF16)
    e2bf_i = din("e2bf", [128, 2], BF16)
    e2tf_i = din("e2tf", [2, 128])
    ones1_i = din("ones1", [1, 128])
    neg1_i = din("neg1", [1, 128])
    onesc_i = din("onesc", [128, 1])
    identf_i = din("identf", [128, 128])
    o64_i = din("o64", [64, 1], BF16)
    oc128_i = din("oc128", [128, 1], BF16)

    xout = nc.dram_tensor("xout", [T, D], F32, kind="ExternalOutput").ap()
    dbgo = {}
    if debug:
        for nm in ("d_x1", "d_x2", "d_x3"):
            dbgo[nm] = nc.dram_tensor(nm, [128, NCH, T], F32, kind="ExternalOutput").ap()
        dbgo["d_k2"] = nc.dram_tensor("d_k2", [128, S], F32, kind="ExternalOutput").ap()
        dbgo["d_vt1"] = nc.dram_tensor("d_vt1", [128, NKT * VW], F32,
                                       kind="ExternalOutput").ap()
        dbgo["d_negb"] = nc.dram_tensor("d_negb", [128, 1], F32, kind="ExternalOutput").ap()
        dbgo["d_qg"] = nc.dram_tensor("d_qg", [128, NCH, 256], F32,
                                      kind="ExternalOutput").ap()
        dbgo["d_pt0"] = nc.dram_tensor("d_pt0", [128, 1024], F32, kind="ExternalOutput").ap()
        dbgo["d_ob1"] = nc.dram_tensor("d_ob1", [128, NCH, 256], F32,
                                       kind="ExternalOutput").ap()
        dbgo["d_scal"] = nc.dram_tensor("d_scal", [1, 16], F32, kind="ExternalOutput").ap()
        dbgo["d_xr"] = nc.dram_tensor("d_xr", [128, NCH, 512], F32,
                                      kind="ExternalOutput").ap()
        dbgo["d_xr0"] = nc.dram_tensor("d_xr0", [128, NCH, 256], F32,
                                       kind="ExternalOutput").ap()
        dbgo["d_stm"] = nc.dram_tensor("d_stm", [2, 2048], F32, kind="ExternalOutput").ap()
        dbgo["d_sts"] = nc.dram_tensor("d_sts", [2, 2048], F32, kind="ExternalOutput").ap()
        dbgo["d_str"] = nc.dram_tensor("d_str", [2, 2048], F32, kind="ExternalOutput").ap()

    cc_kin = nc.dram_tensor("cc_kin", [64, T], BF16)
    cc_kout = nc.dram_tensor("cc_kout", [128, T], BF16)
    cc_vin = nc.dram_tensor("cc_vin", [128, NQT * VW], BF16)
    cc_vout = nc.dram_tensor("cc_vout", [256, NQT * VW], BF16)
    pairs = [[0, 1], [2, 3], [4, 5], [6, 7]]

    est = ExitStack()
    E = est.enter_context

    sDL = Sem(E(nc.semaphore("sDL")))
    sDT = Sem(E(nc.semaphore("sDT")))
    sST = Sem(E(nc.semaphore("sST")))
    sPE = Sem(E(nc.semaphore("sPE")))
    sVE = Sem(E(nc.semaphore("sVE")))
    sAC = Sem(E(nc.semaphore("sAC")))
    sGP = Sem(E(nc.semaphore("sGP")))
    sCC = Sem(E(nc.semaphore("sCC")))
    sX0 = Sem(E(nc.semaphore("sX0")))
    sX1 = Sem(E(nc.semaphore("sX1")))
    sX = [sX0, sX1]
    sGB = Sem(E(nc.semaphore("sGB")))
    ALLSEMS = [sDL, sDT, sST, sPE, sVE, sAC, sGP, sCC]

    pe = Wr(nc.tensor)
    ve = Wr(nc.vector)
    ac = Wr(nc.scalar)
    gp = Wr(nc.gpsimd)
    sy = Wr(nc.sync)

    def fence(*engs):
        for e_ in engs:
            for sm_ in ALLSEMS:
                e_.wait(sm_, sm_.v)

    def gbar():
        gp.wait(sVE, sVE.v)
        gp.wait(sAC, sAC.v)
        inc(gp.e.nop(), sGB, 1)
        ve.wait(sGB, sGB.v)
        ac.wait(sGB, sGB.v)
        pe.wait(sGB, sGB.v)

    def wload(engw, sem, dst3, src2d, nj, c0, w):
        engw.wait(sem, sem.v)
        for j_ in range(nj):
            inc(engw.e.dma_start(out=dst3[:, j_, :],
                                 in_=src2d[j_ * 128:(j_ + 1) * 128, c0:c0 + w]), sem, 16)

    c_e2bf = E(nc.sbuf_tensor("c_e2bf", [128, 2], BF16))
    c_e2tf = E(nc.sbuf_tensor("c_e2tf", [2, 128], F32))
    c_ones1 = E(nc.sbuf_tensor("c_ones1", [1, 128], F32))
    c_neg1 = E(nc.sbuf_tensor("c_neg1", [1, 128], F32))
    c_onesc = E(nc.sbuf_tensor("c_onesc", [128, 1], F32))
    c_ident = E(nc.sbuf_tensor("c_ident", [128, 128], F32))
    c_maska = E(nc.sbuf_tensor("c_maska", [128, 128], F32))
    c_maskb = E(nc.sbuf_tensor("c_maskb", [128, 128], F32))
    c_o64 = E(nc.sbuf_tensor("c_o64", [64, 1], BF16))
    c_oc128 = E(nc.sbuf_tensor("c_oc128", [128, 1], BF16))
    rowb = E(nc.sbuf_tensor("rowb", [1, 128], F32))
    qmtmp = E(nc.sbuf_tensor("qmtmp", [1, 1], F32))
    qmaxs = E(nc.sbuf_tensor("qmaxs", [1, 1], F32))
    dscal = E(nc.sbuf_tensor("dscal", [1, 16], F32))
    c_gq = E(nc.sbuf_tensor("c_gq", [128, 1], F32))
    c_bq = E(nc.sbuf_tensor("c_bq", [128, 1], F32))
    c_gk = E(nc.sbuf_tensor("c_gk", [128, 1], F32))
    c_bk = E(nc.sbuf_tensor("c_bk", [128, 1], F32))
    c_bo = E(nc.sbuf_tensor("c_bo", [128, 8], F32))
    c_b1 = E(nc.sbuf_tensor("c_b1", [128, 32], F32))
    c_b2 = E(nc.sbuf_tensor("c_b2", [128, 8], F32))
    c_l1b = E(nc.sbuf_tensor("c_l1b", [128, 8], F32))
    c_zero = E(nc.sbuf_tensor("c_zero", [128, 1], F32))
    c_i64 = E(nc.sbuf_tensor("c_i64", [2, 1], F32))
    c_iD = E(nc.sbuf_tensor("c_iD", [1, 1], F32))
    c_iD128 = E(nc.sbuf_tensor("c_iD128", [128, 1], F32))
    c_eps2 = E(nc.sbuf_tensor("c_eps2", [2, 1], F32))

    amax = E(nc.sbuf_tensor("amax", [128, 16], F32))
    negm2 = E(nc.sbuf_tensor("negm2", [128, 2], F32))
    ellp = E(nc.sbuf_tensor("ellp", [128, 8], F32))
    recl2 = E(nc.sbuf_tensor("recl2", [128, 2], F32))
    reclT = E(nc.sbuf_tensor("reclT", [2, 128], F32))
    mtile = E(nc.sbuf_tensor("mtile", [128, 256], F32))
    st_m = E(nc.sbuf_tensor("st_m", [2, 2048], F32))
    st_s = E(nc.sbuf_tensor("st_s", [2, 2048], F32))
    st_r = E(nc.sbuf_tensor("st_r", [2, 2048], F32))
    st_t = E(nc.sbuf_tensor("st_t", [2, 2048], F32))
    kmax2 = E(nc.sbuf_tensor("kmax2", [1, 1], F32))
    kmtmp = E(nc.sbuf_tensor("kmtmp", [1, 1], F32))
    qmax2 = E(nc.sbuf_tensor("qmax2", [2, 8], F32))
    qmaxr = E(nc.sbuf_tensor("qmaxr", [2, 1], F32))
    qmaxf = E(nc.sbuf_tensor("qmaxf", [1, 2], F32))
    bsc = E(nc.sbuf_tensor("bsc", [1, 1], F32))
    negB = E(nc.sbuf_tensor("negB", [128, 1], F32))
    tmpa = E(nc.sbuf_tensor("tmpa", [128, 512], F32))
    tmpb = E(nc.sbuf_tensor("tmpb", [128, 512], F32))
    mtil2 = tmpb
    stg = E(nc.sbuf_tensor("stg", [128, 512], BF16))
    stg2 = E(nc.sbuf_tensor("stg2", [128, 512], BF16))

    X1BF = E(nc.sbuf_tensor("X1BF", [128, NCH, T], BF16))
    XTM = [E(nc.sbuf_tensor(f"XTM{i}", [128, 1024], BF16)) for i in range(2)]

    PS_A = E(nc.psum_tensor("PS_A", [128, 2048], F32))
    PS_B = E(nc.psum_tensor("PS_B", [128, 2048], F32))

    def bA(i, w=512, p0=0, p1=128):
        return PS_A[p0:p1, i * 512:i * 512 + w]

    def bB(i, w=512, p0=0, p1=128):
        return PS_B[p0:p1, i * 512:i * 512 + w]

    bg = {}

    def guard(key):
        if key in bg:
            sem, val = bg[key]
            pe.wait(sem, val)

    def release(key, sem):
        bg[key] = (sem, sem.v)

    for dst, src in [
        (c_e2bf, e2bf_i), (c_e2tf, e2tf_i), (c_ones1, ones1_i), (c_neg1, neg1_i),
        (c_onesc, onesc_i), (c_ident, identf_i), (c_maska, mask_a), (c_maskb, mask_b),
        (c_o64, o64_i), (c_oc128, oc128_i),
        (c_gq, l0_gq), (c_bq, l0_bq), (c_gk, l0_gk), (c_bk, l0_bk),
        (c_bo, l0_bo), (c_b1, l0_b1), (c_b2, l0_b2), (c_l1b, l1_bias),
    ]:
        inc(sy.e.dma_start(out=dst[:, :], in_=src), sDL, 16)
    nc.vector.memset(c_zero[:, :], 0.0)
    nc.vector.memset(c_i64[:, :], 1.0 / 64.0)
    nc.vector.memset(c_iD[:, :], 1.0 / 1024.0)
    nc.vector.memset(c_iD128[:, :], 1.0 / 1024.0)
    inc(nc.vector.memset(c_eps2[:, :], 1e-5), sVE, 1)
    consts_d = sDL.v
    nc.sync.wait_ge(sDL.h, consts_d)

    xtm_free = [0, 0]

    def load_fm(dst3, src2d, nt, coff=0):
        sy.wait(sDT, sDT.v)
        for t in range(nt):
            b = t % 2
            sb = sX[b]
            gp.wait(sDT, sDT.v)
            inc(gp.e.dma_start(out=XTM[b][:, :], in_=src2d[t * 128:(t + 1) * 128, :]), sb, 16)
            sy.wait(sb, sb.v)
            inc(sy.e.dma_start_transpose(
                out=dst3[:, :, coff + t * 128:coff + (t + 1) * 128], in_=XTM[b][:, :]), sDT, 16)
            xtm_free[b] = sDT.v

    def rstd_from_stats(n, rows=2):
        ve.wait(sVE, sVE.v)
        ve.wait(sAC, sAC.v)
        inc(ve.e.tensor_tensor(st_t[0:rows, :n], st_m[0:rows, :n], st_m[0:rows, :n],
                               op=OP.mult), sVE, 1)
        ve.wait(sVE, sVE.v)
        inc(ve.e.tensor_tensor(st_t[0:rows, :n], st_s[0:rows, :n], st_t[0:rows, :n],
                               op=OP.subtract), sVE, 1)
        ac.wait(sVE, sVE.v)
        ac.wait(sPE, sPE.v)
        inc(ac.e.activation(st_r[0:rows, :n], st_t[0:rows, :n], AF.Ln,
                            bias=c_eps2[0:rows, :]), sAC, 1)
        ac.wait(sAC, sAC.v)
        inc(ac.e.activation(st_r[0:rows, :n], st_r[0:rows, :n], AF.Exp,
                            scale=np.float32(-0.5)), sAC, 1)
        return sAC.v

    def postln(XRES, c0, ct):
        nqb = max(1, ct // 512)
        bw = min(512, ct)
        for qb in range(nqb):
            cs = c0 + qb * bw
            lo = qb * bw
            guard("A0")
            guard("A1")
            for j in range(NCH):
                src = XRES[:, j, cs:cs + bw]
                ve.wait(sPE, sPE.v)
                inc(ve.e.tensor_tensor(tmpa[:, :bw], src, src, op=OP.mult), sVE, 1)
                pe.wait(sVE, sVE.v)
                nc.tensor.matmul(bA(0, bw, 0, 1), c_onesc[:, :], src,
                                 start=(j == 0), stop=(j == NCH - 1))
                mm = nc.tensor.matmul(bA(1, bw, 0, 1), c_onesc[:, :], tmpa[:, :bw],
                                      start=(j == 0), stop=(j == NCH - 1))
                inc(mm, sPE, 1)
            ve.wait(sPE, sPE.v)
            ve.wait(sAC, sAC.v)
            inc(ve.e.tensor_scalar(st_m[0:1, lo:lo + bw], bA(0, bw, 0, 1),
                                   c_iD[0:1, :], None, op0=OP.mult), sVE, 1)
            inc(ve.e.tensor_scalar(st_s[0:1, lo:lo + bw], bA(1, bw, 0, 1),
                                   c_iD[0:1, :], None, op0=OP.mult), sVE, 1)
            release("A0", sVE)
            release("A1", sVE)
        rstd_at = rstd_from_stats(ct, rows=1)
        for qb in range(nqb):
            cs = c0 + qb * bw
            lo = qb * bw
            guard("A2")
            guard("A3")
            pe.wait(sAC, rstd_at)
            inc(nc.tensor.matmul(bA(2, bw), c_ones1[:, :], st_m[0:1, lo:lo + bw],
                                 start=True, stop=True), sPE, 1)
            inc(nc.tensor.matmul(bA(3, bw), c_ones1[:, :], st_r[0:1, lo:lo + bw],
                                 start=True, stop=True), sPE, 1)
            ve.wait(sPE, sPE.v)
            for j in range(NCH):
                src = XRES[:, j, cs:cs + bw]
                inc(ve.e.scalar_tensor_tensor(tmpa[:, :bw], src, c_zero[:, :], bA(2, bw),
                                              op0=OP.add, op1=OP.subtract), sVE, 1)
                ve.wait(sVE, sVE.v)
                inc(ve.e.tensor_tensor(src, tmpa[:, :bw], bA(3, bw), op=OP.mult), sVE, 1)
                ve.wait(sVE, sVE.v)
                inc(ve.e.tensor_copy(X1BF[:, j, cs:cs + bw], src), sVE, 1)
            release("A2", sVE)
            release("A3", sVE)

    def kqstats(buf_slice, m_dst, s_dst):
        ve.wait(sPE, sPE.v)
        inc(ve.e.tensor_tensor(stg[:, :], buf_slice, buf_slice, op=OP.mult), sVE, 1)
        guard("B2")
        guard("B3")
        pe.wait(sVE, sVE.v)
        nc.tensor.matmul(bB(2, 512, 0, 2), c_e2bf[:, :], buf_slice, start=True, stop=True)
        inc(nc.tensor.matmul(bB(3, 512, 0, 2), c_e2bf[:, :], stg[:, :],
                             start=True, stop=True), sPE, 1)
        ve.wait(sPE, sPE.v)
        inc(ve.e.tensor_scalar(m_dst, bB(2, 512, 0, 2), c_i64[:, :], None, op0=OP.mult),
            sVE, 1)
        inc(ve.e.tensor_scalar(s_dst, bB(3, 512, 0, 2), c_i64[:, :], None, op0=OP.mult),
            sVE, 1)
        release("B2", sVE)
        release("B3", sVE)

    def lnapply(buf3, cl, c0, n, rstd_at, gv, bv):
        for cb_ in range(n // 512):
            col = c0 + cb_ * 512
            guard("A2")
            guard("A3")
            pe.wait(sAC, rstd_at)
            pe.wait(sVE, sVE.v)
            inc(nc.tensor.matmul(bA(2), c_e2tf[:, :], st_m[0:2, cb_ * 512:(cb_ + 1) * 512],
                                 start=True, stop=True), sPE, 1)
            inc(nc.tensor.matmul(bA(3), c_e2tf[:, :], st_r[0:2, cb_ * 512:(cb_ + 1) * 512],
                                 start=True, stop=True), sPE, 1)
            dst = buf3[:, cl, col:col + 512] if cl is not None else buf3[:, col:col + 512]
            ve.wait(sPE, sPE.v)
            inc(ve.e.scalar_tensor_tensor(tmpa[:, :], dst, c_zero[:, :], bA(2),
                                          op0=OP.add, op1=OP.subtract), sVE, 1)
            ve.wait(sVE, sVE.v)
            inc(ve.e.tensor_tensor(tmpb[:, :], tmpa[:, :], bA(3), op=OP.mult), sVE, 1)
            ve.wait(sVE, sVE.v)
            inc(ve.e.tensor_scalar(dst, tmpb[:, :], gv[:, :], bv[:, :],
                                   op0=OP.mult, op1=OP.add), sVE, 1)
            release("A2", sVE)
            release("A3", sVE)

    sy.wait(sDL, consts_d)
    with ExitStack() as l0s:
        L = l0s.enter_context
        XQBF = L(nc.sbuf_tensor("XQBF", [128, NCH, T], BF16))
        XB0 = L(nc.sbuf_tensor("XB0", [128, NCH, 1024], BF16))
        XB1 = L(nc.sbuf_tensor("XB1", [128, NCH, 1024], BF16))
        XB = [XB0, XB1]
        xb_at = [0, 0]
        KH = L(nc.sbuf_tensor("KH", [128, S], BF16))
        VT = L(nc.sbuf_tensor("VT", [128, NKT * 128], BF16))
        QH = L(nc.sbuf_tensor("QH", [128, T], BF16))
        PB = [L(nc.sbuf_tensor(f"PB{i}", [128, 4096], BF16)) for i in range(2)]
        PT = [L(nc.sbuf_tensor(f"PT{i}", [128, NKT, 128], BF16)) for i in range(2)]
        wq_g = L(nc.sbuf_tensor("wq_g", [128, NCH, 128], BF16))
        wk_g = L(nc.sbuf_tensor("wk_g", [128, NCH, 128], BF16))
        wv_g = L(nc.sbuf_tensor("wv_g", [128, NCH, 128], BF16))

        load_fm(XQBF, xq, 16)
        xq_fm_at = sDT.v
        pb_free = 0
        pt_free = 0

        for g in range(8):
            gp.wait(sPE, sPE.v)
            for dst, srcw in [(wq_g, l0_wqT), (wk_g, l0_wkT), (wv_g, l0_wvT)]:
                wload(gp, sGP, dst, srcw, 8, g * 128, 128)
            wg_at = sGP.v

            if g == 0:
                sy.wait(sPE, sPE.v)
                load_fm(XB[0], xb[0:1024, :], 8)
                xb_at[0] = sDT.v
            for half in range(2):
                for qq in range(2):
                    qr = half * 2 + qq
                    nq = qr + 1
                    sy.wait(sPE, sPE.v)
                    if nq < 4:
                        load_fm(XB[nq % 2], xb[nq * 1024:(nq + 1) * 1024, :], 8)
                        xb_at[nq % 2] = sDT.v
                    elif g < 7:
                        load_fm(XB[0], xb[0:1024, :], 8)
                        xb_at[0] = sDT.v
                    XBF = XB[qr % 2]
                    for tb2 in range(2):
                        tb = qq * 2 + tb2
                        key = f"A{tb % 2}"
                        guard(key)
                        pe.wait(sGP, wg_at)
                        pe.wait(sDT, xb_at[qr % 2])
                        for j in range(NCH):
                            mm = nc.tensor.matmul(bA(tb % 2), wk_g[:, j, :],
                                                  XBF[:, j, tb2 * 512:(tb2 + 1) * 512],
                                                  start=(j == 0), stop=(j == NCH - 1))
                        inc(mm, sPE, 1)
                        khs = KH[:, half * T + tb * 512:half * T + (tb + 1) * 512]
                        ve.wait(sPE, sPE.v)
                        inc(ve.e.tensor_copy(khs, bA(tb % 2)), sVE, 1)
                        release(key, sVE)
                        ve.wait(sVE, sVE.v)
                        kqstats(khs, st_m[:, tb * 512:(tb + 1) * 512],
                                st_s[:, tb * 512:(tb + 1) * 512])
                    for kt2 in range(8):
                        kt = qq * 8 + kt2
                        key = f"B{kt % 2}"
                        guard(key)
                        for j in range(NCH):
                            mm = nc.tensor.matmul(bB(kt % 2, 128),
                                                  XBF[:, j, kt2 * 128:(kt2 + 1) * 128],
                                                  wv_g[:, j, :], start=(j == 0),
                                                  stop=(j == NCH - 1))
                        inc(mm, sPE, 1)
                        gkt = half * 16 + kt
                        ve.wait(sPE, sPE.v)
                        inc(ve.e.tensor_copy(VT[:, gkt * 128:(gkt + 1) * 128],
                                             bB(kt % 2, 128)), sVE, 1)
                        release(key, sVE)
                r_at = rstd_from_stats(T)
                lnapply(KH, None, half * T, T, r_at, c_gk, c_bk)
            for tb in range(4):
                key = f"A{tb % 2}"
                guard(key)
                pe.wait(sDT, xq_fm_at)
                for j in range(NCH):
                    mm = nc.tensor.matmul(bA(tb % 2), wq_g[:, j, :],
                                          XQBF[:, j, tb * 512:(tb + 1) * 512],
                                          start=(j == 0), stop=(j == NCH - 1))
                inc(mm, sPE, 1)
                qhs = QH[:, tb * 512:(tb + 1) * 512]
                ve.wait(sPE, sPE.v)
                inc(ve.e.tensor_copy(qhs, bA(tb % 2)), sVE, 1)
                release(key, sVE)
                ve.wait(sVE, sVE.v)
                kqstats(qhs, st_m[:, tb * 512:(tb + 1) * 512],
                        st_s[:, tb * 512:(tb + 1) * 512])
            r_at = rstd_from_stats(T)
            lnapply(QH, None, 0, T, r_at, c_gq, c_bq)
            lnk_at = sVE.v
            for qt in range(NQT):
                nk = 2 * qt + 2
                qlh = QH[0:64, qt * 128:(qt + 1) * 128]
                qhh = QH[64:128, qt * 128:(qt + 1) * 128]
                ca = (nk + 3) // 4
                for c in range(ca):
                    w = min(4, nk - 4 * c) * 128
                    k0 = c * 512
                    b1_, b2_ = c % 2, 2 + c % 2
                    guard(f"A{b1_}")
                    guard(f"A{b2_}")
                    pe.wait(sVE, lnk_at)
                    nc.tensor.matmul(bA(b1_, w), qlh, KH[0:64, k0:k0 + w],
                                     start=True, stop=True)
                    inc(nc.tensor.matmul(bA(b2_, w), qhh, KH[64:128, k0:k0 + w],
                                         start=True, stop=True, tile_position=(64, 0)),
                        sPE, 1)
                    ve.wait(sPE, sPE.v)
                    if c == ca - 1:
                        for bi in (b1_, b2_):
                            inc(ve.e.tensor_tensor(bA(bi, 512)[:, w - 256:w - 128],
                                                   bA(bi, 512)[:, w - 256:w - 128],
                                                   c_maska[:, :], op=OP.add), sVE, 1)
                            inc(ve.e.tensor_tensor(bA(bi, 512)[:, w - 128:w],
                                                   bA(bi, 512)[:, w - 128:w],
                                                   c_maskb[:, :], op=OP.add), sVE, 1)
                        ve.wait(sVE, sVE.v)
                    inc(ve.e.tensor_reduce(amax[:, c:c + 1], bA(b1_, w), axis=AX.X,
                                           op=OP.max), sVE, 1)
                    inc(ve.e.tensor_reduce(amax[:, 8 + c:9 + c], bA(b2_, w), axis=AX.X,
                                           op=OP.max), sVE, 1)
                    release(f"A{b1_}", sVE)
                    release(f"A{b2_}", sVE)
                ve.wait(sVE, sVE.v)
                ve.wait(sAC, sAC.v)
                inc(ve.e.tensor_reduce(negm2[:, 0:1], amax[:, 0:ca], axis=AX.X,
                                       op=OP.max, negate=True), sVE, 1)
                inc(ve.e.tensor_reduce(negm2[:, 1:2], amax[:, 8:8 + ca], axis=AX.X,
                                       op=OP.max, negate=True), sVE, 1)
                negm_at = sVE.v
                cb = (nk + 7) // 8
                for c in range(cb):
                    w = min(8, nk - 8 * c) * 128
                    k0 = c * 1024
                    PSX = PS_A if c % 2 == 0 else PS_B
                    kyb = "A" if c % 2 == 0 else "B"
                    for bi in range(4):
                        guard(f"{kyb}{bi}")
                    for sub in range((w + 511) // 512):
                        ws = min(512, w - sub * 512)
                        o_ = sub * 512
                        nc.tensor.matmul(PSX[:, o_:o_ + ws], qlh,
                                         KH[0:64, k0 + o_:k0 + o_ + ws],
                                         start=True, stop=True)
                        mm = nc.tensor.matmul(PSX[:, 1024 + o_:1024 + o_ + ws], qhh,
                                              KH[64:128, k0 + o_:k0 + o_ + ws],
                                              start=True, stop=True, tile_position=(64, 0))
                    inc(mm, sPE, 1)
                    if c == cb - 1:
                        ve.wait(sPE, sPE.v)
                        for off in (0, 1024):
                            inc(ve.e.tensor_tensor(PSX[:, off + w - 256:off + w - 128],
                                                   PSX[:, off + w - 256:off + w - 128],
                                                   c_maska[:, :], op=OP.add), sVE, 1)
                            inc(ve.e.tensor_tensor(PSX[:, off + w - 128:off + w],
                                                   PSX[:, off + w - 128:off + w],
                                                   c_maskb[:, :], op=OP.add), sVE, 1)
                    ac.wait(sPE, sPE.v)
                    ac.wait(sVE, sVE.v)
                    ac.wait(sDT, pb_free)
                    inc(ac.e.activation(PB[0][:, k0:k0 + w], PSX[:, 0:w], AF.Exp,
                                        bias=negm2[:, 0:1], accum_out=ellp[:, c:c + 1]),
                        sAC, 1)
                    inc(ac.e.activation(PB[1][:, k0:k0 + w], PSX[:, 1024:1024 + w], AF.Exp,
                                        bias=negm2[:, 1:2], accum_out=ellp[:, 4 + c:5 + c]),
                        sAC, 1)
                    for bi in range(4):
                        release(f"{kyb}{bi}", sAC)
                exp_at = sAC.v
                ve.wait(sAC, exp_at)
                ve.wait(sPE, sPE.v)
                inc(ve.e.tensor_reduce(recl2[:, 0:1], ellp[:, 0:cb], axis=AX.X, op=OP.add),
                    sVE, 1)
                inc(ve.e.tensor_reduce(recl2[:, 1:2], ellp[:, 4:4 + cb], axis=AX.X,
                                       op=OP.add), sVE, 1)
                ve.wait(sVE, sVE.v)
                inc(ve.e.reciprocal(recl2[:, 0:2], recl2[:, 0:2]), sVE, 1)
                recl_at = sVE.v
                sy.wait(sAC, exp_at)
                sy.wait(sPE, pt_free)
                sy.wait(sDT, sDT.v)
                inc(sy.e.dma_start_transpose(out=PT[0][:, 0:nk, :],
                                             in_=PB[0][:, 0:nk * 128]), sDT, 16)
                inc(sy.e.dma_start_transpose(out=PT[1][:, 0:nk, :],
                                             in_=PB[1][:, 0:nk * 128]), sDT, 16)
                pb_free = sDT.v
                guard("B1")
                pe.wait(sVE, recl_at)
                inc(nc.tensor.transpose(PS_B[0:2, 512:640], recl2[:, 0:2], c_ident[:, :]),
                    sPE, 1)
                ve.wait(sPE, sPE.v)
                inc(ve.e.tensor_copy(reclT[:, :], PS_B[0:2, 512:640]), sVE, 1)
                release("B1", sVE)
                guard("B2")
                pe.wait(sVE, sVE.v)
                inc(nc.tensor.matmul(bB(2, 128), c_e2tf[:, :], reclT[:, :],
                                     start=True, stop=True), sPE, 1)
                ve.wait(sPE, sPE.v)
                inc(ve.e.tensor_copy(mtile[:, 0:128], bB(2, 128)), sVE, 1)
                release("B2", sVE)
                guard("B0")
                pe.wait(sDT, sDT.v)
                for kt in range(nk):
                    base = kt * 128
                    nc.tensor.matmul(bB(0, 128, 0, 64), VT[:, base:base + 64],
                                     PT[0][:, kt, :], start=(kt == 0), stop=(kt == nk - 1),
                                     skip_group_check=True)
                    mm = nc.tensor.matmul(bB(0, 128, 64, 128), VT[:, base + 64:base + 128],
                                          PT[1][:, kt, :], start=(kt == 0),
                                          stop=(kt == nk - 1), tile_position=(0, 64),
                                          skip_group_check=True)
                inc(mm, sPE, 1)
                pt_free = sPE.v
                ve.wait(sVE, sVE.v)
                ve.wait(sPE, sPE.v)
                inc(ve.e.tensor_tensor(X1BF[:, g, qt * 128:(qt + 1) * 128],
                                       bB(0, 128), mtile[:, 0:128], op=OP.mult), sVE, 1)
                release("B0", sVE)

    XRES = E(nc.sbuf_tensor("XRES", [128, NCH, T], F32))
    wo_stack = ExitStack()
    XTF = wo_stack.enter_context(nc.sbuf_tensor("XTF", [128, 1024], F32))
    wsl = wo_stack.enter_context(nc.sbuf_tensor("wsl", [128, NCH, 256], BF16))
    fence(ve, pe, sy, gp, ac)

    for t in range(NQT):
        sy.wait(sVE, sVE.v)
        sy.wait(sDL, sDL.v)
        inc(sy.e.dma_start(out=XTF[:, :], in_=xq[t * 128:(t + 1) * 128, :]), sDL, 16)
        pe.wait(sDL, sDL.v)
        for j in range(NCH):
            key = f"B{j % 2}"
            guard(key)
            inc(nc.tensor.transpose(bB(j % 2, 128), XTF[:, j * 128:(j + 1) * 128],
                                    c_ident[:, :]), sPE, 1)
            ve.wait(sPE, sPE.v)
            inc(ve.e.tensor_copy(XRES[:, j, t * 128:(t + 1) * 128], bB(j % 2, 128)), sVE, 1)
            release(key, sVE)
    for g4 in range(4):
        gp.wait(sPE, sPE.v)
        wload(gp, sGP, wsl, l0_woT, 8, g4 * 256, 256)
        wo_at = sGP.v
        for jj in range(2):
            j = g4 * 2 + jj
            for qb in range(4):
                key = f"A{qb % 4}"
                guard(key)
                pe.wait(sGP, wo_at)
                pe.wait(sVE, sVE.v)
                for dch in range(NCH):
                    mm = nc.tensor.matmul(bA(qb % 4), wsl[:, dch, jj * 128:(jj + 1) * 128],
                                          X1BF[:, dch, qb * 512:(qb + 1) * 512],
                                          start=(dch == 0), stop=(dch == NCH - 1))
                inc(mm, sPE, 1)
                dst = XRES[:, j, qb * 512:(qb + 1) * 512]
                ve.wait(sPE, sPE.v)
                inc(ve.e.scalar_tensor_tensor(dst, bA(qb % 4), c_bo[:, j:j + 1], dst,
                                              op0=OP.add, op1=OP.add), sVE, 1)
                release(key, sVE)
    postln(XRES, 0, T)
    wo_stack.close()
    if debug:
        sy.wait(sVE, sVE.v)
        inc(sy.e.dma_start(out=dbgo["d_x1"], in_=XRES[:, :, :]), sST, 16)

    with ExitStack() as f0s:
        L = f0s.enter_context
        u1b = L(nc.sbuf_tensor("u1b", [128, NCH, 256], BF16))
        v1b = L(nc.sbuf_tensor("v1b", [128, 2, 4096], BF16))
        u2b = L(nc.sbuf_tensor("u2b", [128, 32, 256], BF16))
        v2b = L(nc.sbuf_tensor("v2b", [128, 2, 1024], BF16))
        pqb = L(nc.sbuf_tensor("pqb", [128, 2, 512], BF16))
        qqb = L(nc.sbuf_tensor("qqb", [128, 2, 512], BF16))
        fence(gp)
        wload(gp, sGP, u1b, l0_u1, 8, 0, 256)
        wload(gp, sGP, v1b, l0_v1, 2, 0, 4096)
        wload(gp, sGP, u2b, l0_u2, 32, 0, 256)
        wload(gp, sGP, v2b, l0_v2, 2, 0, 1024)
        ffw_at = sGP.v
        for tb in range(4):
            tc = tb * 512
            for rc in range(2):
                key = f"A{rc}"
                guard(key)
                pe.wait(sGP, ffw_at)
                for j in range(NCH):
                    mm = nc.tensor.matmul(bA(rc), u1b[:, j, rc * 128:(rc + 1) * 128],
                                          X1BF[:, j, tc:tc + 512],
                                          start=(j == 0), stop=(j == NCH - 1))
                inc(mm, sPE, 1)
                ve.wait(sPE, sPE.v)
                inc(ve.e.tensor_copy(pqb[:, rc, :], bA(rc)), sVE, 1)
                release(key, sVE)
            guard("B0")
            guard("B1")
            p_at = sVE.v
            for cc in range(32):
                key = f"A{2 + cc % 2}"
                guard(key)
                pe.wait(sVE, p_at)
                for rc in range(2):
                    mm = nc.tensor.matmul(bA(2 + cc % 2), v1b[:, rc, cc * 128:(cc + 1) * 128],
                                          pqb[:, rc, :], start=(rc == 0), stop=(rc == 1))
                inc(mm, sPE, 1)
                hbuf = stg if cc % 2 == 0 else stg2
                hkey = f"H{cc % 2}"
                ac.wait(sPE, sPE.v)
                if hkey in bg:
                    ac.wait(bg[hkey][0], bg[hkey][1])
                inc(ac.e.activation(hbuf[:, :], bA(2 + cc % 2), AF.Gelu_apprx_tanh,
                                    bias=c_b1[:, cc:cc + 1]), sAC, 1)
                release(key, sAC)
                pe.wait(sAC, sAC.v)
                for rc in range(2):
                    mm = nc.tensor.matmul(bB(rc), u2b[:, cc, rc * 128:(rc + 1) * 128],
                                          hbuf[:, :], start=(cc == 0), stop=(cc == 31))
                inc(mm, sPE, 1)
                bg[hkey] = (sPE, sPE.v)
            ve.wait(sPE, sPE.v)
            inc(ve.e.tensor_copy(qqb[:, 0, :], bB(0)), sVE, 1)
            inc(ve.e.tensor_copy(qqb[:, 1, :], bB(1)), sVE, 1)
            release("B0", sVE)
            release("B1", sVE)
            q_at = sVE.v
            for j in range(NCH):
                key = f"A{j % 2}"
                guard(key)
                pe.wait(sVE, q_at)
                for rc in range(2):
                    mm = nc.tensor.matmul(bA(j % 2), v2b[:, rc, j * 128:(j + 1) * 128],
                                          qqb[:, rc, :], start=(rc == 0), stop=(rc == 1))
                inc(mm, sPE, 1)
                dst = XRES[:, j, tc:tc + 512]
                ve.wait(sPE, sPE.v)
                inc(ve.e.scalar_tensor_tensor(dst, bA(j % 2), c_b2[:, j:j + 1], dst,
                                              op0=OP.add, op1=OP.add), sVE, 1)
                release(key, sVE)
        postln(XRES, 0, T)
        if debug:
            sy.wait(sVE, sVE.v)
            inc(sy.e.dma_start(out=dbgo["d_x2"], in_=XRES[:, :, :]), sST, 16)

    with ExitStack() as l1s:
        L = l1s.enter_context
        wq1 = L(nc.sbuf_tensor("wq1", [128, NCH, 512], BF16))
        wk1 = L(nc.sbuf_tensor("wk1", [128, NCH, 128], BF16))
        wv1 = L(nc.sbuf_tensor("wv1", [128, NCH, 64], BF16))
        K2 = L(nc.sbuf_tensor("K2", [128, S], BF16))
        VT1 = L(nc.sbuf_tensor("VT1", [128, NKT * VW], BF16))
        QG = L(nc.sbuf_tensor("QG", [128, NCH, 256], BF16))
        PT0 = L(nc.sbuf_tensor("PX0", [128, NKT * 256], BF16))
        PT1 = L(nc.sbuf_tensor("PX1", [128, NKT * 256], BF16))
        OB1 = L(nc.sbuf_tensor("OB1", [128, NCH, 256], BF16))
        fence(gp)
        wload(gp, sGP, wk1, l1_wkT2, 8, 0, 128)
        wload(gp, sGP, wv1, l1_wvT, 8, 0, 64)
        w1_at = sGP.v
        for tb in range(4):
            key = f"A{tb % 2}"
            guard(key)
            pe.wait(sGP, w1_at)
            for j in range(NCH):
                mm = nc.tensor.matmul(bA(tb % 2), wk1[:, j, :],
                                      X1BF[:, j, tb * 512:(tb + 1) * 512],
                                      start=(j == 0), stop=(j == NCH - 1))
            inc(mm, sPE, 1)
            ve.wait(sPE, sPE.v)
            inc(ve.e.tensor_copy(K2[:, tb * 512:(tb + 1) * 512], bA(tb % 2)), sVE, 1)
            release(key, sVE)
        for kt in range(16):
            key = f"B{kt % 2}"
            guard(key)
            for j in range(NCH):
                mm = nc.tensor.matmul(bB(kt % 2, 64), X1BF[:, j, kt * 128:(kt + 1) * 128],
                                      wv1[:, j, :], start=(j == 0), stop=(j == NCH - 1))
            inc(mm, sPE, 1)
            ve.wait(sPE, sPE.v)
            inc(ve.e.tensor_copy(VT1[:, kt * VW:kt * VW + 64], bB(kt % 2, 64)), sVE, 1)
            release(key, sVE)
            inc(ve.e.memset(VT1[:, kt * VW + 64:kt * VW + 65], 1.0), sVE, 1)
        kv_at = sVE.v
        gp.wait(sVE, kv_at)
        inc(gp.e.dma_start(out=cc_kin.ap(), in_=K2[0:64, 0:T]), sGP, 16)
        gp.wait(sGP, sGP.v)
        inc(gp.e.collective_compute(
            "AllGather", OP.bypass, replica_groups=pairs,
            ins=[cc_kin.ap().opt()], outs=[cc_kout.ap().opt()]), sCC, 1)
        gp.wait(sCC, sCC.v)
        inc(gp.e.dma_start(out=cc_vin.ap(), in_=VT1[:, 0:16 * VW]), sGP, 16)
        gp.wait(sGP, sGP.v)
        inc(gp.e.collective_compute(
            "AllGather", OP.bypass, replica_groups=pairs,
            ins=[cc_vin.ap().opt()], outs=[cc_vout.ap().opt()]), sCC, 1)
        gp.wait(sCC, sCC.v)
        inc(gp.e.dma_start(out=K2[0:64, 0:T], in_=cc_kout.ap()[0:64, :]), sGP, 16)
        gp.wait(sGP, sGP.v)
        inc(gp.e.dma_start(out=K2[0:64, T:S], in_=cc_kout.ap()[64:128, :]), sGP, 16)
        gp.wait(sGP, sGP.v)
        inc(gp.e.dma_start(out=VT1[:, 0:16 * VW], in_=cc_vout.ap()[0:128, :]), sGP, 16)
        gp.wait(sGP, sGP.v)
        inc(gp.e.dma_start(out=VT1[:, 16 * VW:32 * VW], in_=cc_vout.ap()[128:256, :]), sGP, 16)
        gp.wait(sGP, sGP.v)
        inc(gp.e.dma_start(out=K2[64:128, 0:S], in_=K2[0:64, 0:S]), sGP, 16)
        kg_at = sGP.v
        for cb2 in range(8):
            sl = K2[0:64, cb2 * 512:(cb2 + 1) * 512]
            ve.wait(sGP, kg_at)
            ve.wait(sPE, sPE.v)
            inc(ve.e.tensor_tensor(stg[0:64, :], sl, sl, op=OP.mult), sVE, 1)
            guard("B3")
            pe.wait(sVE, sVE.v)
            inc(nc.tensor.matmul(bB(3, 512, 0, 1), c_o64[:, :], stg[0:64, :],
                                 start=True, stop=True), sPE, 1)
            ve.wait(sPE, sPE.v)
            if cb2 == 0:
                inc(ve.e.tensor_reduce(kmax2[:, :], bB(3, 512, 0, 1), axis=AX.X, op=OP.max),
                    sVE, 1)
            else:
                inc(ve.e.tensor_reduce(kmtmp[:, :], bB(3, 512, 0, 1), axis=AX.X, op=OP.max),
                    sVE, 1)
                ve.wait(sVE, sVE.v)
                inc(ve.e.tensor_tensor(kmax2[:, :], kmax2[:, :], kmtmp[:, :], op=OP.max),
                    sVE, 1)
            release("B3", sVE)
        kmax_at = sVE.v
        for qg in range(8):
            qc = qg * 256
            for j in range(NCH):
                if j % 4 == 0:
                    hh = j // 4
                    gp.wait(sPE, sPE.v)
                    wload(gp, sGP, wq1, l1_wqT, 8, hh * 512, 512)
                    wq_at = sGP.v
                key = f"A{j % 2}"
                guard(key)
                pe.wait(sGP, wq_at)
                for dch in range(NCH):
                    mm = nc.tensor.matmul(bA(j % 2, 256),
                                          wq1[:, dch, (j % 4) * 128:(j % 4 + 1) * 128],
                                          X1BF[:, dch, qc:qc + 256],
                                          start=(dch == 0), stop=(dch == NCH - 1))
                inc(mm, sPE, 1)
                ve.wait(sPE, sPE.v)
                inc(ve.e.tensor_copy(QG[:, j, :], bA(j % 2, 256)), sVE, 1)
                release(key, sVE)
                qsl = QG[:, j, :]
                ve.wait(sVE, sVE.v)
                ve.wait(sPE, sPE.v)
                inc(ve.e.tensor_tensor(stg[:, 0:256], qsl, qsl, op=OP.mult), sVE, 1)
                guard("B3")
                pe.wait(sVE, sVE.v)
                inc(nc.tensor.matmul(bB(3, 256, 0, 1), c_oc128[:, :], stg[:, 0:256],
                                     start=True, stop=True), sPE, 1)
                ve.wait(sPE, sPE.v)
                if j == 0:
                    inc(ve.e.tensor_reduce(qmaxs[:, :], bB(3, 256, 0, 1), axis=AX.X,
                                           op=OP.max), sVE, 1)
                else:
                    inc(ve.e.tensor_reduce(qmtmp[:, :], bB(3, 256, 0, 1), axis=AX.X,
                                           op=OP.max), sVE, 1)
                    ve.wait(sVE, sVE.v)
                    inc(ve.e.tensor_tensor(qmaxs[:, :], qmaxs[:, :], qmtmp[:, :],
                                           op=OP.max), sVE, 1)
                release("B3", sVE)
            ve.wait(sVE, sVE.v)
            ve.wait(sVE, kmax_at)
            inc(ve.e.tensor_tensor(bsc[:, :], qmaxs[:, :], kmax2[:, :], op=OP.mult), sVE, 1)
            ac.wait(sVE, sVE.v)
            inc(ac.e.activation(bsc[:, :], bsc[:, :], AF.Ln), sAC, 1)
            ac.wait(sAC, sAC.v)
            inc(ac.e.activation(bsc[:, :], bsc[:, :], AF.Exp, scale=np.float32(0.5),
                                bias=np.float32(-2.0794415416798357)), sAC, 1)
            ve.wait(sAC, sAC.v)
            ve.wait(sPE, sPE.v)
            inc(ve.e.tensor_scalar(rowb[:, :], c_ones1[:, :], bsc[0:1, :], None,
                                   op0=OP.mult), sVE, 1)
            guard("B3")
            pe.wait(sVE, sVE.v)
            inc(nc.tensor.transpose(PS_B[0:128, 1536:1537], rowb[:, :], c_ones1[0:1, 0:1]),
                sPE, 1)
            ve.wait(sPE, sPE.v)
            inc(ve.e.scalar_tensor_tensor(negB[:, :], c_zero[:, :], c_zero[:, :],
                                          PS_B[0:128, 1536:1537],
                                          op0=OP.add, op1=OP.subtract), sVE, 1)
            release("B3", sVE)
            negb_at = sVE.v
            for pr in range(NCH):
                q_lo = QG[0:64, pr, :]
                q_hi = QG[64:128, pr, :]
                for ktg in range(8):
                    for bi in range(4):
                        guard(f"A{bi}")
                    pe.wait(sGP, kg_at)
                    pe.wait(sVE, negb_at)
                    for kk in range(4):
                        kt = ktg * 4 + kk
                        nc.tensor.matmul(PS_A[:, kk * 256:kk * 256 + 256],
                                         K2[0:64, kt * 128:(kt + 1) * 128], q_lo,
                                         start=True, stop=True)
                        mm = nc.tensor.matmul(PS_A[:, 1024 + kk * 256:1024 + kk * 256 + 256],
                                              K2[64:128, kt * 128:(kt + 1) * 128], q_hi,
                                              start=True, stop=True, tile_position=(64, 0))
                    inc(mm, sPE, 1)
                    ac.wait(sPE, sPE.v)
                    ac.wait(sVE, negb_at)
                    inc(ac.e.activation(PT0[:, ktg * 1024:(ktg + 1) * 1024], PS_A[:, 0:1024],
                                        AF.Exp, bias=negB[:, :], scale=np.float32(0.125)),
                        sAC, 1)
                    inc(ac.e.activation(PT1[:, ktg * 1024:(ktg + 1) * 1024],
                                        PS_A[:, 1024:2048], AF.Exp, bias=negB[:, :],
                                        scale=np.float32(0.125)), sAC, 1)
                    for bi in range(4):
                        release(f"A{bi}", sAC)
                av_at = [0, 0]
                for hl, PTx in ((0, PT0), (1, PT1)):
                    guard(f"B{hl}")
                    pe.wait(sAC, sAC.v)
                    for kt in range(NKT):
                        mm = nc.tensor.matmul(bB(hl, 256, 0, 65), VT1[:, kt * VW:kt * VW + 65],
                                              PTx[:, kt * 256:(kt + 1) * 256],
                                              start=(kt == 0), stop=(kt == NKT - 1))
                    inc(mm, sPE, 1)
                    av_at[hl] = sPE.v
                rc_at = [0, 0]
                for hl, mt in ((0, mtile), (1, mtil2)):
                    ve.wait(sPE, av_at[hl])
                    inc(ve.e.reciprocal(mt[0:1, 0:256],
                                        PS_B[64:65, hl * 512:hl * 512 + 256]), sVE, 1)
                    rc_at[hl] = sVE.v
                for hl, mt in ((0, mtile), (1, mtil2)):
                    guard(f"B{2 + hl}")
                    pe.wait(sVE, rc_at[hl])
                    inc(nc.tensor.matmul(bB(2 + hl, 256), c_ones1[:, :], mt[0:1, 0:256],
                                         start=True, stop=True), sPE, 1)
                    ve.wait(sPE, sPE.v)
                    inc(ve.e.tensor_copy(mt[:, 0:256], bB(2 + hl, 256)), sVE, 1)
                    release(f"B{2 + hl}", sVE)
                for hl, mt in ((0, mtile), (1, mtil2)):
                    ve.wait(sVE, sVE.v)
                    inc(ve.e.tensor_tensor(OB1[hl * 64:(hl + 1) * 64, pr, :],
                                           PS_B[0:64, hl * 512:hl * 512 + 256],
                                           mt[0:64, 0:256], op=OP.mult), sVE, 1)
                    release(f"B{hl}", sVE)
            gbar()
            fence(ve, ac, pe)
            for j in range(NCH):
                dst = XRES[:, j, qc:qc + 256]
                ve.wait(sVE, sVE.v)
                inc(ve.e.scalar_tensor_tensor(dst, OB1[:, j, :], c_zero[:, :], dst,
                                              op0=OP.add, op1=OP.add), sVE, 1)
            postln(XRES, qc, 256)
        if debug:
            sy.wait(sVE, sVE.v)
            inc(sy.e.dma_start(out=dbgo["d_x3"], in_=XRES[:, :, :]), sST, 16)

    with ExitStack() as f1s:
        L = f1s.enter_context
        T1b = L(nc.sbuf_tensor("T1b", [128, NCH, 256], BF16))
        T2b = L(nc.sbuf_tensor("T2b", [128, NCH, 256], BF16))
        wrb = L(nc.sbuf_tensor("wrb", [128, NCH, 512], BF16))
        wib = L(nc.sbuf_tensor("wib", [128, NCH, 512], BF16))
        csl = L(nc.sbuf_tensor("csl", [128, NCH, 256], BF16))
        ssl = L(nc.sbuf_tensor("ssl", [128, NCH, 256], BF16))
        snl = L(nc.sbuf_tensor("snl", [128, NCH, 256], BF16))
        cdo = L(nc.sbuf_tensor("cdo", [128, NCH, 128], BF16))
        sdo = L(nc.sbuf_tensor("sdo", [128, NCH, 128], BF16))
        WEF = L(nc.sbuf_tensor("WEF", [128, NCH, 256], BF16))
        fence(gp, sy)
        for cs in range(4):
            sy.wait(sPE, sPE.v)
            for dst, src_ in ((csl, cCb), (ssl, cSb), (snl, cSnb)):
                wload(sy, sDL, dst, src_, 8, cs * 256, 256)
            cs_at = sDL.v
            for hh in range(2):
                gp.wait(sPE, sPE.v)
                wload(gp, sGP, wrb, wrT_i, 8, hh * 512, 512)
                wload(gp, sGP, wib, wiT_i, 8, hh * 512, 512)
                wr_at = sGP.v
                for fq in range(4):
                    fo = hh * 4 + fq
                    key = f"A{fo % 4}"
                    guard(key)
                    pe.wait(sGP, wr_at)
                    pe.wait(sDL, cs_at)
                    for j in range(NCH):
                        nc.tensor.matmul(bA(fo % 4, 256),
                                         wrb[:, j, fq * 128:(fq + 1) * 128],
                                         csl[:, j, :], start=(j == 0), stop=False)
                    for j in range(NCH):
                        mm = nc.tensor.matmul(bA(fo % 4, 256),
                                              wib[:, j, fq * 128:(fq + 1) * 128],
                                              snl[:, j, :], start=False,
                                              stop=(j == NCH - 1))
                    inc(mm, sPE, 1)
                    ve.wait(sPE, sPE.v)
                    inc(ve.e.tensor_copy(T1b[:, fo, :], bA(fo % 4, 256)), sVE, 1)
                    release(key, sVE)
                    key = f"B{fo % 4}"
                    guard(key)
                    for j in range(NCH):
                        nc.tensor.matmul(bB(fo % 4, 256),
                                         wib[:, j, fq * 128:(fq + 1) * 128],
                                         csl[:, j, :], start=(j == 0), stop=False)
                    for j in range(NCH):
                        mm = nc.tensor.matmul(bB(fo % 4, 256),
                                              wrb[:, j, fq * 128:(fq + 1) * 128],
                                              ssl[:, j, :], start=False,
                                              stop=(j == NCH - 1))
                    inc(mm, sPE, 1)
                    ve.wait(sPE, sPE.v)
                    inc(ve.e.tensor_copy(T2b[:, fo, :], bB(fo % 4, 256)), sVE, 1)
                    release(key, sVE)
            t12_at = sVE.v
            for do in range(NCH):
                sy.wait(sPE, sPE.v)
                wload(sy, sDL, cdo, cCb, 8, do * 128, 128)
                wload(sy, sDL, sdo, cSb, 8, do * 128, 128)
                do_at = sDL.v
                key = f"A{do % 4}"
                guard(key)
                pe.wait(sDL, do_at)
                pe.wait(sVE, t12_at)
                for f_ in range(NCH):
                    nc.tensor.matmul(bA(do % 4, 256), cdo[:, f_, :], T1b[:, f_, :],
                                     start=(f_ == 0), stop=False)
                for f_ in range(NCH):
                    mm = nc.tensor.matmul(bA(do % 4, 256), sdo[:, f_, :], T2b[:, f_, :],
                                          start=False, stop=(f_ == NCH - 1))
                inc(mm, sPE, 1)
                ve.wait(sPE, sPE.v)
                inc(ve.e.tensor_scalar(WEF[:, do, :], bA(do % 4, 256),
                                       c_iD128[:, :], None, op0=OP.mult), sVE, 1)
                release(key, sVE)
            wef_at = sVE.v
            for jj in range(2):
                j = cs * 2 + jj
                for qb in range(4):
                    key = f"B{qb % 2}"
                    guard(key)
                    pe.wait(sVE, wef_at)
                    for dch in range(NCH):
                        mm = nc.tensor.matmul(
                            bB(qb % 2), WEF[:, dch, jj * 128:(jj + 1) * 128],
                            X1BF[:, dch, qb * 512:(qb + 1) * 512],
                            start=(dch == 0), stop=(dch == NCH - 1))
                    inc(mm, sPE, 1)
                    dst = XRES[:, j, qb * 512:(qb + 1) * 512]
                    ve.wait(sPE, sPE.v)
                    inc(ve.e.scalar_tensor_tensor(dst, bB(qb % 2), c_l1b[:, j:j + 1], dst,
                                                  op0=OP.add, op1=OP.add), sVE, 1)
                    release(key, sVE)
        postln(XRES, 0, T)

    XOT = [E(nc.sbuf_tensor(f"XOT{i}", [128, 1024], F32)) for i in range(2)]
    xot_free = [0, 0]
    for t in range(NQT):
        b = t % 2
        for j in range(NCH):
            key = f"B{2 + j % 2}"
            guard(key)
            pe.wait(sVE, sVE.v)
            inc(nc.tensor.transpose(bB(2 + j % 2, 128), XRES[:, j, t * 128:(t + 1) * 128],
                                    c_ident[:, :]), sPE, 1)
            ve.wait(sPE, sPE.v)
            ve.wait(sST, xot_free[b])
            inc(ve.e.tensor_copy(XOT[b][:, j * 128:(j + 1) * 128], bB(2 + j % 2, 128)), sVE, 1)
            release(key, sVE)
        sy.wait(sVE, sVE.v)
        sy.wait(sST, sST.v)
        inc(sy.e.dma_start(out=xout[t * 128:(t + 1) * 128, :], in_=XOT[b][:, :]), sST, 16)
        xot_free[b] = sST.v
    gp.wait(sST, sST.v)
    gp.wait(sDT, sDT.v)
    gp.wait(sDL, sDL.v)
    gp.wait(sGP, sGP.v)

    est.close()
    return nc


def _mk_consts():
    d = np.arange(D, dtype=np.int64)
    prod = np.outer(d, d) % D
    ang = 2.0 * np.pi * prod.astype(np.float64) / D
    return np.cos(ang), np.sin(ang)


def kernel(
    L0_attn_x, L0_attn_mask, L0_attn_Wq, L0_attn_Wk, L0_attn_Wv, L0_attn_gq, L0_attn_bq,
    L0_attn_gk, L0_attn_bk, L0_attn_Wo, L0_attn_bo, L0_ffn_U1, L0_ffn_V1, L0_ffn_b1,
    L0_ffn_U2, L0_ffn_V2, L0_ffn_b2, L1_attn_Wq, L1_attn_Wk, L1_attn_Wv,
    L1_ffn_w_freq_real, L1_ffn_w_freq_imag, L1_ffn_bias,
):
    f32 = np.float32
    bf16 = ml_dtypes.bfloat16
    x = np.ascontiguousarray(np.asarray(L0_attn_x, f32))

    def t2(a):
        return np.ascontiguousarray(np.asarray(a, f32).T)

    def col(a, n=128):
        a = np.asarray(a, f32).reshape(-1)
        return np.ascontiguousarray(a.reshape(-1, n).T)

    def rep2(a):
        a = np.asarray(a, f32).reshape(-1)
        return np.ascontiguousarray(np.concatenate([a, a])[:, None])

    C, Sm = _mk_consts()
    e2 = np.zeros((128, 2), f32)
    e2[0:64, 0] = 1.0
    e2[64:128, 1] = 1.0
    shared = {
        "l0_wqT": t2(L0_attn_Wq), "l0_wkT": t2(L0_attn_Wk), "l0_wvT": t2(L0_attn_Wv),
        "l0_woT": t2(L0_attn_Wo), "l0_bo": col(L0_attn_bo),
        "l0_gq": rep2(L0_attn_gq), "l0_bq": rep2(L0_attn_bq),
        "l0_gk": rep2(L0_attn_gk), "l0_bk": rep2(L0_attn_bk),
        "l0_u1": np.ascontiguousarray(np.asarray(L0_ffn_U1, f32)),
        "l0_v1": np.ascontiguousarray(np.asarray(L0_ffn_V1, f32)),
        "l0_b1": col(L0_ffn_b1),
        "l0_u2": np.ascontiguousarray(np.asarray(L0_ffn_U2, f32)),
        "l0_v2": np.ascontiguousarray(np.asarray(L0_ffn_V2, f32)),
        "l0_b2": col(L0_ffn_b2),
        "l1_wqT": t2(L1_attn_Wq),
        "l1_wkT2": np.ascontiguousarray(
            np.concatenate([np.asarray(L1_attn_Wk, f32).T] * 2, axis=1)),
        "l1_wvT": t2(L1_attn_Wv),
        "l1_bias": col(L1_ffn_bias),
        "wrT": t2(L1_ffn_w_freq_real), "wiT": t2(L1_ffn_w_freq_imag),
        "cCb": C.astype(bf16), "cSb": Sm.astype(bf16), "cSnb": (-Sm).astype(bf16),
        "e2bf": e2.astype(bf16),
        "e2tf": np.ascontiguousarray(e2.T),
        "ones1": np.ones((1, 128), f32),
        "neg1": -np.ones((1, 128), f32),
        "onesc": np.ones((128, 1), f32),
        "identf": np.eye(128, dtype=f32),
        "o64": np.ones((64, 1), bf16),
        "oc128": np.ones((128, 1), bf16),
    }
    trilm = np.where(np.tril(np.ones((128, 128), bool)), 0.0, -1e30).astype(f32)
    fullm = np.full((128, 128), -1e30, f32)
    zerom = np.zeros((128, 128), f32)

    nc = build()
    in_maps = []
    for c in range(8):
        n, f = c // 2, c % 2
        tiles = [2 * i + f for i in range(16)]
        xq_rows = np.ascontiguousarray(
            np.concatenate([x[n, t * 128:(t + 1) * 128, :] for t in tiles], axis=0))
        im = dict(shared)
        im["xq"] = xq_rows
        im["xb"] = np.ascontiguousarray(x[n])
        im["mask_a"] = trilm if f == 0 else zerom
        im["mask_b"] = fullm if f == 0 else trilm
        in_maps.append(im)

    res = run_bass_kernel_spmd(nc, in_maps, core_ids=list(range(8)))
    out = np.empty((4, 4096, 1024), f32)
    for c in range(8):
        n, f = c // 2, c % 2
        r = np.asarray(res.results[c]["xout"], f32)
        for i in range(16):
            t = 2 * i + f
            out[n, t * 128:(t + 1) * 128, :] = r[i * 128:(i + 1) * 128, :]
    return out


# version 10 (final)
# speedup vs baseline: 1.1549x; 1.0101x over previous
import sys

sys.path.insert(0, "/opt/trn_rl_repo")
from contextlib import ExitStack

import numpy as np
import ml_dtypes
import concourse.bass as bass
import concourse.mybir as mybir
from concourse.bass_utils import run_bass_kernel_spmd
from concourse.alu_op_type import AluOpType as OP

F32 = mybir.dt.float32
BF16 = mybir.dt.bfloat16
AF = mybir.ActivationFunctionType
AX = mybir.AxisListType

D = 1024
S = 4096
NCH = 8
T = 2048
NQT = 16
NKT = 32
VW = 72


class Sem:
    __slots__ = ("h", "v")

    def __init__(self, h):
        self.h = h
        self.v = 0


class Wr:
    def __init__(self, e):
        self.e = e
        self.seen = {}

    def wait(self, sem, val=None):
        if val is None:
            val = sem.v
        if val <= 0 or self.seen.get(id(sem), 0) >= val:
            return
        self.e.wait_ge(sem.h, val)
        self.seen[id(sem)] = val


def inc(instr, sem, n=1):
    instr.then_inc(sem.h, n)
    sem.v += n


DBGQG = 0


def build(debug=0):
    nc = bass.Bass("TRN2", target_bir_lowering=False)

    def din(name, shape, dt=F32):
        return nc.dram_tensor(name, shape, dt, kind="ExternalInput").ap()

    xq = din("xq", [T, D])
    xb = din("xb", [S, D])
    mask_a = din("mask_a", [128, 128])
    mask_b = din("mask_b", [128, 128])
    l0_wqT = din("l0_wqT", [D, D])
    l0_wkT = din("l0_wkT", [D, D])
    l0_wvT = din("l0_wvT", [D, D])
    l0_woT = din("l0_woT", [D, D])
    l0_bo = din("l0_bo", [128, 8])
    l0_gq = din("l0_gq", [128, 1])
    l0_bq = din("l0_bq", [128, 1])
    l0_gk = din("l0_gk", [128, 1])
    l0_bk = din("l0_bk", [128, 1])
    l0_u1 = din("l0_u1", [D, 256])
    l0_v1 = din("l0_v1", [256, 4096])
    l0_b1 = din("l0_b1", [128, 32])
    l0_u2 = din("l0_u2", [4096, 256])
    l0_v2 = din("l0_v2", [256, D])
    l0_b2 = din("l0_b2", [128, 8])
    l1_wqT = din("l1_wqT", [D, D])
    l1_wkT2 = din("l1_wkT2", [D, 128])
    l1_wvT = din("l1_wvT", [D, 64])
    l1_bias = din("l1_bias", [128, 8])
    wrT_i = din("wrT", [D, D])
    wiT_i = din("wiT", [D, D])
    cCb = din("cCb", [D, D], BF16)
    cSb = din("cSb", [D, D], BF16)
    cSnb = din("cSnb", [D, D], BF16)
    e2bf_i = din("e2bf", [128, 2], BF16)
    e2tf_i = din("e2tf", [2, 128])
    ones1_i = din("ones1", [1, 128])
    neg1_i = din("neg1", [1, 128])
    onesc_i = din("onesc", [128, 1])
    identf_i = din("identf", [128, 128])
    o64_i = din("o64", [64, 1], BF16)
    oc128_i = din("oc128", [128, 1], BF16)

    xout = nc.dram_tensor("xout", [T, D], F32, kind="ExternalOutput").ap()
    dbgo = {}
    if debug:
        for nm in ("d_x1", "d_x2", "d_x3"):
            dbgo[nm] = nc.dram_tensor(nm, [128, NCH, T], F32, kind="ExternalOutput").ap()
        dbgo["d_k2"] = nc.dram_tensor("d_k2", [128, S], F32, kind="ExternalOutput").ap()
        dbgo["d_vt1"] = nc.dram_tensor("d_vt1", [128, NKT * VW], F32,
                                       kind="ExternalOutput").ap()
        dbgo["d_negb"] = nc.dram_tensor("d_negb", [128, 1], F32, kind="ExternalOutput").ap()
        dbgo["d_qg"] = nc.dram_tensor("d_qg", [128, NCH, 256], F32,
                                      kind="ExternalOutput").ap()
        dbgo["d_pt0"] = nc.dram_tensor("d_pt0", [128, 1024], F32, kind="ExternalOutput").ap()
        dbgo["d_ob1"] = nc.dram_tensor("d_ob1", [128, NCH, 256], F32,
                                       kind="ExternalOutput").ap()
        dbgo["d_scal"] = nc.dram_tensor("d_scal", [1, 16], F32, kind="ExternalOutput").ap()
        dbgo["d_xr"] = nc.dram_tensor("d_xr", [128, NCH, 512], F32,
                                      kind="ExternalOutput").ap()
        dbgo["d_xr0"] = nc.dram_tensor("d_xr0", [128, NCH, 256], F32,
                                       kind="ExternalOutput").ap()
        dbgo["d_stm"] = nc.dram_tensor("d_stm", [2, 2048], F32, kind="ExternalOutput").ap()
        dbgo["d_sts"] = nc.dram_tensor("d_sts", [2, 2048], F32, kind="ExternalOutput").ap()
        dbgo["d_str"] = nc.dram_tensor("d_str", [2, 2048], F32, kind="ExternalOutput").ap()

    cc_kin = nc.dram_tensor("cc_kin", [64, T], BF16)
    cc_kout = nc.dram_tensor("cc_kout", [128, T], BF16)
    cc_vin = nc.dram_tensor("cc_vin", [128, NQT * VW], BF16)
    cc_vout = nc.dram_tensor("cc_vout", [256, NQT * VW], BF16)
    pairs = [[0, 1], [2, 3], [4, 5], [6, 7]]

    est = ExitStack()
    E = est.enter_context

    sDL = Sem(E(nc.semaphore("sDL")))
    sDT = Sem(E(nc.semaphore("sDT")))
    sST = Sem(E(nc.semaphore("sST")))
    sPE = Sem(E(nc.semaphore("sPE")))
    sVE = Sem(E(nc.semaphore("sVE")))
    sAC = Sem(E(nc.semaphore("sAC")))
    sGP = Sem(E(nc.semaphore("sGP")))
    sCC = Sem(E(nc.semaphore("sCC")))
    sX0 = Sem(E(nc.semaphore("sX0")))
    sX1 = Sem(E(nc.semaphore("sX1")))
    sX = [sX0, sX1]
    sGB = Sem(E(nc.semaphore("sGB")))
    ALLSEMS = [sDL, sDT, sST, sPE, sVE, sAC, sGP, sCC]

    pe = Wr(nc.tensor)
    ve = Wr(nc.vector)
    ac = Wr(nc.scalar)
    gp = Wr(nc.gpsimd)
    sy = Wr(nc.sync)

    def fence(*engs):
        for e_ in engs:
            for sm_ in ALLSEMS:
                e_.wait(sm_, sm_.v)

    def gbar():
        gp.wait(sVE, sVE.v)
        gp.wait(sAC, sAC.v)
        inc(gp.e.nop(), sGB, 1)
        ve.wait(sGB, sGB.v)
        ac.wait(sGB, sGB.v)
        pe.wait(sGB, sGB.v)

    def wload(engw, sem, dst3, src2d, nj, c0, w):
        engw.wait(sem, sem.v)
        for j_ in range(nj):
            inc(engw.e.dma_start(out=dst3[:, j_, :],
                                 in_=src2d[j_ * 128:(j_ + 1) * 128, c0:c0 + w]), sem, 16)

    c_e2bf = E(nc.sbuf_tensor("c_e2bf", [128, 2], BF16))
    c_e2tf = E(nc.sbuf_tensor("c_e2tf", [2, 128], F32))
    c_ones1 = E(nc.sbuf_tensor("c_ones1", [1, 128], F32))
    c_neg1 = E(nc.sbuf_tensor("c_neg1", [1, 128], F32))
    c_onesc = E(nc.sbuf_tensor("c_onesc", [128, 1], F32))
    c_ident = E(nc.sbuf_tensor("c_ident", [128, 128], F32))
    c_maska = E(nc.sbuf_tensor("c_maska", [128, 128], F32))
    c_maskb = E(nc.sbuf_tensor("c_maskb", [128, 128], F32))
    c_o64 = E(nc.sbuf_tensor("c_o64", [64, 1], BF16))
    c_oc128 = E(nc.sbuf_tensor("c_oc128", [128, 1], BF16))
    rowb = E(nc.sbuf_tensor("rowb", [1, 128], F32))
    qmtmp = E(nc.sbuf_tensor("qmtmp", [1, 1], F32))
    qmaxs = E(nc.sbuf_tensor("qmaxs", [1, 1], F32))
    dscal = E(nc.sbuf_tensor("dscal", [1, 16], F32))
    c_gq = E(nc.sbuf_tensor("c_gq", [128, 1], F32))
    c_bq = E(nc.sbuf_tensor("c_bq", [128, 1], F32))
    c_gk = E(nc.sbuf_tensor("c_gk", [128, 1], F32))
    c_bk = E(nc.sbuf_tensor("c_bk", [128, 1], F32))
    c_bo = E(nc.sbuf_tensor("c_bo", [128, 8], F32))
    c_b1 = E(nc.sbuf_tensor("c_b1", [128, 32], F32))
    c_b2 = E(nc.sbuf_tensor("c_b2", [128, 8], F32))
    c_l1b = E(nc.sbuf_tensor("c_l1b", [128, 8], F32))
    c_zero = E(nc.sbuf_tensor("c_zero", [128, 1], F32))
    c_i64 = E(nc.sbuf_tensor("c_i64", [2, 1], F32))
    c_iD = E(nc.sbuf_tensor("c_iD", [1, 1], F32))
    c_iD128 = E(nc.sbuf_tensor("c_iD128", [128, 1], F32))
    c_eps2 = E(nc.sbuf_tensor("c_eps2", [2, 1], F32))

    amax = E(nc.sbuf_tensor("amax", [128, 16], F32))
    negm2 = E(nc.sbuf_tensor("negm2", [128, 2], F32))
    ellp = E(nc.sbuf_tensor("ellp", [128, 8], F32))
    recl2 = E(nc.sbuf_tensor("recl2", [128, 2], F32))
    reclT = E(nc.sbuf_tensor("reclT", [2, 128], F32))
    mtile = E(nc.sbuf_tensor("mtile", [128, 256], F32))
    st_m = E(nc.sbuf_tensor("st_m", [2, 2048], F32))
    st_s = E(nc.sbuf_tensor("st_s", [2, 2048], F32))
    st_r = E(nc.sbuf_tensor("st_r", [2, 2048], F32))
    st_t = E(nc.sbuf_tensor("st_t", [2, 2048], F32))
    kmax2 = E(nc.sbuf_tensor("kmax2", [1, 1], F32))
    kmtmp = E(nc.sbuf_tensor("kmtmp", [1, 1], F32))
    qmax2 = E(nc.sbuf_tensor("qmax2", [2, 8], F32))
    qmaxr = E(nc.sbuf_tensor("qmaxr", [2, 1], F32))
    qmaxf = E(nc.sbuf_tensor("qmaxf", [1, 2], F32))
    bsc = E(nc.sbuf_tensor("bsc", [1, 1], F32))
    negB = E(nc.sbuf_tensor("negB", [128, 1], F32))
    tmpa = E(nc.sbuf_tensor("tmpa", [128, 512], F32))
    tmpb = E(nc.sbuf_tensor("tmpb", [128, 512], F32))
    mtil2 = tmpb
    stg = E(nc.sbuf_tensor("stg", [128, 512], BF16))
    stg2 = E(nc.sbuf_tensor("stg2", [128, 512], BF16))

    X1BF = E(nc.sbuf_tensor("X1BF", [128, NCH, T], BF16))
    XTM = [E(nc.sbuf_tensor(f"XTM{i}", [128, 1024], BF16)) for i in range(2)]

    PS_A = E(nc.psum_tensor("PS_A", [128, 2048], F32))
    PS_B = E(nc.psum_tensor("PS_B", [128, 2048], F32))

    def bA(i, w=512, p0=0, p1=128):
        return PS_A[p0:p1, i * 512:i * 512 + w]

    def bB(i, w=512, p0=0, p1=128):
        return PS_B[p0:p1, i * 512:i * 512 + w]

    bg = {}

    def guard(key):
        if key in bg:
            sem, val = bg[key]
            pe.wait(sem, val)

    def release(key, sem):
        bg[key] = (sem, sem.v)

    for dst, src in [
        (c_e2bf, e2bf_i), (c_e2tf, e2tf_i), (c_ones1, ones1_i), (c_neg1, neg1_i),
        (c_onesc, onesc_i), (c_ident, identf_i), (c_maska, mask_a), (c_maskb, mask_b),
        (c_o64, o64_i), (c_oc128, oc128_i),
        (c_gq, l0_gq), (c_bq, l0_bq), (c_gk, l0_gk), (c_bk, l0_bk),
        (c_bo, l0_bo), (c_b1, l0_b1), (c_b2, l0_b2), (c_l1b, l1_bias),
    ]:
        inc(sy.e.dma_start(out=dst[:, :], in_=src), sDL, 16)
    nc.vector.memset(c_zero[:, :], 0.0)
    nc.vector.memset(c_i64[:, :], 1.0 / 64.0)
    nc.vector.memset(c_iD[:, :], 1.0 / 1024.0)
    nc.vector.memset(c_iD128[:, :], 1.0 / 1024.0)
    inc(nc.vector.memset(c_eps2[:, :], 1e-5), sVE, 1)
    consts_d = sDL.v
    nc.sync.wait_ge(sDL.h, consts_d)

    xtm_free = [0, 0]

    def load_fm(dst3, src2d, nt, coff=0):
        sy.wait(sDT, sDT.v)
        for t in range(nt):
            b = t % 2
            sb = sX[b]
            gp.wait(sDT, sDT.v)
            inc(gp.e.dma_start(out=XTM[b][:, :], in_=src2d[t * 128:(t + 1) * 128, :]), sb, 16)
            sy.wait(sb, sb.v)
            inc(sy.e.dma_start_transpose(
                out=dst3[:, :, coff + t * 128:coff + (t + 1) * 128], in_=XTM[b][:, :]), sDT, 16)
            xtm_free[b] = sDT.v

    def rstd_from_stats(n, rows=2):
        ve.wait(sVE, sVE.v)
        ve.wait(sAC, sAC.v)
        inc(ve.e.tensor_tensor(st_t[0:rows, :n], st_m[0:rows, :n], st_m[0:rows, :n],
                               op=OP.mult), sVE, 1)
        ve.wait(sVE, sVE.v)
        inc(ve.e.tensor_tensor(st_t[0:rows, :n], st_s[0:rows, :n], st_t[0:rows, :n],
                               op=OP.subtract), sVE, 1)
        ac.wait(sVE, sVE.v)
        ac.wait(sPE, sPE.v)
        inc(ac.e.activation(st_r[0:rows, :n], st_t[0:rows, :n], AF.Ln,
                            bias=c_eps2[0:rows, :]), sAC, 1)
        ac.wait(sAC, sAC.v)
        inc(ac.e.activation(st_r[0:rows, :n], st_r[0:rows, :n], AF.Exp,
                            scale=np.float32(-0.5)), sAC, 1)
        return sAC.v

    def postln(XRES, c0, ct):
        nqb = max(1, ct // 512)
        bw = min(512, ct)
        for qb in range(nqb):
            cs = c0 + qb * bw
            lo = qb * bw
            guard("A0")
            guard("A1")
            for j in range(NCH):
                src = XRES[:, j, cs:cs + bw]
                ve.wait(sPE, sPE.v)
                inc(ve.e.tensor_tensor(tmpa[:, :bw], src, src, op=OP.mult), sVE, 1)
                pe.wait(sVE, sVE.v)
                nc.tensor.matmul(bA(0, bw, 0, 1), c_onesc[:, :], src,
                                 start=(j == 0), stop=(j == NCH - 1))
                mm = nc.tensor.matmul(bA(1, bw, 0, 1), c_onesc[:, :], tmpa[:, :bw],
                                      start=(j == 0), stop=(j == NCH - 1))
                inc(mm, sPE, 1)
            ve.wait(sPE, sPE.v)
            ve.wait(sAC, sAC.v)
            inc(ve.e.tensor_scalar(st_m[0:1, lo:lo + bw], bA(0, bw, 0, 1),
                                   c_iD[0:1, :], None, op0=OP.mult), sVE, 1)
            inc(ve.e.tensor_scalar(st_s[0:1, lo:lo + bw], bA(1, bw, 0, 1),
                                   c_iD[0:1, :], None, op0=OP.mult), sVE, 1)
            release("A0", sVE)
            release("A1", sVE)
        rstd_at = rstd_from_stats(ct, rows=1)
        for qb in range(nqb):
            cs = c0 + qb * bw
            lo = qb * bw
            guard("A2")
            guard("A3")
            pe.wait(sAC, rstd_at)
            inc(nc.tensor.matmul(bA(2, bw), c_ones1[:, :], st_m[0:1, lo:lo + bw],
                                 start=True, stop=True), sPE, 1)
            inc(nc.tensor.matmul(bA(3, bw), c_ones1[:, :], st_r[0:1, lo:lo + bw],
                                 start=True, stop=True), sPE, 1)
            ve.wait(sPE, sPE.v)
            for j in range(NCH):
                src = XRES[:, j, cs:cs + bw]
                inc(ve.e.scalar_tensor_tensor(tmpa[:, :bw], src, c_zero[:, :], bA(2, bw),
                                              op0=OP.add, op1=OP.subtract), sVE, 1)
                ve.wait(sVE, sVE.v)
                inc(ve.e.tensor_tensor(src, tmpa[:, :bw], bA(3, bw), op=OP.mult), sVE, 1)
                ve.wait(sVE, sVE.v)
                inc(ve.e.tensor_copy(X1BF[:, j, cs:cs + bw], src), sVE, 1)
            release("A2", sVE)
            release("A3", sVE)

    def kqstats(buf_slice, m_dst, s_dst):
        ve.wait(sPE, sPE.v)
        inc(ve.e.tensor_tensor(stg[:, :], buf_slice, buf_slice, op=OP.mult), sVE, 1)
        guard("B2")
        guard("B3")
        pe.wait(sVE, sVE.v)
        nc.tensor.matmul(bB(2, 512, 0, 2), c_e2bf[:, :], buf_slice, start=True, stop=True)
        inc(nc.tensor.matmul(bB(3, 512, 0, 2), c_e2bf[:, :], stg[:, :],
                             start=True, stop=True), sPE, 1)
        ve.wait(sPE, sPE.v)
        inc(ve.e.tensor_scalar(m_dst, bB(2, 512, 0, 2), c_i64[:, :], None, op0=OP.mult),
            sVE, 1)
        inc(ve.e.tensor_scalar(s_dst, bB(3, 512, 0, 2), c_i64[:, :], None, op0=OP.mult),
            sVE, 1)
        release("B2", sVE)
        release("B3", sVE)

    def lnapply(buf3, cl, c0, n, rstd_at, gv, bv):
        for cb_ in range(n // 512):
            col = c0 + cb_ * 512
            guard("A2")
            guard("A3")
            pe.wait(sAC, rstd_at)
            pe.wait(sVE, sVE.v)
            inc(nc.tensor.matmul(bA(2), c_e2tf[:, :], st_m[0:2, cb_ * 512:(cb_ + 1) * 512],
                                 start=True, stop=True), sPE, 1)
            inc(nc.tensor.matmul(bA(3), c_e2tf[:, :], st_r[0:2, cb_ * 512:(cb_ + 1) * 512],
                                 start=True, stop=True), sPE, 1)
            dst = buf3[:, cl, col:col + 512] if cl is not None else buf3[:, col:col + 512]
            ve.wait(sPE, sPE.v)
            inc(ve.e.scalar_tensor_tensor(tmpa[:, :], dst, c_zero[:, :], bA(2),
                                          op0=OP.add, op1=OP.subtract), sVE, 1)
            ve.wait(sVE, sVE.v)
            inc(ve.e.tensor_tensor(tmpb[:, :], tmpa[:, :], bA(3), op=OP.mult), sVE, 1)
            ve.wait(sVE, sVE.v)
            inc(ve.e.tensor_scalar(dst, tmpb[:, :], gv[:, :], bv[:, :],
                                   op0=OP.mult, op1=OP.add), sVE, 1)
            release("A2", sVE)
            release("A3", sVE)

    sy.wait(sDL, consts_d)
    with ExitStack() as l0s:
        L = l0s.enter_context
        XQBF = L(nc.sbuf_tensor("XQBF", [128, NCH, T], BF16))
        XB0 = L(nc.sbuf_tensor("XB0", [128, NCH, 1024], BF16))
        XB1 = L(nc.sbuf_tensor("XB1", [128, NCH, 1024], BF16))
        XB = [XB0, XB1]
        xb_at = [0, 0]
        KH = L(nc.sbuf_tensor("KH", [128, S], BF16))
        VT = L(nc.sbuf_tensor("VT", [128, NKT * 128], BF16))
        QH = L(nc.sbuf_tensor("QH", [128, T], BF16))
        PB = [L(nc.sbuf_tensor(f"PB{i}", [128, 4096], BF16)) for i in range(2)]
        PT = [L(nc.sbuf_tensor(f"PT{i}", [128, NKT, 128], BF16)) for i in range(2)]
        wq_g = L(nc.sbuf_tensor("wq_g", [128, NCH, 128], BF16))
        wk_g = L(nc.sbuf_tensor("wk_g", [128, NCH, 128], BF16))
        wv_g = L(nc.sbuf_tensor("wv_g", [128, NCH, 128], BF16))

        load_fm(XQBF, xq, 16)
        xq_fm_at = sDT.v
        pb_free = 0
        pt_free = 0

        for g in range(8):
            gp.wait(sPE, sPE.v)
            for dst, srcw in [(wq_g, l0_wqT), (wk_g, l0_wkT), (wv_g, l0_wvT)]:
                wload(gp, sGP, dst, srcw, 8, g * 128, 128)
            wg_at = sGP.v

            if g == 0:
                sy.wait(sPE, sPE.v)
                load_fm(XB[0], xb[0:1024, :], 8)
                xb_at[0] = sDT.v
            for half in range(2):
                for qq in range(2):
                    qr = half * 2 + qq
                    nq = qr + 1
                    sy.wait(sPE, sPE.v)
                    if nq < 4:
                        load_fm(XB[nq % 2], xb[nq * 1024:(nq + 1) * 1024, :], 8)
                        xb_at[nq % 2] = sDT.v
                    elif g < 7:
                        load_fm(XB[0], xb[0:1024, :], 8)
                        xb_at[0] = sDT.v
                    XBF = XB[qr % 2]
                    for tb2 in range(2):
                        tb = qq * 2 + tb2
                        key = f"A{tb % 2}"
                        guard(key)
                        pe.wait(sGP, wg_at)
                        pe.wait(sDT, xb_at[qr % 2])
                        for j in range(NCH):
                            mm = nc.tensor.matmul(bA(tb % 2), wk_g[:, j, :],
                                                  XBF[:, j, tb2 * 512:(tb2 + 1) * 512],
                                                  start=(j == 0), stop=(j == NCH - 1))
                        inc(mm, sPE, 1)
                        khs = KH[:, half * T + tb * 512:half * T + (tb + 1) * 512]
                        ve.wait(sPE, sPE.v)
                        inc(ve.e.tensor_copy(khs, bA(tb % 2)), sVE, 1)
                        release(key, sVE)
                        ve.wait(sVE, sVE.v)
                        kqstats(khs, st_m[:, tb * 512:(tb + 1) * 512],
                                st_s[:, tb * 512:(tb + 1) * 512])
                    for kt2 in range(8):
                        kt = qq * 8 + kt2
                        key = f"B{kt % 2}"
                        guard(key)
                        for j in range(NCH):
                            mm = nc.tensor.matmul(bB(kt % 2, 128),
                                                  XBF[:, j, kt2 * 128:(kt2 + 1) * 128],
                                                  wv_g[:, j, :], start=(j == 0),
                                                  stop=(j == NCH - 1))
                        inc(mm, sPE, 1)
                        gkt = half * 16 + kt
                        ve.wait(sPE, sPE.v)
                        inc(ve.e.tensor_copy(VT[:, gkt * 128:(gkt + 1) * 128],
                                             bB(kt % 2, 128)), sVE, 1)
                        release(key, sVE)
                r_at = rstd_from_stats(T)
                lnapply(KH, None, half * T, T, r_at, c_gk, c_bk)
            for tb in range(4):
                key = f"A{tb % 2}"
                guard(key)
                pe.wait(sDT, xq_fm_at)
                for j in range(NCH):
                    mm = nc.tensor.matmul(bA(tb % 2), wq_g[:, j, :],
                                          XQBF[:, j, tb * 512:(tb + 1) * 512],
                                          start=(j == 0), stop=(j == NCH - 1))
                inc(mm, sPE, 1)
                qhs = QH[:, tb * 512:(tb + 1) * 512]
                ve.wait(sPE, sPE.v)
                inc(ve.e.tensor_copy(qhs, bA(tb % 2)), sVE, 1)
                release(key, sVE)
                ve.wait(sVE, sVE.v)
                kqstats(qhs, st_m[:, tb * 512:(tb + 1) * 512],
                        st_s[:, tb * 512:(tb + 1) * 512])
            r_at = rstd_from_stats(T)
            lnapply(QH, None, 0, T, r_at, c_gq, c_bq)
            lnk_at = sVE.v
            for qt in range(NQT):
                nk = 2 * qt + 2
                qlh = QH[0:64, qt * 128:(qt + 1) * 128]
                qhh = QH[64:128, qt * 128:(qt + 1) * 128]
                ca = (nk + 3) // 4
                for c in range(ca):
                    w = min(4, nk - 4 * c) * 128
                    k0 = c * 512
                    b1_, b2_ = c % 2, 2 + c % 2
                    guard(f"A{b1_}")
                    guard(f"A{b2_}")
                    pe.wait(sVE, lnk_at)
                    nc.tensor.matmul(bA(b1_, w), qlh, KH[0:64, k0:k0 + w],
                                     start=True, stop=True)
                    inc(nc.tensor.matmul(bA(b2_, w), qhh, KH[64:128, k0:k0 + w],
                                         start=True, stop=True, tile_position=(64, 0)),
                        sPE, 1)
                    ve.wait(sPE, sPE.v)
                    if c == ca - 1:
                        for bi in (b1_, b2_):
                            inc(ve.e.tensor_tensor(bA(bi, 512)[:, w - 256:w - 128],
                                                   bA(bi, 512)[:, w - 256:w - 128],
                                                   c_maska[:, :], op=OP.add), sVE, 1)
                            inc(ve.e.tensor_tensor(bA(bi, 512)[:, w - 128:w],
                                                   bA(bi, 512)[:, w - 128:w],
                                                   c_maskb[:, :], op=OP.add), sVE, 1)
                        ve.wait(sVE, sVE.v)
                    inc(ve.e.tensor_reduce(amax[:, c:c + 1], bA(b1_, w), axis=AX.X,
                                           op=OP.max), sVE, 1)
                    inc(ve.e.tensor_reduce(amax[:, 8 + c:9 + c], bA(b2_, w), axis=AX.X,
                                           op=OP.max), sVE, 1)
                    release(f"A{b1_}", sVE)
                    release(f"A{b2_}", sVE)
                ve.wait(sVE, sVE.v)
                ve.wait(sAC, sAC.v)
                inc(ve.e.tensor_reduce(negm2[:, 0:1], amax[:, 0:ca], axis=AX.X,
                                       op=OP.max, negate=True), sVE, 1)
                inc(ve.e.tensor_reduce(negm2[:, 1:2], amax[:, 8:8 + ca], axis=AX.X,
                                       op=OP.max, negate=True), sVE, 1)
                negm_at = sVE.v
                ech = []
                cb = (nk + 7) // 8
                for c in range(cb):
                    w = min(8, nk - 8 * c) * 128
                    k0 = c * 1024
                    PSX = PS_A if c % 2 == 0 else PS_B
                    kyb = "A" if c % 2 == 0 else "B"
                    for bi in range(4):
                        guard(f"{kyb}{bi}")
                    for sub in range((w + 511) // 512):
                        ws = min(512, w - sub * 512)
                        o_ = sub * 512
                        nc.tensor.matmul(PSX[:, o_:o_ + ws], qlh,
                                         KH[0:64, k0 + o_:k0 + o_ + ws],
                                         start=True, stop=True)
                        mm = nc.tensor.matmul(PSX[:, 1024 + o_:1024 + o_ + ws], qhh,
                                              KH[64:128, k0 + o_:k0 + o_ + ws],
                                              start=True, stop=True, tile_position=(64, 0))
                    inc(mm, sPE, 1)
                    if c == cb - 1:
                        ve.wait(sPE, sPE.v)
                        for off in (0, 1024):
                            inc(ve.e.tensor_tensor(PSX[:, off + w - 256:off + w - 128],
                                                   PSX[:, off + w - 256:off + w - 128],
                                                   c_maska[:, :], op=OP.add), sVE, 1)
                            inc(ve.e.tensor_tensor(PSX[:, off + w - 128:off + w],
                                                   PSX[:, off + w - 128:off + w],
                                                   c_maskb[:, :], op=OP.add), sVE, 1)
                    ac.wait(sPE, sPE.v)
                    ac.wait(sVE, sVE.v)
                    ac.wait(sDT, pb_free)
                    inc(ac.e.activation(PB[0][:, k0:k0 + w], PSX[:, 0:w], AF.Exp,
                                        bias=negm2[:, 0:1], accum_out=ellp[:, c:c + 1]),
                        sAC, 1)
                    inc(ac.e.activation(PB[1][:, k0:k0 + w], PSX[:, 1024:1024 + w], AF.Exp,
                                        bias=negm2[:, 1:2], accum_out=ellp[:, 4 + c:5 + c]),
                        sAC, 1)
                    ech.append(sAC.v)
                    for bi in range(4):
                        release(f"{kyb}{bi}", sAC)
                exp_at = sAC.v
                ve.wait(sAC, exp_at)
                ve.wait(sPE, sPE.v)
                inc(ve.e.tensor_reduce(recl2[:, 0:1], ellp[:, 0:cb], axis=AX.X, op=OP.add),
                    sVE, 1)
                inc(ve.e.tensor_reduce(recl2[:, 1:2], ellp[:, 4:4 + cb], axis=AX.X,
                                       op=OP.add), sVE, 1)
                ve.wait(sVE, sVE.v)
                inc(ve.e.reciprocal(recl2[:, 0:2], recl2[:, 0:2]), sVE, 1)
                recl_at = sVE.v
                sy.wait(sPE, pt_free)
                sy.wait(sDT, sDT.v)
                for c in range(cb):
                    w = min(8, nk - 8 * c) * 128
                    kt0 = c * 8
                    sy.wait(sAC, ech[c])
                    inc(sy.e.dma_start_transpose(
                        out=PT[0][:, kt0:kt0 + w // 128, :],
                        in_=PB[0][:, kt0 * 128:kt0 * 128 + w]), sDT, 16)
                    inc(sy.e.dma_start_transpose(
                        out=PT[1][:, kt0:kt0 + w // 128, :],
                        in_=PB[1][:, kt0 * 128:kt0 * 128 + w]), sDT, 16)
                pb_free = sDT.v
                guard("B1")
                pe.wait(sVE, recl_at)
                inc(nc.tensor.transpose(PS_B[0:2, 512:640], recl2[:, 0:2], c_ident[:, :]),
                    sPE, 1)
                ve.wait(sPE, sPE.v)
                inc(ve.e.tensor_copy(reclT[:, :], PS_B[0:2, 512:640]), sVE, 1)
                release("B1", sVE)
                guard("B2")
                pe.wait(sVE, sVE.v)
                inc(nc.tensor.matmul(bB(2, 128), c_e2tf[:, :], reclT[:, :],
                                     start=True, stop=True), sPE, 1)
                ve.wait(sPE, sPE.v)
                inc(ve.e.tensor_copy(mtile[:, 0:128], bB(2, 128)), sVE, 1)
                release("B2", sVE)
                guard("B0")
                pe.wait(sDT, sDT.v)
                for kt in range(nk):
                    base = kt * 128
                    nc.tensor.matmul(bB(0, 128, 0, 64), VT[:, base:base + 64],
                                     PT[0][:, kt, :], start=(kt == 0), stop=(kt == nk - 1),
                                     skip_group_check=True)
                    mm = nc.tensor.matmul(bB(0, 128, 64, 128), VT[:, base + 64:base + 128],
                                          PT[1][:, kt, :], start=(kt == 0),
                                          stop=(kt == nk - 1), tile_position=(0, 64),
                                          skip_group_check=True)
                inc(mm, sPE, 1)
                pt_free = sPE.v
                ve.wait(sVE, sVE.v)
                ve.wait(sPE, sPE.v)
                inc(ve.e.tensor_tensor(X1BF[:, g, qt * 128:(qt + 1) * 128],
                                       bB(0, 128), mtile[:, 0:128], op=OP.mult), sVE, 1)
                release("B0", sVE)

    XRES = E(nc.sbuf_tensor("XRES", [128, NCH, T], F32))
    wo_stack = ExitStack()
    XTF = wo_stack.enter_context(nc.sbuf_tensor("XTF", [128, 1024], F32))
    wsl = wo_stack.enter_context(nc.sbuf_tensor("wsl", [128, NCH, 256], BF16))
    fence(ve, pe, sy, gp, ac)

    for t in range(NQT):
        sy.wait(sVE, sVE.v)
        sy.wait(sDL, sDL.v)
        inc(sy.e.dma_start(out=XTF[:, :], in_=xq[t * 128:(t + 1) * 128, :]), sDL, 16)
        pe.wait(sDL, sDL.v)
        for j in range(NCH):
            key = f"B{j % 2}"
            guard(key)
            inc(nc.tensor.transpose(bB(j % 2, 128), XTF[:, j * 128:(j + 1) * 128],
                                    c_ident[:, :]), sPE, 1)
            ve.wait(sPE, sPE.v)
            inc(ve.e.tensor_copy(XRES[:, j, t * 128:(t + 1) * 128], bB(j % 2, 128)), sVE, 1)
            release(key, sVE)
    for g4 in range(4):
        gp.wait(sPE, sPE.v)
        wload(gp, sGP, wsl, l0_woT, 8, g4 * 256, 256)
        wo_at = sGP.v
        for jj in range(2):
            j = g4 * 2 + jj
            for qb in range(4):
                key = f"A{qb % 4}"
                guard(key)
                pe.wait(sGP, wo_at)
                pe.wait(sVE, sVE.v)
                for dch in range(NCH):
                    mm = nc.tensor.matmul(bA(qb % 4), wsl[:, dch, jj * 128:(jj + 1) * 128],
                                          X1BF[:, dch, qb * 512:(qb + 1) * 512],
                                          start=(dch == 0), stop=(dch == NCH - 1))
                inc(mm, sPE, 1)
                dst = XRES[:, j, qb * 512:(qb + 1) * 512]
                ve.wait(sPE, sPE.v)
                inc(ve.e.scalar_tensor_tensor(dst, bA(qb % 4), c_bo[:, j:j + 1], dst,
                                              op0=OP.add, op1=OP.add), sVE, 1)
                release(key, sVE)
    postln(XRES, 0, T)
    wo_stack.close()
    if debug:
        sy.wait(sVE, sVE.v)
        inc(sy.e.dma_start(out=dbgo["d_x1"], in_=XRES[:, :, :]), sST, 16)

    with ExitStack() as f0s:
        L = f0s.enter_context
        u1b = L(nc.sbuf_tensor("u1b", [128, NCH, 256], BF16))
        v1b = L(nc.sbuf_tensor("v1b", [128, 2, 4096], BF16))
        u2b = L(nc.sbuf_tensor("u2b", [128, 32, 256], BF16))
        v2b = L(nc.sbuf_tensor("v2b", [128, 2, 1024], BF16))
        pqb = L(nc.sbuf_tensor("pqb", [128, 2, 512], BF16))
        qqb = L(nc.sbuf_tensor("qqb", [128, 2, 512], BF16))
        fence(gp)
        wload(gp, sGP, u1b, l0_u1, 8, 0, 256)
        wload(gp, sGP, v1b, l0_v1, 2, 0, 4096)
        wload(gp, sGP, u2b, l0_u2, 32, 0, 256)
        wload(gp, sGP, v2b, l0_v2, 2, 0, 1024)
        ffw_at = sGP.v
        for tb in range(4):
            tc = tb * 512
            for rc in range(2):
                key = f"A{rc}"
                guard(key)
                pe.wait(sGP, ffw_at)
                for j in range(NCH):
                    mm = nc.tensor.matmul(bA(rc), u1b[:, j, rc * 128:(rc + 1) * 128],
                                          X1BF[:, j, tc:tc + 512],
                                          start=(j == 0), stop=(j == NCH - 1))
                inc(mm, sPE, 1)
                ve.wait(sPE, sPE.v)
                inc(ve.e.tensor_copy(pqb[:, rc, :], bA(rc)), sVE, 1)
                release(key, sVE)
            guard("B0")
            guard("B1")
            p_at = sVE.v
            for cc in range(32):
                key = f"A{2 + cc % 2}"
                guard(key)
                pe.wait(sVE, p_at)
                for rc in range(2):
                    mm = nc.tensor.matmul(bA(2 + cc % 2), v1b[:, rc, cc * 128:(cc + 1) * 128],
                                          pqb[:, rc, :], start=(rc == 0), stop=(rc == 1))
                inc(mm, sPE, 1)
                hbuf = stg if cc % 2 == 0 else stg2
                hkey = f"H{cc % 2}"
                ac.wait(sPE, sPE.v)
                if hkey in bg:
                    ac.wait(bg[hkey][0], bg[hkey][1])
                inc(ac.e.activation(hbuf[:, :], bA(2 + cc % 2), AF.Gelu_apprx_tanh,
                                    bias=c_b1[:, cc:cc + 1]), sAC, 1)
                release(key, sAC)
                pe.wait(sAC, sAC.v)
                for rc in range(2):
                    mm = nc.tensor.matmul(bB(rc), u2b[:, cc, rc * 128:(rc + 1) * 128],
                                          hbuf[:, :], start=(cc == 0), stop=(cc == 31))
                inc(mm, sPE, 1)
                bg[hkey] = (sPE, sPE.v)
            ve.wait(sPE, sPE.v)
            inc(ve.e.tensor_copy(qqb[:, 0, :], bB(0)), sVE, 1)
            inc(ve.e.tensor_copy(qqb[:, 1, :], bB(1)), sVE, 1)
            release("B0", sVE)
            release("B1", sVE)
            q_at = sVE.v
            for j in range(NCH):
                key = f"A{j % 2}"
                guard(key)
                pe.wait(sVE, q_at)
                for rc in range(2):
                    mm = nc.tensor.matmul(bA(j % 2), v2b[:, rc, j * 128:(j + 1) * 128],
                                          qqb[:, rc, :], start=(rc == 0), stop=(rc == 1))
                inc(mm, sPE, 1)
                dst = XRES[:, j, tc:tc + 512]
                ve.wait(sPE, sPE.v)
                inc(ve.e.scalar_tensor_tensor(dst, bA(j % 2), c_b2[:, j:j + 1], dst,
                                              op0=OP.add, op1=OP.add), sVE, 1)
                release(key, sVE)
        postln(XRES, 0, T)
        if debug:
            sy.wait(sVE, sVE.v)
            inc(sy.e.dma_start(out=dbgo["d_x2"], in_=XRES[:, :, :]), sST, 16)

    with ExitStack() as l1s:
        L = l1s.enter_context
        wq1 = L(nc.sbuf_tensor("wq1", [128, NCH, 512], BF16))
        wk1 = L(nc.sbuf_tensor("wk1", [128, NCH, 128], BF16))
        wv1 = L(nc.sbuf_tensor("wv1", [128, NCH, 64], BF16))
        K2 = L(nc.sbuf_tensor("K2", [128, S], BF16))
        VT1 = L(nc.sbuf_tensor("VT1", [128, NKT * VW], BF16))
        QG = L(nc.sbuf_tensor("QG", [128, NCH, 256], BF16))
        PT0 = L(nc.sbuf_tensor("PX0", [128, NKT * 256], BF16))
        PT1 = L(nc.sbuf_tensor("PX1", [128, NKT * 256], BF16))
        OB1 = L(nc.sbuf_tensor("OB1", [128, NCH, 256], BF16))
        fence(gp)
        wload(gp, sGP, wk1, l1_wkT2, 8, 0, 128)
        wload(gp, sGP, wv1, l1_wvT, 8, 0, 64)
        w1_at = sGP.v
        for tb in range(4):
            key = f"A{tb % 2}"
            guard(key)
            pe.wait(sGP, w1_at)
            for j in range(NCH):
                mm = nc.tensor.matmul(bA(tb % 2), wk1[:, j, :],
                                      X1BF[:, j, tb * 512:(tb + 1) * 512],
                                      start=(j == 0), stop=(j == NCH - 1))
            inc(mm, sPE, 1)
            ve.wait(sPE, sPE.v)
            inc(ve.e.tensor_copy(K2[:, tb * 512:(tb + 1) * 512], bA(tb % 2)), sVE, 1)
            release(key, sVE)
        for kt in range(16):
            key = f"B{kt % 2}"
            guard(key)
            for j in range(NCH):
                mm = nc.tensor.matmul(bB(kt % 2, 64), X1BF[:, j, kt * 128:(kt + 1) * 128],
                                      wv1[:, j, :], start=(j == 0), stop=(j == NCH - 1))
            inc(mm, sPE, 1)
            ve.wait(sPE, sPE.v)
            inc(ve.e.tensor_copy(VT1[:, kt * VW:kt * VW + 64], bB(kt % 2, 64)), sVE, 1)
            release(key, sVE)
            inc(ve.e.memset(VT1[:, kt * VW + 64:kt * VW + 65], 1.0), sVE, 1)
        kv_at = sVE.v
        gp.wait(sVE, kv_at)
        inc(gp.e.dma_start(out=cc_kin.ap(), in_=K2[0:64, 0:T]), sGP, 16)
        gp.wait(sGP, sGP.v)
        inc(gp.e.collective_compute(
            "AllGather", OP.bypass, replica_groups=pairs,
            ins=[cc_kin.ap().opt()], outs=[cc_kout.ap().opt()]), sCC, 1)
        gp.wait(sCC, sCC.v)
        inc(gp.e.dma_start(out=cc_vin.ap(), in_=VT1[:, 0:16 * VW]), sGP, 16)
        gp.wait(sGP, sGP.v)
        inc(gp.e.collective_compute(
            "AllGather", OP.bypass, replica_groups=pairs,
            ins=[cc_vin.ap().opt()], outs=[cc_vout.ap().opt()]), sCC, 1)
        gp.wait(sCC, sCC.v)
        inc(gp.e.dma_start(out=K2[0:64, 0:T], in_=cc_kout.ap()[0:64, :]), sGP, 16)
        gp.wait(sGP, sGP.v)
        inc(gp.e.dma_start(out=K2[0:64, T:S], in_=cc_kout.ap()[64:128, :]), sGP, 16)
        gp.wait(sGP, sGP.v)
        inc(gp.e.dma_start(out=VT1[:, 0:16 * VW], in_=cc_vout.ap()[0:128, :]), sGP, 16)
        gp.wait(sGP, sGP.v)
        inc(gp.e.dma_start(out=VT1[:, 16 * VW:32 * VW], in_=cc_vout.ap()[128:256, :]), sGP, 16)
        gp.wait(sGP, sGP.v)
        inc(gp.e.dma_start(out=K2[64:128, 0:S], in_=K2[0:64, 0:S]), sGP, 16)
        kg_at = sGP.v
        for cb2 in range(8):
            sl = K2[0:64, cb2 * 512:(cb2 + 1) * 512]
            ve.wait(sGP, kg_at)
            ve.wait(sPE, sPE.v)
            inc(ve.e.tensor_tensor(stg[0:64, :], sl, sl, op=OP.mult), sVE, 1)
            guard("B3")
            pe.wait(sVE, sVE.v)
            inc(nc.tensor.matmul(bB(3, 512, 0, 1), c_o64[:, :], stg[0:64, :],
                                 start=True, stop=True), sPE, 1)
            ve.wait(sPE, sPE.v)
            if cb2 == 0:
                inc(ve.e.tensor_reduce(kmax2[:, :], bB(3, 512, 0, 1), axis=AX.X, op=OP.max),
                    sVE, 1)
            else:
                inc(ve.e.tensor_reduce(kmtmp[:, :], bB(3, 512, 0, 1), axis=AX.X, op=OP.max),
                    sVE, 1)
                ve.wait(sVE, sVE.v)
                inc(ve.e.tensor_tensor(kmax2[:, :], kmax2[:, :], kmtmp[:, :], op=OP.max),
                    sVE, 1)
            release("B3", sVE)
        kmax_at = sVE.v
        for qg in range(8):
            qc = qg * 256
            for j in range(NCH):
                if j % 4 == 0:
                    hh = j // 4
                    gp.wait(sPE, sPE.v)
                    wload(gp, sGP, wq1, l1_wqT, 8, hh * 512, 512)
                    wq_at = sGP.v
                key = f"A{j % 2}"
                guard(key)
                pe.wait(sGP, wq_at)
                for dch in range(NCH):
                    mm = nc.tensor.matmul(bA(j % 2, 256),
                                          wq1[:, dch, (j % 4) * 128:(j % 4 + 1) * 128],
                                          X1BF[:, dch, qc:qc + 256],
                                          start=(dch == 0), stop=(dch == NCH - 1))
                inc(mm, sPE, 1)
                ve.wait(sPE, sPE.v)
                inc(ve.e.tensor_copy(QG[:, j, :], bA(j % 2, 256)), sVE, 1)
                release(key, sVE)
                qsl = QG[:, j, :]
                ve.wait(sVE, sVE.v)
                ve.wait(sPE, sPE.v)
                inc(ve.e.tensor_tensor(stg[:, 0:256], qsl, qsl, op=OP.mult), sVE, 1)
                guard("B3")
                pe.wait(sVE, sVE.v)
                inc(nc.tensor.matmul(bB(3, 256, 0, 1), c_oc128[:, :], stg[:, 0:256],
                                     start=True, stop=True), sPE, 1)
                ve.wait(sPE, sPE.v)
                if j == 0:
                    inc(ve.e.tensor_reduce(qmaxs[:, :], bB(3, 256, 0, 1), axis=AX.X,
                                           op=OP.max), sVE, 1)
                else:
                    inc(ve.e.tensor_reduce(qmtmp[:, :], bB(3, 256, 0, 1), axis=AX.X,
                                           op=OP.max), sVE, 1)
                    ve.wait(sVE, sVE.v)
                    inc(ve.e.tensor_tensor(qmaxs[:, :], qmaxs[:, :], qmtmp[:, :],
                                           op=OP.max), sVE, 1)
                release("B3", sVE)
            ve.wait(sVE, sVE.v)
            ve.wait(sVE, kmax_at)
            inc(ve.e.tensor_tensor(bsc[:, :], qmaxs[:, :], kmax2[:, :], op=OP.mult), sVE, 1)
            ac.wait(sVE, sVE.v)
            inc(ac.e.activation(bsc[:, :], bsc[:, :], AF.Ln), sAC, 1)
            ac.wait(sAC, sAC.v)
            inc(ac.e.activation(bsc[:, :], bsc[:, :], AF.Exp, scale=np.float32(0.5),
                                bias=np.float32(-2.0794415416798357)), sAC, 1)
            ve.wait(sAC, sAC.v)
            ve.wait(sPE, sPE.v)
            inc(ve.e.tensor_scalar(rowb[:, :], c_ones1[:, :], bsc[0:1, :], None,
                                   op0=OP.mult), sVE, 1)
            guard("B3")
            pe.wait(sVE, sVE.v)
            inc(nc.tensor.transpose(PS_B[0:128, 1536:1537], rowb[:, :], c_ones1[0:1, 0:1]),
                sPE, 1)
            ve.wait(sPE, sPE.v)
            inc(ve.e.scalar_tensor_tensor(negB[:, :], c_zero[:, :], c_zero[:, :],
                                          PS_B[0:128, 1536:1537],
                                          op0=OP.add, op1=OP.subtract), sVE, 1)
            release("B3", sVE)
            negb_at = sVE.v
            for pr in range(NCH):
                q_lo = QG[0:64, pr, :]
                q_hi = QG[64:128, pr, :]
                for ktg in range(8):
                    for bi in range(4):
                        guard(f"A{bi}")
                    pe.wait(sGP, kg_at)
                    pe.wait(sVE, negb_at)
                    for kk in range(4):
                        kt = ktg * 4 + kk
                        nc.tensor.matmul(PS_A[:, kk * 256:kk * 256 + 256],
                                         K2[0:64, kt * 128:(kt + 1) * 128], q_lo,
                                         start=True, stop=True)
                        mm = nc.tensor.matmul(PS_A[:, 1024 + kk * 256:1024 + kk * 256 + 256],
                                              K2[64:128, kt * 128:(kt + 1) * 128], q_hi,
                                              start=True, stop=True, tile_position=(64, 0))
                    inc(mm, sPE, 1)
                    ac.wait(sPE, sPE.v)
                    ac.wait(sVE, negb_at)
                    inc(ac.e.activation(PT0[:, ktg * 1024:(ktg + 1) * 1024], PS_A[:, 0:1024],
                                        AF.Exp, bias=negB[:, :], scale=np.float32(0.125)),
                        sAC, 1)
                    inc(ac.e.activation(PT1[:, ktg * 1024:(ktg + 1) * 1024],
                                        PS_A[:, 1024:2048], AF.Exp, bias=negB[:, :],
                                        scale=np.float32(0.125)), sAC, 1)
                    for bi in range(4):
                        release(f"A{bi}", sAC)
                av_at = [0, 0]
                for hl, PTx in ((0, PT0), (1, PT1)):
                    guard(f"B{hl}")
                    pe.wait(sAC, sAC.v)
                    for kt in range(NKT):
                        mm = nc.tensor.matmul(bB(hl, 256, 0, 65), VT1[:, kt * VW:kt * VW + 65],
                                              PTx[:, kt * 256:(kt + 1) * 256],
                                              start=(kt == 0), stop=(kt == NKT - 1))
                    inc(mm, sPE, 1)
                    av_at[hl] = sPE.v
                rc_at = [0, 0]
                for hl, mt in ((0, mtile), (1, mtil2)):
                    ve.wait(sPE, av_at[hl])
                    inc(ve.e.reciprocal(mt[0:1, 0:256],
                                        PS_B[64:65, hl * 512:hl * 512 + 256]), sVE, 1)
                    rc_at[hl] = sVE.v
                for hl, mt in ((0, mtile), (1, mtil2)):
                    guard(f"B{2 + hl}")
                    pe.wait(sVE, rc_at[hl])
                    inc(nc.tensor.matmul(bB(2 + hl, 256), c_ones1[:, :], mt[0:1, 0:256],
                                         start=True, stop=True), sPE, 1)
                    ve.wait(sPE, sPE.v)
                    inc(ve.e.tensor_copy(mt[:, 0:256], bB(2 + hl, 256)), sVE, 1)
                    release(f"B{2 + hl}", sVE)
                for hl, mt in ((0, mtile), (1, mtil2)):
                    ve.wait(sVE, sVE.v)
                    inc(ve.e.tensor_tensor(OB1[hl * 64:(hl + 1) * 64, pr, :],
                                           PS_B[0:64, hl * 512:hl * 512 + 256],
                                           mt[0:64, 0:256], op=OP.mult), sVE, 1)
                    release(f"B{hl}", sVE)
            gbar()
            fence(ve, ac, pe)
            for j in range(NCH):
                dst = XRES[:, j, qc:qc + 256]
                ve.wait(sVE, sVE.v)
                inc(ve.e.scalar_tensor_tensor(dst, OB1[:, j, :], c_zero[:, :], dst,
                                              op0=OP.add, op1=OP.add), sVE, 1)
            postln(XRES, qc, 256)
        if debug:
            sy.wait(sVE, sVE.v)
            inc(sy.e.dma_start(out=dbgo["d_x3"], in_=XRES[:, :, :]), sST, 16)

    with ExitStack() as f1s:
        L = f1s.enter_context
        T1b = L(nc.sbuf_tensor("T1b", [128, NCH, 256], BF16))
        T2b = L(nc.sbuf_tensor("T2b", [128, NCH, 256], BF16))
        wrb = L(nc.sbuf_tensor("wrb", [128, NCH, 512], BF16))
        wib = L(nc.sbuf_tensor("wib", [128, NCH, 512], BF16))
        csl = L(nc.sbuf_tensor("csl", [128, NCH, 256], BF16))
        ssl = L(nc.sbuf_tensor("ssl", [128, NCH, 256], BF16))
        snl = L(nc.sbuf_tensor("snl", [128, NCH, 256], BF16))
        cdo = L(nc.sbuf_tensor("cdo", [128, NCH, 128], BF16))
        sdo = L(nc.sbuf_tensor("sdo", [128, NCH, 128], BF16))
        WEF = L(nc.sbuf_tensor("WEF", [128, NCH, 256], BF16))
        fence(gp, sy)
        for cs in range(4):
            sy.wait(sPE, sPE.v)
            for dst, src_ in ((csl, cCb), (ssl, cSb), (snl, cSnb)):
                wload(sy, sDL, dst, src_, 8, cs * 256, 256)
            cs_at = sDL.v
            for hh in range(2):
                gp.wait(sPE, sPE.v)
                wload(gp, sGP, wrb, wrT_i, 8, hh * 512, 512)
                wload(gp, sGP, wib, wiT_i, 8, hh * 512, 512)
                wr_at = sGP.v
                for fq in range(4):
                    fo = hh * 4 + fq
                    key = f"A{fo % 4}"
                    guard(key)
                    pe.wait(sGP, wr_at)
                    pe.wait(sDL, cs_at)
                    for j in range(NCH):
                        nc.tensor.matmul(bA(fo % 4, 256),
                                         wrb[:, j, fq * 128:(fq + 1) * 128],
                                         csl[:, j, :], start=(j == 0), stop=False)
                    for j in range(NCH):
                        mm = nc.tensor.matmul(bA(fo % 4, 256),
                                              wib[:, j, fq * 128:(fq + 1) * 128],
                                              snl[:, j, :], start=False,
                                              stop=(j == NCH - 1))
                    inc(mm, sPE, 1)
                    ve.wait(sPE, sPE.v)
                    inc(ve.e.tensor_copy(T1b[:, fo, :], bA(fo % 4, 256)), sVE, 1)
                    release(key, sVE)
                    key = f"B{fo % 4}"
                    guard(key)
                    for j in range(NCH):
                        nc.tensor.matmul(bB(fo % 4, 256),
                                         wib[:, j, fq * 128:(fq + 1) * 128],
                                         csl[:, j, :], start=(j == 0), stop=False)
                    for j in range(NCH):
                        mm = nc.tensor.matmul(bB(fo % 4, 256),
                                              wrb[:, j, fq * 128:(fq + 1) * 128],
                                              ssl[:, j, :], start=False,
                                              stop=(j == NCH - 1))
                    inc(mm, sPE, 1)
                    ve.wait(sPE, sPE.v)
                    inc(ve.e.tensor_copy(T2b[:, fo, :], bB(fo % 4, 256)), sVE, 1)
                    release(key, sVE)
            t12_at = sVE.v
            for do in range(NCH):
                sy.wait(sPE, sPE.v)
                wload(sy, sDL, cdo, cCb, 8, do * 128, 128)
                wload(sy, sDL, sdo, cSb, 8, do * 128, 128)
                do_at = sDL.v
                key = f"A{do % 4}"
                guard(key)
                pe.wait(sDL, do_at)
                pe.wait(sVE, t12_at)
                for f_ in range(NCH):
                    nc.tensor.matmul(bA(do % 4, 256), cdo[:, f_, :], T1b[:, f_, :],
                                     start=(f_ == 0), stop=False)
                for f_ in range(NCH):
                    mm = nc.tensor.matmul(bA(do % 4, 256), sdo[:, f_, :], T2b[:, f_, :],
                                          start=False, stop=(f_ == NCH - 1))
                inc(mm, sPE, 1)
                ve.wait(sPE, sPE.v)
                inc(ve.e.tensor_scalar(WEF[:, do, :], bA(do % 4, 256),
                                       c_iD128[:, :], None, op0=OP.mult), sVE, 1)
                release(key, sVE)
            wef_at = sVE.v
            for jj in range(2):
                j = cs * 2 + jj
                for qb in range(4):
                    key = f"B{qb % 2}"
                    guard(key)
                    pe.wait(sVE, wef_at)
                    for dch in range(NCH):
                        mm = nc.tensor.matmul(
                            bB(qb % 2), WEF[:, dch, jj * 128:(jj + 1) * 128],
                            X1BF[:, dch, qb * 512:(qb + 1) * 512],
                            start=(dch == 0), stop=(dch == NCH - 1))
                    inc(mm, sPE, 1)
                    dst = XRES[:, j, qb * 512:(qb + 1) * 512]
                    ve.wait(sPE, sPE.v)
                    inc(ve.e.scalar_tensor_tensor(dst, bB(qb % 2), c_l1b[:, j:j + 1], dst,
                                                  op0=OP.add, op1=OP.add), sVE, 1)
                    release(key, sVE)
        postln(XRES, 0, T)

    XOT = [E(nc.sbuf_tensor(f"XOT{i}", [128, 1024], F32)) for i in range(2)]
    xot_free = [0, 0]
    for t in range(NQT):
        b = t % 2
        for j in range(NCH):
            key = f"B{2 + j % 2}"
            guard(key)
            pe.wait(sVE, sVE.v)
            inc(nc.tensor.transpose(bB(2 + j % 2, 128), XRES[:, j, t * 128:(t + 1) * 128],
                                    c_ident[:, :]), sPE, 1)
            ve.wait(sPE, sPE.v)
            ve.wait(sST, xot_free[b])
            inc(ve.e.tensor_copy(XOT[b][:, j * 128:(j + 1) * 128], bB(2 + j % 2, 128)), sVE, 1)
            release(key, sVE)
        sy.wait(sVE, sVE.v)
        sy.wait(sST, sST.v)
        inc(sy.e.dma_start(out=xout[t * 128:(t + 1) * 128, :], in_=XOT[b][:, :]), sST, 16)
        xot_free[b] = sST.v
    gp.wait(sST, sST.v)
    gp.wait(sDT, sDT.v)
    gp.wait(sDL, sDL.v)
    gp.wait(sGP, sGP.v)

    est.close()
    return nc


def _mk_consts():
    d = np.arange(D, dtype=np.int64)
    prod = np.outer(d, d) % D
    ang = 2.0 * np.pi * prod.astype(np.float64) / D
    return np.cos(ang), np.sin(ang)


def kernel(
    L0_attn_x, L0_attn_mask, L0_attn_Wq, L0_attn_Wk, L0_attn_Wv, L0_attn_gq, L0_attn_bq,
    L0_attn_gk, L0_attn_bk, L0_attn_Wo, L0_attn_bo, L0_ffn_U1, L0_ffn_V1, L0_ffn_b1,
    L0_ffn_U2, L0_ffn_V2, L0_ffn_b2, L1_attn_Wq, L1_attn_Wk, L1_attn_Wv,
    L1_ffn_w_freq_real, L1_ffn_w_freq_imag, L1_ffn_bias,
):
    f32 = np.float32
    bf16 = ml_dtypes.bfloat16
    x = np.ascontiguousarray(np.asarray(L0_attn_x, f32))

    def t2(a):
        return np.ascontiguousarray(np.asarray(a, f32).T)

    def col(a, n=128):
        a = np.asarray(a, f32).reshape(-1)
        return np.ascontiguousarray(a.reshape(-1, n).T)

    def rep2(a):
        a = np.asarray(a, f32).reshape(-1)
        return np.ascontiguousarray(np.concatenate([a, a])[:, None])

    C, Sm = _mk_consts()
    e2 = np.zeros((128, 2), f32)
    e2[0:64, 0] = 1.0
    e2[64:128, 1] = 1.0
    shared = {
        "l0_wqT": t2(L0_attn_Wq), "l0_wkT": t2(L0_attn_Wk), "l0_wvT": t2(L0_attn_Wv),
        "l0_woT": t2(L0_attn_Wo), "l0_bo": col(L0_attn_bo),
        "l0_gq": rep2(L0_attn_gq), "l0_bq": rep2(L0_attn_bq),
        "l0_gk": rep2(L0_attn_gk), "l0_bk": rep2(L0_attn_bk),
        "l0_u1": np.ascontiguousarray(np.asarray(L0_ffn_U1, f32)),
        "l0_v1": np.ascontiguousarray(np.asarray(L0_ffn_V1, f32)),
        "l0_b1": col(L0_ffn_b1),
        "l0_u2": np.ascontiguousarray(np.asarray(L0_ffn_U2, f32)),
        "l0_v2": np.ascontiguousarray(np.asarray(L0_ffn_V2, f32)),
        "l0_b2": col(L0_ffn_b2),
        "l1_wqT": t2(L1_attn_Wq),
        "l1_wkT2": np.ascontiguousarray(
            np.concatenate([np.asarray(L1_attn_Wk, f32).T] * 2, axis=1)),
        "l1_wvT": t2(L1_attn_Wv),
        "l1_bias": col(L1_ffn_bias),
        "wrT": t2(L1_ffn_w_freq_real), "wiT": t2(L1_ffn_w_freq_imag),
        "cCb": C.astype(bf16), "cSb": Sm.astype(bf16), "cSnb": (-Sm).astype(bf16),
        "e2bf": e2.astype(bf16),
        "e2tf": np.ascontiguousarray(e2.T),
        "ones1": np.ones((1, 128), f32),
        "neg1": -np.ones((1, 128), f32),
        "onesc": np.ones((128, 1), f32),
        "identf": np.eye(128, dtype=f32),
        "o64": np.ones((64, 1), bf16),
        "oc128": np.ones((128, 1), bf16),
    }
    trilm = np.where(np.tril(np.ones((128, 128), bool)), 0.0, -1e30).astype(f32)
    fullm = np.full((128, 128), -1e30, f32)
    zerom = np.zeros((128, 128), f32)

    nc = build()
    in_maps = []
    for c in range(8):
        n, f = c // 2, c % 2
        tiles = [2 * i + f for i in range(16)]
        xq_rows = np.ascontiguousarray(
            np.concatenate([x[n, t * 128:(t + 1) * 128, :] for t in tiles], axis=0))
        im = dict(shared)
        im["xq"] = xq_rows
        im["xb"] = np.ascontiguousarray(x[n])
        im["mask_a"] = trilm if f == 0 else zerom
        im["mask_b"] = fullm if f == 0 else trilm
        in_maps.append(im)

    res = run_bass_kernel_spmd(nc, in_maps, core_ids=list(range(8)))
    out = np.empty((4, 4096, 1024), f32)
    for c in range(8):
        n, f = c // 2, c % 2
        r = np.asarray(res.results[c]["xout"], f32)
        for i in range(16):
            t = 2 * i + f
            out[n, t * 128:(t + 1) * 128, :] = r[i * 128:(i + 1) * 128, :]
    return out
